# version 03
# baseline (speedup 1.0000x reference)
import numpy as np
import concourse.bass as bass
import concourse.mybir as mybir
from concourse.bass_utils import run_bass_kernel_spmd

F32 = mybir.dt.float32
BF16 = mybir.dt.bfloat16
I32 = mybir.dt.int32
AF = mybir.ActivationFunctionType
ALU = mybir.AluOpType
AX = mybir.AxisListType

D = 1024
DH = 4096

ENGS = ("sync", "tensor", "vector", "scalar", "gpsimd")


def dm0(m):
    return 256 * (m % 4) + 128 * (m // 4)


class Tok:
    __slots__ = ("eng", "val")

    def __init__(self, eng, val):
        self.eng = eng
        self.val = val


class Sched:

    SELF_DIST = 4

    def __init__(self, nc):
        self.nc = nc
        self.semnames = list(ENGS) + [e + "_dma" for e in ("gpsimd", "scalar", "sync")]
        self.sem = {n: nc.semaphore("sem_" + n).__enter__() for n in self.semnames}
        self.cnt = {n: 0 for n in self.semnames}
        self.events = []
        self.pos = {e: 0 for e in ENGS}
        self.tok_pos = {}
        self.fence_w = []

    def fence(self):
        self.fence_w = [(n, self.cnt[n]) for n in self.semnames if self.cnt[n] > 0]

    def op(self, eng, fn, after=(), dmainc=False):
        sn = eng + "_dma" if (dmainc and eng in ("gpsimd", "scalar", "sync")) else eng
        waits = list(self.fence_w)
        for t in after:
            if t is None:
                continue
            v = self.cnt[t.eng] if t.eng.endswith("_dma") else t.val
            waits.append((t.eng, v))
        n = 16 if dmainc else 1
        self.cnt[sn] += n
        tok = Tok(sn, self.cnt[sn])
        self.pos[eng] += 1
        self.tok_pos[(sn, tok.val)] = (eng, self.pos[eng], dmainc)
        self.events.append((eng, sn, fn, waits, n, self.pos[eng]))
        return tok

    def emit(self):
        ev_by_eng = {e: [] for e in ENGS}
        for eng, sn, fn, waits, n, pos in self.events:
            ev_by_eng[eng].append((sn, fn, waits, n, pos))
        nc = self.nc
        sem = self.sem
        tok_pos = self.tok_pos
        SELF = self.SELF_DIST

        def body(engine, eng_name):
            done = {}
            for sn, fn, waits, n, pos in ev_by_eng[eng_name]:
                for wsn, wval in waits:
                    info = tok_pos.get((wsn, wval))
                    if info is not None:
                        weng, dpos, wdma = info
                        if (weng == eng_name and not wdma
                                and pos - dpos > SELF):
                            continue
                    if done.get(wsn, 0) >= wval:
                        continue
                    engine.wait_ge(sem[wsn], wval)
                    done[wsn] = wval
                ins = fn(engine)
                ins.then_inc(sem[sn], n)

        with nc.Block() as blk:
            for e in ENGS:
                if ev_by_eng[e]:
                    getattr(blk, e)(lambda eng, e=e: body(eng, e))
        self.events = []
        self.pos = {e: 0 for e in ENGS}
        self.tok_pos = {}


def build_program(S=4096):
    PT1 = min(512, S)
    PSC = min(256, S)
    PT4 = min(256, S)
    NT1 = S // PT1
    NSC = S // PSC
    NT4 = S // PT4
    NG1 = PT1 // 128
    NG4 = max(1, PT4 // 128)

    nc = bass.Bass("TRN2", target_bir_lowering=False)

    x_in = nc.dram_tensor("x_in", [S, D], F32, kind="ExternalInput")
    w_in_d = nc.dram_tensor("w_in_d", [2 * D, D], F32, kind="ExternalInput")
    b_in_d = nc.dram_tensor("b_in_d", [2 * D], F32, kind="ExternalInput")
    w_h_d = nc.dram_tensor("w_h_d", [D, D], F32, kind="ExternalInput")
    b_h_d = nc.dram_tensor("b_h_d", [D], F32, kind="ExternalInput")
    w_out_d = nc.dram_tensor("w_out_d", [D, D], F32, kind="ExternalInput")
    b_out_d = nc.dram_tensor("b_out_d", [D], F32, kind="ExternalInput")
    ln_w_d = nc.dram_tensor("ln_w_d", [D], F32, kind="ExternalInput")
    ln_b_d = nc.dram_tensor("ln_b_d", [D], F32, kind="ExternalInput")
    w1_d = nc.dram_tensor("w1_d", [DH, D], F32, kind="ExternalInput")
    cw_d = nc.dram_tensor("cw_d", [DH, 3], F32, kind="ExternalInput")
    cb_d = nc.dram_tensor("cb_d", [DH], F32, kind="ExternalInput")
    w2_d = nc.dram_tensor("w2_d", [D, DH], F32, kind="ExternalInput")
    y_out = nc.dram_tensor("y_out", [S, D], F32, kind="ExternalOutput")

    z_scr = nc.dram_tensor("z_scr", [128, 8 * S], F32)
    a_scr = nc.dram_tensor("a_scr", [128, 8 * S], F32)
    hs_scr = nc.dram_tensor("hs_scr", [128, 8 * S], F32)
    y1_scr = nc.dram_tensor("y1_scr", [S, D], F32)
    xq_scr = nc.dram_tensor("xq_scr", [128, 8 * (S + 2)], BF16)
    xq2_scr = nc.dram_tensor("xq2_scr", [128, 32 * S], BF16)
    m1_scr = nc.dram_tensor("m1_scr", [S + 2, 1], F32)
    dq2_scr = nc.dram_tensor("dq2_scr", [S, 1], F32)

    A = nc.alloc_sbuf_tensor
    arf = A("arf", [128, 8192], F32)
    gen = A("gen", [128, 8192], F32)
    bfa = A("bfa", [128, 8192], BF16)
    bfb = A("bfb", [128, 8192], BF16)
    whT = A("whT", [128, 16384], BF16)
    wbig = A("wbig", [128, 32768], BF16)
    it32 = A("it32", [128, 512], I32)
    lnw_bc = A("lnw_bc", [128, 1024], F32)
    lnb_bc = A("lnb_bc", [128, 1024], F32)
    bu_fm = A("bu_fm", [128, 8], F32)
    zb_fm = A("zb_fm", [128, 8], F32)
    bh_fm = A("bh_fm", [128, 8], F32)
    cw_fm = A("cw_fm", [128, 96], F32)
    cb_fm = A("cb_fm", [128, 32], F32)
    fmat = A("fmat", [128, 4], F32)
    idn = A("idn", [128, 128], F32)
    idnb = A("idnb", [128, 128], BF16)
    ones_col = A("ones_col", [128, 1], F32)
    ones_rf = A("ones_rf", [1, 128], F32)
    ones_rb = A("ones_rb", [1, 128], BF16)
    c1_row = A("c1_row", [1, 128], F32)
    c2_col = A("c2_col", [128, 1], F32)
    thr1 = A("thr1", [128, 1], F32)
    thr2 = A("thr2", [128, 1], F32)
    scs = A("scs", [128, 16], F32)
    vrow = A("vrow", [1, 16], F32)
    zrow = A("zrow", [128, 16], BF16)
    pair = A("pair", [128, 16], BF16)
    dHf = A("dHf", [128, 8], F32)
    dsub = A("dsub", [128, 8], F32)
    dsub2 = A("dsub2", [128, 8], F32)
    dfull = A("dfull", [128, 8], F32)
    qsc = A("qsc", [128, 8], F32)
    cand = A("cand", [128, 8], F32)
    hcur = A("hcur", [128, 8], F32)
    mcol = A("mcol", [128, 8], F32)
    tstage = A("tstage", [128, 1024], BF16)
    bo_bf = wbig[:, 16384:18432]
    Ssc = arf[:, 6144:6400]
    mxt = gen[:, 4096:4096 + PT4]
    dq1b = gen[:, 4608:4608 + PT4 + 2]
    s2b = gen[:, 5632:5632 + PT4]
    mx_tm = gen[:, 6656:6656 + NG4]
    dq2_tm = gen[:, 6720:6720 + NG4]
    mrow = gen[:, 7168:7168 + PT4 + 2]

    p_y = nc.alloc_psum_tensor("p_y", [128, 512], F32)
    p_t0 = nc.alloc_psum_tensor("p_t0", [128, 512], F32)
    p_t1 = nc.alloc_psum_tensor("p_t1", [128, 512], F32)
    p_a = nc.alloc_psum_tensor("p_a", [128, 512], F32)
    p_b = nc.alloc_psum_tensor("p_b", [128, 512], F32)
    p_c = nc.alloc_psum_tensor("p_c", [128, 512], F32)
    p_t = [p_t0, p_t1]

    sch = Sched(nc)
    OP = sch.op
    global_state = {}
    BAR = nc.all_engine_barrier

    def flush():
        sch.emit()
        sch.fence()
        BAR()

    def scr_view(scr, t0, nsteps):
        return scr.rearrange("p (c t) -> p c t", c=8)[:, :, t0:t0 + nsteps]

    ldc = []
    ldc.append(OP("sync", lambda e: e.dma_start(out=lnw_bc[0:1, :], in_=ln_w_d[None, :]), dmainc=True))
    ldc.append(OP("sync", lambda e: e.dma_start(out=lnb_bc[0:1, :], in_=ln_b_d[None, :]), dmainc=True))
    ldc.append(OP("sync", lambda e: e.dma_start(
        out=bu_fm[:, :], in_=b_in_d.rearrange("(c p) -> p c", p=128)[:, 0:8], allow_slow_non_contiguous=True), dmainc=True))
    ldc.append(OP("sync", lambda e: e.dma_start(
        out=zb_fm[:, :], in_=b_in_d.rearrange("(c p) -> p c", p=128)[:, 8:16], allow_slow_non_contiguous=True), dmainc=True))
    ldc.append(OP("sync", lambda e: e.dma_start(
        out=bh_fm[:, :], in_=b_h_d.rearrange("(c p) -> p c", p=128), allow_slow_non_contiguous=True), dmainc=True))
    ldc.append(OP("sync", lambda e: e.dma_start(
        out=cw_fm.rearrange("p (c k) -> p c k", c=32), in_=cw_d.rearrange("(c p) k -> p c k", p=128), allow_slow_non_contiguous=True), dmainc=True))
    ldc.append(OP("sync", lambda e: e.dma_start(
        out=cb_fm[:, :], in_=cb_d.rearrange("(c p) -> p c", p=128), allow_slow_non_contiguous=True), dmainc=True))
    ldwh = []
    for i in range(8):
        ldwh.append(OP("sync", lambda e, i=i: e.dma_start(
            out=arf[:, 1024 * i:1024 * (i + 1)],
            in_=w_h_d[128 * i:128 * (i + 1), :]), dmainc=True))
    t_iota = OP("gpsimd", lambda e: e.iota(idn[:, :], [[1, 128]], channel_multiplier=-1,
                                           allow_small_or_imprecise_dtypes=True))
    for fn in (lambda e: e.memset(zrow[:, :], 0.0),
               lambda e: e.memset(ones_col[:, :], 1.0),
               lambda e: e.memset(ones_rf[:, :], 1.0),
               lambda e: e.memset(ones_rb[:, :], 1.0),
               lambda e: e.memset(hcur[:, :], 0.0),
               lambda e: e.memset(pair[:, :], 0.0),
               lambda e: e.memset(scs[:, :], 0.0),
               lambda e: e.memset(p_y[:, :], 0.0),
               lambda e: e.memset(p_t0[:, :], 0.0),
               lambda e: e.memset(p_t1[:, :], 0.0),
               lambda e: e.memset(fmat[:, :], 0.0)):
        fz = OP("vector", fn)
    for j in range(4):
        fz = OP("vector", lambda e, j=j: e.memset(fmat[32 * j:32 * j + 2, j:j + 1], 1.0),
                after=(fz,))
    flush()

    t1 = OP("vector", lambda e: e.tensor_scalar(out=idn[:, :], in0=idn[:, :],
                                                scalar1=0.0, scalar2=None, op0=ALU.is_equal))
    OP("vector", lambda e: e.tensor_copy(out=idnb[:, :], in_=idn[:, :]), after=(t1,))
    OP("vector", lambda e: e.tensor_tensor(out=zb_fm[:, :], in0=zb_fm[:, :],
                                           in1=bh_fm[:, :], op=ALU.add))
    m1 = OP("tensor", lambda e: e.matmul(p_c[:, 0:512], ones_rf[0:1, :], lnw_bc[0:1, 0:512],
                                         start=True, stop=True))
    m2 = OP("tensor", lambda e: e.matmul(p_b[:, 0:512], ones_rf[0:1, :], lnw_bc[0:1, 512:1024],
                                         start=True, stop=True))
    m3 = OP("tensor", lambda e: e.matmul(p_a[:, 0:512], ones_rf[0:1, :], lnb_bc[0:1, 0:512],
                                         start=True, stop=True))
    m4 = OP("tensor", lambda e: e.matmul(p_t0[:, 0:512], ones_rf[0:1, :], lnb_bc[0:1, 512:1024],
                                         start=True, stop=True))
    OP("scalar", lambda e: e.copy(out=lnw_bc[:, 0:512], in_=p_c[:, 0:512]), after=(m1,))
    OP("scalar", lambda e: e.copy(out=lnw_bc[:, 512:1024], in_=p_b[:, 0:512]), after=(m2,))
    OP("scalar", lambda e: e.copy(out=lnb_bc[:, 0:512], in_=p_a[:, 0:512]), after=(m3,))
    OP("scalar", lambda e: e.copy(out=lnb_bc[:, 512:1024], in_=p_t0[:, 0:512]), after=(m4,))
    flush()

    OP("vector", lambda e: e.memset(p_t0[:, :], 0.0))
    OP("vector", lambda e: e.memset(p_a[:, :], 0.0))
    OP("vector", lambda e: e.memset(p_b[:, :], 0.0))
    OP("vector", lambda e: e.memset(p_c[:, :], 0.0))
    OP("vector", lambda e: e.memset(mrow[0:1, :], 0.0))
    for c in range(8):
        OP("gpsimd", lambda e, c=c: e.dma_start(
            out=xq_scr[:, c * (S + 2):c * (S + 2) + 1], in_=zrow[:, 0:1], allow_slow_non_contiguous=True), dmainc=True)
        OP("gpsimd", lambda e, c=c: e.dma_start(
            out=xq_scr[:, c * (S + 2) + S + 1:c * (S + 2) + S + 2], in_=zrow[:, 0:1], allow_slow_non_contiguous=True), dmainc=True)
    OP("gpsimd", lambda e: e.dma_start(out=m1_scr[0:1, :], in_=scs[0:1, 0:1]), dmainc=True)
    OP("gpsimd", lambda e: e.dma_start(out=m1_scr[S + 1:S + 2, :], in_=scs[0:1, 0:1]), dmainc=True)
    flush()

    def split_hl(src_view, n, tmp0, deps=()):
        h = OP("vector", lambda e: e.tensor_copy(out=bfa[:, 0:n], in_=src_view), after=deps)
        u = OP("vector", lambda e: e.tensor_copy(out=gen[:, tmp0:tmp0 + n], in_=bfa[:, 0:n]),
               after=(h,))
        l0 = OP("vector", lambda e: e.tensor_tensor(out=gen[:, tmp0:tmp0 + n], in0=src_view,
                                                    in1=gen[:, tmp0:tmp0 + n], op=ALU.subtract),
                after=(u,))
        l = OP("vector", lambda e: e.tensor_copy(out=bfa[:, 4096:4096 + n],
                                                 in_=gen[:, tmp0:tmp0 + n]), after=(l0,))
        return h, l

    for rnd in range(2):
        hH, hL = split_hl(arf[:, 4096 * rnd:4096 * (rnd + 1)], 4096, 4096,
                          deps=(ldwh[-1],))
        for iloc in range(4):
            i = 4 * rnd + iloc
            j, u = i // 2, i % 2
            for m in range(8):
                for half, base in ((0, 0), (1, 4096)):
                    dep = hH if half == 0 else hL
                    OP("scalar", lambda e, iloc=iloc, j=j, u=u, m=m, half=half, base=base:
                       e.dma_start(
                           out=whT[:, 8192 * half + 256 * (4 * m + j) + 128 * u:
                                   8192 * half + 256 * (4 * m + j) + 128 * (u + 1)],
                           in_=bfa[:, base + 1024 * iloc + dm0(m):
                                   base + 1024 * iloc + dm0(m) + 128],
                           transpose=True), after=(dep, hL), dmainc=True)
        flush()

    for rnd in range(4):
        lds = []
        for iloc in range(4):
            i = 4 * rnd + iloc
            lds.append(OP("sync", lambda e, iloc=iloc, i=i: e.dma_start(
                out=gen[:, 1024 * iloc:1024 * (iloc + 1)],
                in_=w_in_d[128 * i:128 * (i + 1), :]), dmainc=True))
        hH, hL = split_hl(gen[:, 0:4096], 4096, 4096, deps=tuple(lds))
        for iloc in range(4):
            i = 4 * rnd + iloc
            for dc in range(8):
                for half, base, wb in ((0, 0, 0), (1, 4096, 16384)):
                    OP("scalar", lambda e, i=i, iloc=iloc, dc=dc, base=base, wb=wb:
                       e.dma_start(
                           out=wbig[:, wb + 2048 * dc + 128 * i:wb + 2048 * dc + 128 * (i + 1)],
                           in_=bfa[:, base + 1024 * iloc + 128 * dc:
                                   base + 1024 * iloc + 128 * (dc + 1)],
                           transpose=True), after=(hH, hL), dmainc=True)
        flush()

    for scol, wd, ntile in ((0, w1_d, 32), (8, w2_d, 8)):
        colw = wd.shape[1] if hasattr(wd, "shape") else None
        colw = 1024 if wd is w1_d else 4096
        per = 4096 // colw
        nrnd = ntile // per
        for rnd in range(nrnd):
            lds = []
            for k in range(per):
                i = per * rnd + k
                lds.append(OP("sync", lambda e, k=k, i=i, colw=colw, wd=wd: e.dma_start(
                    out=gen[:, colw * k:colw * (k + 1)],
                    in_=wd[128 * i:128 * (i + 1), :]), dmainc=True))
            OP("vector", lambda e, scol=scol, rnd=rnd: e.reduce_sum(
                out=scs[:, scol + rnd:scol + rnd + 1], in_=gen[:, 0:4096], axis=AX.X,
                apply_absolute_value=True), after=tuple(lds))
            flush()
    mm = OP("tensor", lambda e: e.matmul(p_c[0:1, 0:16], ones_col[:, 0:1], scs[:, 0:16],
                                         start=True, stop=True))
    v0 = OP("vector", lambda e: e.tensor_copy(out=vrow[0:1, 0:16], in_=p_c[0:1, 0:16]),
            after=(mm,))
    v1 = OP("vector", lambda e: e.reduce_sum(out=vrow[0:1, 0:1], in_=vrow[0:1, 0:8], axis=AX.X),
            after=(v0,))
    v2 = OP("vector", lambda e: e.reduce_sum(out=vrow[0:1, 1:2], in_=vrow[0:1, 8:16], axis=AX.X),
            after=(v0, v1))
    v3 = OP("vector", lambda e: e.tensor_scalar(out=vrow[0:1, 2:3], in0=vrow[0:1, 0:1],
                                                scalar1=1.0 / (DH * D), scalar2=1e-5,
                                                op0=ALU.mult, op1=ALU.max), after=(v1,))
    v4 = OP("vector", lambda e: e.tensor_scalar(out=vrow[0:1, 3:4], in0=vrow[0:1, 1:2],
                                                scalar1=1.0 / (DH * D), scalar2=1e-5,
                                                op0=ALU.mult, op1=ALU.max), after=(v2,))
    v5 = OP("vector", lambda e: e.tensor_scalar(out=c1_row[0:1, :], in0=ones_rf[0:1, :],
                                                scalar1=vrow[0:1, 2:3], scalar2=1.0 / 127.0,
                                                op0=ALU.mult, op1=ALU.mult), after=(v3,))
    v6 = OP("vector", lambda e: e.tensor_scalar(out=vrow[0:1, 4:5], in0=vrow[0:1, 2:3],
                                                scalar1=0.5, scalar2=None, op0=ALU.mult),
            after=(v3,))
    v7 = OP("vector", lambda e: e.tensor_scalar(out=vrow[0:1, 5:6], in0=vrow[0:1, 3:4],
                                                scalar1=0.5, scalar2=None, op0=ALU.mult),
            after=(v4,))
    v8 = OP("vector", lambda e: e.tensor_scalar(out=vrow[0:1, 6:7], in0=vrow[0:1, 3:4],
                                                scalar1=1.0 / 127.0, scalar2=None, op0=ALU.mult),
            after=(v4,))
    b1 = OP("tensor", lambda e: e.matmul(p_c[:, 32:33], ones_rf[0:1, :], vrow[0:1, 4:5],
                                         start=True, stop=True), after=(v6,))
    b2 = OP("tensor", lambda e: e.matmul(p_c[:, 36:37], ones_rf[0:1, :], vrow[0:1, 5:6],
                                         start=True, stop=True), after=(v7,))
    b3 = OP("tensor", lambda e: e.matmul(p_c[:, 40:41], ones_rf[0:1, :], vrow[0:1, 6:7],
                                         start=True, stop=True), after=(v8,))
    OP("vector", lambda e: e.tensor_copy(out=thr1[:, :], in_=p_c[:, 32:33]), after=(b1,))
    OP("vector", lambda e: e.tensor_copy(out=thr2[:, :], in_=p_c[:, 36:37]), after=(b2,))
    OP("vector", lambda e: e.tensor_copy(out=c2_col[:, :], in_=p_c[:, 40:41]), after=(b3,))
    flush()

    ln_state = {"prev": None}

    def ln_plain(src, dst, t_tmp, sq_tmp, ms, deps=()):
        if ln_state["prev"] is not None:
            deps = tuple(deps) + (ln_state["prev"],)
        a1 = OP("vector", lambda e: e.reduce_sum(out=mcol[:, ms:ms + 1], in_=src, axis=AX.X),
                after=deps)
        a2 = OP("vector", lambda e: e.tensor_scalar(out=mcol[:, ms:ms + 1], in0=mcol[:, ms:ms + 1],
                                                    scalar1=1.0 / D, scalar2=None, op0=ALU.mult),
                after=(a1,))
        a3 = OP("vector", lambda e: e.tensor_scalar(out=t_tmp, in0=src,
                                                    scalar1=mcol[:, ms:ms + 1], scalar2=None,
                                                    op0=ALU.subtract), after=(a2,))
        a4 = OP("vector", lambda e: e.tensor_tensor(out=sq_tmp, in0=t_tmp, in1=t_tmp,
                                                    op=ALU.mult), after=(a3,))
        a5 = OP("vector", lambda e: e.reduce_sum(out=mcol[:, ms + 1:ms + 2], in_=sq_tmp,
                                                 axis=AX.X), after=(a4,))
        a6p = OP("vector", lambda e: e.tensor_scalar(out=mcol[:, ms + 1:ms + 2],
                                                     in0=mcol[:, ms + 1:ms + 2],
                                                     scalar1=1.0 / D, scalar2=1e-5,
                                                     op0=ALU.mult, op1=ALU.add), after=(a5,))
        a6a = OP("scalar", lambda e: e.activation(out=mcol[:, ms + 1:ms + 2],
                                                  in_=mcol[:, ms + 1:ms + 2], func=AF.Sqrt),
                 after=(a6p,))
        a6 = OP("vector", lambda e: e.reciprocal(out=mcol[:, ms + 1:ms + 2],
                                                 in_=mcol[:, ms + 1:ms + 2]), after=(a6a,))
        a7 = OP("vector", lambda e: e.tensor_scalar(out=dst, in0=t_tmp,
                                                    scalar1=mcol[:, ms + 1:ms + 2], scalar2=None,
                                                    op0=ALU.mult), after=(a6, a3))
        ln_state["prev"] = a7
        return a7

    for r in range(NT1):
        ldx = []
        for g in range(NG1):
            ldx.append(OP("sync", lambda e, g=g, r=r: e.dma_start(
                out=arf[:, 1024 * g:1024 * (g + 1)],
                in_=x_in[PT1 * r + 128 * g:PT1 * r + 128 * (g + 1), :]), dmainc=True))
        trT = []
        for g in range(NG1):
            x0v = arf[:, 1024 * g:1024 * (g + 1)]
            n1tok = ln_plain(x0v, gen[:, 1024:2048], gen[:, 0:1024], gen[:, 4096:5120], 0,
                             deps=(ldx[g],))
            xntok = ln_plain(gen[:, 1024:2048], gen[:, 2048:3072], gen[:, 3072:4096],
                             gen[:, 4096:5120], 2, deps=(n1tok,))
            aff1 = OP("vector", lambda e: e.tensor_tensor(out=gen[:, 2048:3072],
                                                          in0=gen[:, 2048:3072],
                                                          in1=lnw_bc[:, :], op=ALU.mult),
                      after=(xntok,))
            aff2 = OP("vector", lambda e: e.tensor_tensor(out=gen[:, 2048:3072],
                                                          in0=gen[:, 2048:3072],
                                                          in1=lnb_bc[:, :], op=ALU.add),
                      after=(aff1,))
            h1 = OP("vector", lambda e, g=g: e.tensor_copy(
                out=bfa[:, 1024 * g:1024 * (g + 1)], in_=gen[:, 2048:3072]), after=(aff2,))
            h2 = OP("vector", lambda e, g=g: e.tensor_copy(out=gen[:, 4096:5120],
                                                           in_=bfa[:, 1024 * g:1024 * (g + 1)]),
                    after=(h1,))
            h3 = OP("vector", lambda e: e.tensor_tensor(out=gen[:, 5120:6144],
                                                        in0=gen[:, 2048:3072],
                                                        in1=gen[:, 4096:5120], op=ALU.subtract),
                    after=(h2,))
            h4 = OP("vector", lambda e, g=g: e.tensor_copy(
                out=bfa[:, 4096 + 1024 * g:4096 + 1024 * (g + 1)], in_=gen[:, 5120:6144]),
                after=(h3,))
            for dc in range(8):
                for half, base, ob, dep in ((0, 0, 0, h1), (1, 4096, 4096, h4)):
                    trT.append(OP("scalar", lambda e, g=g, dc=dc, base=base, ob=ob:
                                  e.dma_start(
                                      out=bfb[:, ob + PT1 * dc + 128 * g:
                                              ob + PT1 * dc + 128 * (g + 1)],
                                      in_=bfa[:, base + 1024 * g + 128 * dc:
                                              base + 1024 * g + 128 * (dc + 1)],
                                      transpose=True), after=(dep, h4), dmainc=True))
        prev_sp = {0: None, 1: None}
        global_state.setdefault("za_last", None)
        for ec in range(16):
            ps = p_a if ec % 2 == 0 else p_b
            deps = tuple(trT) + ((prev_sp[ec % 2],) if prev_sp[ec % 2] else ())
            mmt = None
            for dc in range(8):
                for wofs, xofs in ((0, 0), (0, 4096), (16384, 0)):
                    first = (dc == 0 and wofs == 0 and xofs == 0)
                    last = (dc == 7 and wofs == 16384)
                    mmt = OP("tensor", lambda e, ec=ec, dc=dc, wofs=wofs, xofs=xofs,
                             first=first, last=last, ps=ps:
                             e.matmul(ps[:, 0:PT1],
                                      wbig[:, wofs + 2048 * dc + 128 * ec:
                                           wofs + 2048 * dc + 128 * (ec + 1)],
                                      bfb[:, xofs + PT1 * dc:xofs + PT1 * (dc + 1)],
                                      start=first, stop=last),
                             after=deps if first else ())
            if ec < 8:
                ev = OP("scalar", lambda e, ec=ec, ps=ps: e.activation(
                    out=gen[:, 6144 + 1024 * (ec % 2):6144 + 1024 * (ec % 2) + PT1],
                    in_=ps[:, 0:PT1], func=AF.Sigmoid, bias=bu_fm[:, ec:ec + 1], scale=1.0),
                    after=(mmt,))
            else:
                ev = OP("vector", lambda e, ec=ec, ps=ps: e.tensor_scalar(
                    out=gen[:, 6144 + 1024 * (ec % 2):6144 + 1024 * (ec % 2) + PT1],
                    in0=ps[:, 0:PT1], scalar1=zb_fm[:, ec - 8:ec - 7], scalar2=None,
                    op0=ALU.add), after=(mmt,))
            scr = a_scr if ec < 8 else z_scr
            c = (ec % 8 % 2) * 4 + (ec % 8) // 2
            prev_sp[ec % 2] = OP(
                "gpsimd", lambda e, scr=scr, c=c, r=r, ec=ec: e.dma_start(
                    out=scr[:, S * c + PT1 * r:S * c + PT1 * (r + 1)],
                    in_=gen[:, 6144 + 1024 * (ec % 2):6144 + 1024 * (ec % 2) + PT1]),
                after=(ev,), dmainc=True)
            global_state["za_last"] = prev_sp[ec % 2]
        flush()

    import os as _os
    if _os.environ.get("KSTOP") == "p1":
        return nc

    for rnd in range(2):
        lds = []
        for iloc in range(4):
            i = 4 * rnd + iloc
            lds.append(OP("sync", lambda e, iloc=iloc, i=i: e.dma_start(
                out=gen[:, 1024 * iloc:1024 * (iloc + 1)],
                in_=w_out_d[128 * i:128 * (i + 1), :]), dmainc=True))
        hH, hL = split_hl(gen[:, 0:4096], 4096, 4096, deps=tuple(lds))
        for iloc in range(4):
            i = 4 * rnd + iloc
            for dc in range(8):
                for half, base, wb in ((0, 0, 0), (1, 4096, 8192)):
                    OP("scalar", lambda e, i=i, iloc=iloc, dc=dc, base=base, wb=wb:
                       e.dma_start(
                           out=wbig[:, wb + 1024 * dc + 128 * i:wb + 1024 * dc + 128 * (i + 1)],
                           in_=bfa[:, base + 1024 * iloc + 128 * dc:
                                   base + 1024 * iloc + 128 * (dc + 1)],
                           transpose=True), after=(hH, hL), dmainc=True)
        flush()

    bl = OP("sync", lambda e: e.dma_start(out=gen[0:1, 0:1024], in_=b_out_d[None, :]),
            dmainc=True)
    u1 = OP("vector", lambda e: e.tensor_copy(out=bo_bf[0:1, 0:1024], in_=gen[0:1, 0:1024]),
            after=(bl,))
    u2 = OP("vector", lambda e: e.tensor_copy(out=gen[0:1, 1024:2048], in_=bo_bf[0:1, 0:1024]),
            after=(u1,))
    u3 = OP("vector", lambda e: e.tensor_tensor(out=gen[0:1, 2048:3072], in0=gen[0:1, 0:1024],
                                                in1=gen[0:1, 1024:2048], op=ALU.subtract),
            after=(u2,))
    OP("vector", lambda e: e.tensor_copy(out=bo_bf[0:1, 1024:2048], in_=gen[0:1, 2048:3072]),
       after=(u3,))
    flush()

    whH = lambda m, j: whT[:, 256 * (4 * m + j):256 * (4 * m + j) + 256]
    whL = lambda m, j: whT[:, 8192 + 256 * (4 * m + j):8192 + 256 * (4 * m + j) + 256]
    zr = arf[:, 0:8 * PSC]
    ar = arf[:, 2048:2048 + 8 * PSC]
    hr = arf[:, 4096:4096 + 8 * PSC]
    zh = bfa[:, 0:8 * PSC]
    zl2 = bfa[:, 2048:2048 + 8 * PSC]

    def slot(ring, tl):
        return ring[:, tl:tl + 1 + 7 * PSC:PSC]

    prev_split = None
    prev_hs_store = None
    prev_hw = None
    for r in range(NSC):
        dep_r0 = (global_state["za_last"],) if r == 0 else ()
        zl = OP("sync", lambda e, r=r: e.dma_start(
            out=zr.rearrange("p (c t) -> p c t", c=8), in_=scr_view(z_scr, PSC * r, PSC)),
            after=((prev_split,) if prev_split else ()) + dep_r0, dmainc=True)
        al = OP("sync", lambda e, r=r: e.dma_start(
            out=ar.rearrange("p (c t) -> p c t", c=8), in_=scr_view(a_scr, PSC * r, PSC)),
            after=((prev_split,) if prev_split else ()) + dep_r0, dmainc=True)
        for tloc in range(PSC):
            t = PSC * r + tloc
            mmt = None
            dep0 = (prev_split,) if prev_split else ()
            for m in range(8):
                for j in range(4):
                    pr = pair[:, m:m + 9:8]
                    mmt = OP("tensor", lambda e, m=m, j=j, pr=pr: e.matmul(
                        p_y[32 * j:32 * j + 2, 0:256], pr, whH(m, j),
                        start=False, stop=False, tile_position=(0, 32 * j),
                        skip_group_check=True),
                        after=dep0 if (m == 0 and j == 0) else ())
                    mmt = OP("tensor", lambda e, m=m, j=j, pr=pr: e.matmul(
                        p_y[32 * j:32 * j + 2, 0:256], pr, whL(m, j),
                        start=False, stop=False, tile_position=(0, 32 * j),
                        skip_group_check=True))
            ev = OP("vector", lambda e: e.tensor_copy(out=Ssc[:, :], in_=p_y[0:128, 0:256]),
                    after=(mmt,))
            f1 = OP("tensor", lambda e, t=t: e.matmul(p_t[t % 2][0:128, 0:4],
                                                      Ssc[0:128, 0:128], fmat[:, :],
                                                      start=True, stop=False), after=(ev,))
            f2 = OP("tensor", lambda e, t=t: e.matmul(p_t[t % 2][0:128, 4:8],
                                                      Ssc[0:128, 128:256], fmat[:, :],
                                                      start=False, stop=True))
            qv = OP("vector", lambda e, t=t, tloc=tloc: e.tensor_tensor(
                out=qsc[:, :], in0=p_t[t % 2][0:128, 0:8],
                in1=slot(zr, tloc), op=ALU.add), after=(f2, zl))
            th = OP("scalar", lambda e: e.activation(
                out=cand[:, :], in_=qsc[:, :], func=AF.Tanh), after=(qv,))
            hprev = hcur[:, :] if t == 0 else slot(hr, (tloc - 1) % PSC)
            s1 = OP("vector", lambda e, hprev=hprev: e.tensor_tensor(
                out=dsub2[:, :], in0=cand[:, :], in1=hprev, op=ALU.subtract),
                after=(th,) + ((prev_hw,) if prev_hw else ()))
            s2 = OP("vector", lambda e, tloc=tloc: e.tensor_tensor(
                out=dfull[:, :], in0=dsub2[:, :], in1=slot(ar, tloc), op=ALU.mult),
                after=(s1, al))
            p1 = OP("vector", lambda e: e.tensor_copy(out=pair[:, 0:8], in_=dfull[:, :]),
                    after=(s2,))
            p2 = OP("vector", lambda e: e.tensor_copy(out=dHf[:, :], in_=pair[:, 0:8]),
                    after=(p1,))
            p3 = OP("vector", lambda e: e.tensor_tensor(out=dsub[:, :], in0=dfull[:, :],
                                                        in1=dHf[:, :], op=ALU.subtract),
                    after=(p2,))
            p4 = OP("vector", lambda e: e.tensor_copy(out=pair[:, 8:16], in_=dsub[:, :]),
                    after=(p3,))
            prev_split = p4
            prev_hw = OP("vector", lambda e, tloc=tloc, hprev=hprev: e.tensor_tensor(
                out=slot(hr, tloc), in0=hprev, in1=dfull[:, :], op=ALU.add),
                after=(s2,) + ((prev_hs_store,) if (tloc == 0 and prev_hs_store) else ()))
        prev_hs_store = OP("gpsimd", lambda e, r=r: e.dma_start(
            out=scr_view(hs_scr, PSC * r, PSC), in_=hr.rearrange("p (c t) -> p c t", c=8)),
            after=(prev_split, prev_hw), dmainc=True)
        sch.emit()
    flush()

    global_state["hs_last"] = prev_hs_store
    import os as _os
    if _os.environ.get("KSTOP") != "p2":
        build_rest(nc, sch, locals())
    return nc


def build_rest(nc, sch, env):
    OP = sch.op
    BAR = nc.all_engine_barrier

    def flush():
        sch.emit()
        sch.fence()
        BAR()

    S = env["S"]
    PT1, PT4 = env["PT1"], env["PT4"]
    NT1, NT4 = env["NT1"], env["NT4"]
    NG1, NG4 = env["NG1"], env["NG4"]
    names = ("arf gen bfa bfb whT wbig it32 lnw_bc lnb_bc bo_bf cw_fm cb_fm "
             "idn ones_col ones_rf ones_rb c1_row c2_col thr1 thr2 scs vrow "
             "mcol mrow dq1b s2b mx_tm dq2_tm mxt tstage p_a p_b p_c p_t0 "
             "p_t1 p_y x_in w1_d w2_d y_out z_scr a_scr hs_scr y1_scr xq_scr "
             "xq2_scr m1_scr dq2_scr").split()
    (arf, gen, bfa, bfb, whT, wbig, it32, lnw_bc, lnb_bc, bo_bf, cw_fm, cb_fm,
     idn, ones_col, ones_rf, ones_rb, c1_row, c2_col, thr1, thr2, scs, vrow,
     mcol, mrow, dq1b, s2b, mx_tm, dq2_tm, mxt, tstage, p_a, p_b, p_c, p_t0,
     p_t1, p_y, x_in, w1_d, w2_d, y_out, z_scr, a_scr, hs_scr, y1_scr, xq_scr,
     xq2_scr, m1_scr, dq2_scr) = (env[k] for k in names)
    ln_plain = env["ln_plain"]
    scr_view = env["scr_view"]
    global_state = env["global_state"]

    def nch(m):
        return dm0(m) // 128

    for r in range(NT1):
        ldx = []
        for g in range(NG1):
            ldx.append(OP("sync", lambda e, g=g, r=r: e.dma_start(
                out=arf[:, 1024 * g:1024 * (g + 1)],
                in_=x_in[PT1 * r + 128 * g:PT1 * r + 128 * (g + 1), :]), dmainc=True))
        hsl = OP("sync", lambda e, r=r: e.dma_start(
            out=arf[:, 4096:4096 + 8 * PT1].rearrange("p (c t) -> p c t", c=8),
            in_=scr_view(hs_scr, PT1 * r, PT1)),
            after=(global_state["hs_last"],) if r == 0 else (), dmainc=True)
        hH = OP("vector", lambda e: e.tensor_copy(out=bfb[:, 0:4096],
                                                  in_=arf[:, 4096:8192]), after=(hsl,))
        hU = OP("vector", lambda e: e.tensor_copy(out=gen[:, 4096:8192], in_=bfb[:, 0:4096]),
                after=(hH,))
        hS = OP("vector", lambda e: e.tensor_tensor(out=gen[:, 4096:8192],
                                                    in0=arf[:, 4096:8192],
                                                    in1=gen[:, 4096:8192], op=ALU.subtract),
                after=(hU,))
        hL = OP("vector", lambda e: e.tensor_copy(out=bfb[:, 4096:8192],
                                                  in_=gen[:, 4096:8192]), after=(hS,))
        prev_q = None
        for g in range(NG1):
            x0v = arf[:, 1024 * g:1024 * (g + 1)]
            mm_done = []
            for eh, ps in ((0, p_a), (1, p_b)):
                mmt = None
                first = True
                for m in range(8):
                    for sofs, wofs in ((0, 0), (0, 8192), (4096, 0)):
                        fdeps = (hH, hL) if first else ()
                        if first and prev_q is not None:
                            fdeps = fdeps + (prev_q,)
                        mmt = OP("tensor", lambda e, g=g, m=m, sofs=sofs, wofs=wofs,
                                 eh=eh, ps=ps, first=first: e.matmul(
                                     ps[:, 0:512],
                                     bfb[:, sofs + PT1 * m + 128 * g:
                                         sofs + PT1 * m + 128 * (g + 1)],
                                     wbig[:, wofs + 1024 * nch(m) + 512 * eh:
                                          wofs + 1024 * nch(m) + 512 * (eh + 1)],
                                     start=first, stop=False), after=fdeps)
                        first = False
                mmt = OP("tensor", lambda e, eh=eh, ps=ps: e.matmul(
                    ps[:, 0:512], ones_rb[0:1, :], bo_bf[0:1, 512 * eh:512 * (eh + 1)],
                    start=False, stop=False))
                mmt = OP("tensor", lambda e, eh=eh, ps=ps: e.matmul(
                    ps[:, 0:512], ones_rb[0:1, :],
                    bo_bf[0:1, 1024 + 512 * eh:1024 + 512 * (eh + 1)],
                    start=False, stop=True))
                mm_done.append(mmt)
            n1tok = ln_plain(x0v, gen[:, 1024:2048], gen[:, 0:1024], gen[:, 2048:3072], 0,
                             deps=(ldx[g],))
            aA = OP("vector", lambda e: e.tensor_tensor(out=gen[:, 2048:2560],
                                                        in0=p_a[:, 0:512],
                                                        in1=gen[:, 1024:1536], op=ALU.add),
                    after=(mm_done[0], n1tok))
            aB = OP("vector", lambda e: e.tensor_tensor(out=gen[:, 2560:3072],
                                                        in0=p_b[:, 0:512],
                                                        in1=gen[:, 1536:2048], op=ALU.add),
                    after=(mm_done[1], n1tok))
            ln3 = ln_plain(gen[:, 2048:3072], gen[:, 3072:4096], gen[:, 0:1024],
                           gen[:, 1024:2048], 2, deps=(aA, aB))
            y1v = OP("vector", lambda e, x0v=x0v: e.tensor_tensor(
                out=gen[:, 2048:3072], in0=x0v,
                in1=gen[:, 3072:4096], op=ALU.add), after=(ln3,))
            global_state["y1_last"] = OP("gpsimd", lambda e, r=r, g=g: e.dma_start(
                out=y1_scr[PT1 * r + 128 * g:PT1 * r + 128 * (g + 1), :],
                in_=gen[:, 2048:3072]), after=(y1v,), dmainc=True)
            n2tok = ln_plain(gen[:, 2048:3072], gen[:, 3072:4096], gen[:, 0:1024],
                             gen[:, 1024:2048], 4, deps=(y1v,))
            q1 = OP("vector", lambda e: e.reduce_max(out=mcol[:, 6:7], in_=gen[:, 3072:4096],
                                                     axis=AX.X, apply_absolute_value=True),
                    after=(n2tok,))
            q2 = OP("vector", lambda e: e.tensor_scalar(out=mcol[:, 6:7], in0=mcol[:, 6:7],
                                                        scalar1=1e-5, scalar2=None,
                                                        op0=ALU.max), after=(q1,))
            global_state["m1_last"] = OP("gpsimd", lambda e, r=r, g=g: e.dma_start(
                out=m1_scr[1 + PT1 * r + 128 * g:1 + PT1 * r + 128 * (g + 1), :],
                in_=mcol[:, 6:7]), after=(q2,), dmainc=True)
            q3 = OP("vector", lambda e: e.reciprocal(out=mcol[:, 7:8], in_=mcol[:, 6:7]),
                    after=(q2,))
            q4 = OP("vector", lambda e: e.tensor_scalar(out=gen[:, 0:1024],
                                                        in0=gen[:, 3072:4096],
                                                        scalar1=mcol[:, 7:8], scalar2=127.0,
                                                        op0=ALU.mult, op1=ALU.mult),
                    after=(q3,))
            q5a = OP("gpsimd", lambda e: e.tensor_copy(out=it32[:, :], in_=gen[:, 0:512]),
                     after=(q4,))
            q5b = OP("gpsimd", lambda e: e.tensor_copy(out=gen[:, 1024:1536], in_=it32[:, :]),
                     after=(q5a,))
            q5c = OP("gpsimd", lambda e: e.tensor_copy(out=it32[:, :], in_=gen[:, 512:1024]),
                     after=(q5b,))
            q6 = OP("gpsimd", lambda e: e.tensor_copy(out=gen[:, 1536:2048], in_=it32[:, :]),
                    after=(q5c,))
            q7 = OP("vector", lambda e: e.tensor_tensor(out=gen[:, 2048:3072],
                                                        in0=gen[:, 0:1024],
                                                        in1=gen[:, 1024:2048],
                                                        op=ALU.subtract),
                    after=(q6, global_state["y1_last"]))
            q8 = OP("vector", lambda e: e.scalar_tensor_tensor(out=gen[:, 3072:4096],
                                                               in0=gen[:, 2048:3072],
                                                               scalar=0.5,
                                                               in1=gen[:, 1024:2048],
                                                               op0=ALU.is_gt, op1=ALU.add),
                    after=(q7,))
            q9 = OP("vector", lambda e: e.tensor_scalar(out=gen[:, 0:1024],
                                                        in0=gen[:, 2048:3072],
                                                        scalar1=-0.5, scalar2=None,
                                                        op0=ALU.is_lt), after=(q7,))
            qa_ = OP("vector", lambda e: e.scalar_tensor_tensor(out=gen[:, 1024:2048],
                                                                in0=gen[:, 0:1024],
                                                                scalar=-1.0,
                                                                in1=gen[:, 3072:4096],
                                                                op0=ALU.mult, op1=ALU.add),
                     after=(q8, q9))
            qb_ = OP("vector", lambda e: e.tensor_copy(out=bfa[:, 0:1024],
                                                       in_=gen[:, 1024:2048]), after=(qa_,))
            prev_q = qb_
            prev_xq = global_state.get("xq_last")
            trs = None
            for dc in range(8):
                trs = OP("scalar", lambda e, dc=dc: e.dma_start(
                    out=tstage[:, 128 * dc:128 * (dc + 1)],
                    in_=bfa[:, 128 * dc:128 * (dc + 1)], transpose=True),
                    after=(qb_,) + ((prev_xq,) if prev_xq else ()), dmainc=True)
            global_state["xq_last"] = OP(
                "gpsimd", lambda e, r=r, g=g: e.dma_start(
                    out=xq_scr.rearrange("p (c t) -> p c t", c=8)[
                        :, :, 1 + PT1 * r + 128 * g:1 + PT1 * r + 128 * (g + 1)],
                    in_=tstage.rearrange("p (c t) -> p c t", c=8)),
                after=(trs,), dmainc=True)
        flush()

    for rnd in range(DH // 512):
        lds = []
        for k in range(4):
            i = 4 * rnd + k
            lds.append(OP("sync", lambda e, k=k, i=i: e.dma_start(
                out=gen[:, 1024 * k:1024 * (k + 1)],
                in_=w1_d[128 * i:128 * (i + 1), :]), dmainc=True))
        g1a = OP("vector", lambda e: e.scalar_tensor_tensor(
            out=gen[:, 4096:8192], in0=gen[:, 0:4096], scalar=-1.0,
            in1=gen[:, 0:4096], op0=ALU.mult, op1=ALU.max), after=tuple(lds))
        g1 = OP("vector", lambda e: e.tensor_scalar(out=gen[:, 4096:8192],
                                                    in0=gen[:, 4096:8192],
                                                    scalar1=thr1[:, 0:1], scalar2=None,
                                                    op0=ALU.is_ge), after=(g1a,))
        g2 = OP("scalar", lambda e: e.activation(out=arf[:, 0:4096], in_=gen[:, 0:4096],
                                                 func=AF.Sign), after=tuple(lds))
        g3 = OP("vector", lambda e: e.tensor_tensor(out=arf[:, 4096:8192],
                                                    in0=gen[:, 4096:8192],
                                                    in1=arf[:, 0:4096], op=ALU.mult),
                after=(g1, g2))
        g4 = OP("vector", lambda e: e.tensor_copy(out=bfa[:, 0:4096], in_=arf[:, 4096:8192]),
                after=(g3,))
        for k in range(4):
            ht = 4 * rnd + k
            for dc in range(8):
                OP("scalar", lambda e, k=k, ht=ht, dc=dc: e.dma_start(
                    out=wbig[:, 4096 * dc + 128 * ht:4096 * dc + 128 * (ht + 1)],
                    in_=bfa[:, 1024 * k + 128 * dc:1024 * k + 128 * (dc + 1)],
                    transpose=True), after=(g4,), dmainc=True)
        flush()

    for r in range(NT4):
        t0 = PT4 * r
        bdep = (global_state["xq_last"], global_state["m1_last"]) if r == 0 else ()
        xl = OP("sync", lambda e, t0=t0: e.dma_start(
            out=bfb[:, 0:8 * (PT4 + 2)].rearrange("p (c t) -> p c t", c=8),
            in_=xq_scr.rearrange("p (c t) -> p c t", c=8)[:, :, t0:t0 + PT4 + 2]),
            after=bdep, dmainc=True)
        ml = OP("sync", lambda e, t0=t0: e.dma_start(
            out=mrow[0:1, :], in_=m1_scr[t0:t0 + PT4 + 2, :].rearrange("t a -> a t"), allow_slow_non_contiguous=True),
            after=bdep, dmainc=True)
        bq = OP("tensor", lambda e: e.matmul(p_c[:, 0:PT4 + 2], c1_row[0:1, :], mrow[0:1, :],
                                             start=True, stop=True), after=(ml,))
        bqe = OP("vector", lambda e: e.tensor_copy(out=dq1b[:, :], in_=p_c[:, 0:PT4 + 2]),
                 after=(bq,))
        mxprev = OP("vector", lambda e: e.memset(mxt[:, :], 0.0))
        prev_ev = {0: None, 1: None}
        sil_ev = []
        for hc in range(32):
            ps = p_a if hc % 2 == 0 else p_b
            mmt = None
            for dc in range(8):
                fdeps = ()
                if dc == 0:
                    fdeps = (xl,)
                    if prev_ev[hc % 2] is not None:
                        fdeps = fdeps + (prev_ev[hc % 2],)
                mmt = OP("tensor", lambda e, hc=hc, dc=dc, ps=ps: e.matmul(
                    ps[:, 0:PT4 + 2],
                    wbig[:, 4096 * dc + 128 * hc:4096 * dc + 128 * (hc + 1)],
                    bfb[:, (PT4 + 2) * dc:(PT4 + 2) * (dc + 1)],
                    start=(dc == 0), stop=(dc == 7)), after=fdeps)
            dv = OP("vector", lambda e, ps=ps: e.tensor_tensor(out=gen[:, 0:PT4 + 2],
                                                               in0=ps[:, 0:PT4 + 2],
                                                               in1=dq1b[:, :], op=ALU.mult),
                    after=(mmt, bqe))
            prev_ev[hc % 2] = dv
            c1v = OP("vector", lambda e, hc=hc: e.tensor_scalar(
                out=gen[:, 1024:1024 + PT4], in0=gen[:, 1:1 + PT4],
                scalar1=cw_fm[:, 3 * hc + 1:3 * hc + 2], scalar2=cb_fm[:, hc:hc + 1],
                op0=ALU.mult, op1=ALU.add), after=(dv,))
            c2v = OP("vector", lambda e, hc=hc: e.scalar_tensor_tensor(
                out=gen[:, 2048:2048 + PT4], in0=gen[:, 0:PT4],
                scalar=cw_fm[:, 3 * hc:3 * hc + 1], in1=gen[:, 1024:1024 + PT4],
                op0=ALU.mult, op1=ALU.add), after=(c1v,))
            c3v = OP("vector", lambda e, hc=hc: e.scalar_tensor_tensor(
                out=gen[:, 1024:1024 + PT4], in0=gen[:, 2:2 + PT4],
                scalar=cw_fm[:, 3 * hc + 2:3 * hc + 3], in1=gen[:, 2048:2048 + PT4],
                op0=ALU.mult, op1=ALU.add), after=(c2v,))
            sl = OP("scalar", lambda e, hc=hc: e.activation(
                out=arf[:, PT4 * hc:PT4 * (hc + 1)], in_=gen[:, 1024:1024 + PT4],
                func=AF.Silu), after=(c3v,))
            mxa = OP("vector", lambda e, hc=hc: e.scalar_tensor_tensor(
                out=gen[:, 3072:3072 + PT4], in0=arf[:, PT4 * hc:PT4 * (hc + 1)],
                scalar=-1.0, in1=arf[:, PT4 * hc:PT4 * (hc + 1)],
                op0=ALU.mult, op1=ALU.max), after=(sl, mxprev))
            mxprev = OP("vector", lambda e: e.tensor_tensor(
                out=mxt[:, :], in0=mxt[:, :], in1=gen[:, 3072:3072 + PT4],
                op=ALU.max), after=(mxa,))
            sil_ev.append(sl)
        rds = []
        trprev = mxprev
        for g in range(NG4):
            tr = OP("tensor", lambda e, g=g: e.transpose(
                p_c[0:128, 128:256], mxt[:, 128 * g:128 * (g + 1)], idn[:, :]),
                after=(trprev, bqe))
            rd = OP("vector", lambda e, g=g: e.reduce_max(
                out=mx_tm[:, g:g + 1], in_=p_c[0:128, 128:256], axis=AX.X), after=(tr,))
            trprev = rd
            rds.append(rd)
        cl = OP("vector", lambda e: e.tensor_scalar(out=mx_tm[:, :], in0=mx_tm[:, :],
                                                    scalar1=1e-5, scalar2=None, op0=ALU.max),
                after=tuple(rds))
        dq2v = OP("vector", lambda e: e.tensor_scalar(out=dq2_tm[:, :], in0=mx_tm[:, :],
                                                      scalar1=c2_col[:, 0:1], scalar2=None,
                                                      op0=ALU.mult), after=(cl,))
        global_state["dq2_last"] = OP("gpsimd", lambda e, t0=t0: e.dma_start(
            out=dq2_scr[t0:t0 + PT4, :].rearrange("(g p) a -> p (g a)", p=128),
            in_=dq2_tm[:, :], allow_slow_non_contiguous=True), after=(dq2v,), dmainc=True)
        rc = OP("vector", lambda e: e.reciprocal(out=mx_tm[:, :], in_=mx_tm[:, :]),
                after=(cl, dq2v))
        s2toks = []
        prevrow = rc
        for g in range(NG4):
            trr = OP("tensor", lambda e, g=g: e.matmul(
                p_c[0:1, 256 + 128 * g:256 + 128 * (g + 1)], mx_tm[:, g:g + 1], idn[:, :],
                start=(g == 0), stop=(g == NG4 - 1)), after=(prevrow,))
            prevrow = trr
            s2toks.append(trr)
        s2r_ = OP("vector", lambda e: e.tensor_scalar(
            out=gen[0:1, 0:PT4], in0=p_c[0:1, 256:256 + PT4], scalar1=127.0,
            scalar2=None, op0=ALU.mult), after=tuple(s2toks))
        bq2 = OP("tensor", lambda e: e.matmul(p_c[:, 256:256 + PT4], ones_rf[0:1, :],
                                              gen[0:1, 0:PT4], start=True, stop=True),
                 after=(s2r_,))
        s2be = OP("vector", lambda e: e.tensor_copy(out=s2b[:, :], in_=p_c[:, 256:256 + PT4]),
                  after=(bq2,))
        prev_st = None
        st_hist = []
        for hc in range(32):
            u = OP("vector", lambda e, hc=hc: e.tensor_tensor(
                out=gen[:, 0:PT4], in0=arf[:, PT4 * hc:PT4 * (hc + 1)], in1=s2b[:, :],
                op=ALU.mult), after=(s2be, sil_ev[hc]) + ((prev_st,) if prev_st else ()))
            i1 = OP("gpsimd", lambda e: e.tensor_copy(out=it32[:, 0:PT4], in_=gen[:, 0:PT4]),
                    after=(u,))
            i2 = OP("gpsimd", lambda e: e.tensor_copy(out=gen[:, 1024:1024 + PT4],
                                                      in_=it32[:, 0:PT4]), after=(i1,))
            d1 = OP("vector", lambda e: e.tensor_tensor(out=gen[:, 2048:2048 + PT4],
                                                        in0=gen[:, 0:PT4],
                                                        in1=gen[:, 1024:1024 + PT4],
                                                        op=ALU.subtract), after=(i2,))
            e1 = OP("vector", lambda e: e.scalar_tensor_tensor(out=gen[:, 3072:3072 + PT4],
                                                               in0=gen[:, 2048:2048 + PT4],
                                                               scalar=0.5,
                                                               in1=gen[:, 1024:1024 + PT4],
                                                               op0=ALU.is_gt, op1=ALU.add),
                    after=(d1,))
            e2 = OP("vector", lambda e: e.tensor_scalar(out=gen[:, 0:PT4],
                                                        in0=gen[:, 2048:2048 + PT4],
                                                        scalar1=-0.5, scalar2=None,
                                                        op0=ALU.is_lt), after=(d1,))
            e3 = OP("vector", lambda e: e.scalar_tensor_tensor(out=gen[:, 1024:1024 + PT4],
                                                               in0=gen[:, 0:PT4],
                                                               scalar=-1.0,
                                                               in1=gen[:, 3072:3072 + PT4],
                                                               op0=ALU.mult, op1=ALU.add),
                    after=(e1, e2))
            e4deps = (e3,)
            if len(st_hist) >= 8:
                e4deps = e4deps + (st_hist[-8],)
            e4 = OP("vector", lambda e, hc=hc: e.tensor_copy(
                out=bfa[:, PT4 * (hc % 8):PT4 * (hc % 8 + 1)], in_=gen[:, 1024:1024 + PT4]),
                after=e4deps)
            prev_st = OP("gpsimd", lambda e, hc=hc, t0=t0: e.dma_start(
                out=xq2_scr[:, S * hc + t0:S * hc + t0 + PT4],
                in_=bfa[:, PT4 * (hc % 8):PT4 * (hc % 8 + 1)]), after=(e4,), dmainc=True)
            st_hist.append(prev_st)
            global_state["xq2_last"] = prev_st
        flush()

    for i in range(8):
        ld = OP("sync", lambda e, i=i: e.dma_start(
            out=gen[:, 0:4096], in_=w2_d[128 * i:128 * (i + 1), :]), dmainc=True)
        g1a = OP("vector", lambda e: e.scalar_tensor_tensor(
            out=gen[:, 4096:8192], in0=gen[:, 0:4096], scalar=-1.0,
            in1=gen[:, 0:4096], op0=ALU.mult, op1=ALU.max), after=(ld,))
        g1 = OP("vector", lambda e: e.tensor_scalar(out=gen[:, 4096:8192],
                                                    in0=gen[:, 4096:8192],
                                                    scalar1=thr2[:, 0:1], scalar2=None,
                                                    op0=ALU.is_ge), after=(g1a,))
        g2 = OP("scalar", lambda e: e.activation(out=arf[:, 0:4096], in_=gen[:, 0:4096],
                                                 func=AF.Sign), after=(ld,))
        g3 = OP("vector", lambda e: e.tensor_tensor(out=arf[:, 4096:8192],
                                                    in0=gen[:, 4096:8192],
                                                    in1=arf[:, 0:4096], op=ALU.mult),
                after=(g1, g2))
        g4 = OP("vector", lambda e: e.tensor_copy(out=bfa[:, 0:4096], in_=arf[:, 4096:8192]),
                after=(g3,))
        for hc in range(32):
            OP("scalar", lambda e, hc=hc, i=i: e.dma_start(
                out=wbig[:, 1024 * hc + 128 * i:1024 * hc + 128 * (i + 1)],
                in_=bfa[:, 128 * hc:128 * (hc + 1)], transpose=True),
               after=(g4,), dmainc=True)
        flush()

    ystores = []
    for r in range(NT4):
        t0 = PT4 * r
        bdep5 = ((global_state["xq2_last"], global_state["dq2_last"],
                  global_state["y1_last"]) if r == 0 else ())
        x2l = OP("sync", lambda e, t0=t0: e.dma_start(
            out=whT[:, 0:32 * PT4].rearrange("p (c t) -> p c t", c=32),
            in_=xq2_scr.rearrange("p (c t) -> p c t", c=32)[:, :, t0:t0 + PT4]),
            after=bdep5, dmainc=True)
        y1l = []
        for g in range(NG4):
            y1l.append(OP("sync", lambda e, t0=t0, g=g: e.dma_start(
                out=arf[:, 1024 * g:1024 * (g + 1)],
                in_=y1_scr[t0 + 128 * g:t0 + 128 * (g + 1), :]), after=bdep5, dmainc=True))
        dql = OP("sync", lambda e, t0=t0: e.dma_start(
            out=dq2_tm[:, :],
            in_=dq2_scr[t0:t0 + PT4, :].rearrange("(g p) a -> p (g a)", p=128), allow_slow_non_contiguous=True),
            after=bdep5, dmainc=True)
        prev_f = {0: None, 1: None}
        for g in range(NG4):
            mm_done = []
            for eh, ps in ((0, p_a), (1, p_b)):
                mmt = None
                for hc in range(32):
                    fdeps = ()
                    if hc == 0:
                        fdeps = (x2l,)
                        if prev_f[eh] is not None:
                            fdeps = fdeps + (prev_f[eh],)
                    mmt = OP("tensor", lambda e, hc=hc, g=g, eh=eh, ps=ps: e.matmul(
                        ps[:, 0:512],
                        whT[:, PT4 * hc + 128 * g:PT4 * hc + 128 * (g + 1)],
                        wbig[:, 1024 * hc + 512 * eh:1024 * hc + 512 * (eh + 1)],
                        start=(hc == 0), stop=(hc == 31)), after=fdeps)
                mm_done.append(mmt)
            f1 = OP("vector", lambda e, g=g: e.tensor_scalar(
                out=gen[:, 0:512], in0=p_a[:, 0:512], scalar1=dq2_tm[:, g:g + 1],
                scalar2=None, op0=ALU.mult), after=(mm_done[0], dql))
            f2 = OP("vector", lambda e, g=g: e.tensor_scalar(
                out=gen[:, 512:1024], in0=p_b[:, 0:512], scalar1=dq2_tm[:, g:g + 1],
                scalar2=None, op0=ALU.mult), after=(mm_done[1], dql))
            prev_f[0], prev_f[1] = f1, f2
            ln4 = ln_plain(gen[:, 0:1024], gen[:, 1024:2048], gen[:, 2048:3072],
                           gen[:, 3072:4096], 0,
                           deps=(f1, f2) + ((ystores[-1],) if ystores else ()))
            y2v = OP("vector", lambda e, g=g: e.tensor_tensor(
                out=gen[:, 2048:3072], in0=arf[:, 1024 * g:1024 * (g + 1)],
                in1=gen[:, 1024:2048], op=ALU.add), after=(ln4, y1l[g]))
            ystores.append(OP("gpsimd", lambda e, t0=t0, g=g: e.dma_start(
                out=y_out[t0 + 128 * g:t0 + 128 * (g + 1), :], in_=gen[:, 2048:3072]),
                after=(y2v,), dmainc=True))
        flush()
    OP("vector", lambda e: e.memset(scs[:, 0:1], 0.0), after=tuple(ystores))
    sch.emit()


def kernel(L0_attn_x, L0_attn_W_in, L0_attn_b_in, L0_attn_W_h, L0_attn_b_h,
           L0_attn_W_out, L0_attn_b_out, L0_attn_ln_w, L0_attn_ln_b,
           L0_ffn_w1, L0_ffn_conv_w, L0_ffn_conv_b, L0_ffn_w2):
    L0_attn_x = np.asarray(L0_attn_x)
    B, S, D_ = L0_attn_x.shape
    nc = build_program(S=S)
    base = {
        "w_in_d": np.ascontiguousarray(L0_attn_W_in, dtype=np.float32),
        "b_in_d": np.ascontiguousarray(L0_attn_b_in, dtype=np.float32),
        "w_h_d": np.ascontiguousarray(L0_attn_W_h, dtype=np.float32),
        "b_h_d": np.ascontiguousarray(L0_attn_b_h, dtype=np.float32),
        "w_out_d": np.ascontiguousarray(L0_attn_W_out, dtype=np.float32),
        "b_out_d": np.ascontiguousarray(L0_attn_b_out, dtype=np.float32),
        "ln_w_d": np.ascontiguousarray(L0_attn_ln_w, dtype=np.float32),
        "ln_b_d": np.ascontiguousarray(L0_attn_ln_b, dtype=np.float32),
        "w1_d": np.ascontiguousarray(L0_ffn_w1, dtype=np.float32),
        "cw_d": np.ascontiguousarray(np.asarray(L0_ffn_conv_w).reshape(DH, 3),
                                     dtype=np.float32),
        "cb_d": np.ascontiguousarray(L0_ffn_conv_b, dtype=np.float32),
        "w2_d": np.ascontiguousarray(L0_ffn_w2, dtype=np.float32),
    }
    in_maps = []
    for b in range(B):
        m = dict(base)
        m["x_in"] = np.ascontiguousarray(L0_attn_x[b], dtype=np.float32)
        in_maps.append(m)
    res = run_bass_kernel_spmd(nc, in_maps, core_ids=list(range(B)))
    return np.stack([res.results[b]["y_out"] for b in range(B)],
                    axis=0).astype(np.float32)


# version 04
# speedup vs baseline: 1.0012x; 1.0012x over previous
import numpy as np
import concourse.bass as bass
import concourse.mybir as mybir
from concourse.bass_utils import run_bass_kernel_spmd

F32 = mybir.dt.float32
BF16 = mybir.dt.bfloat16
I32 = mybir.dt.int32
AF = mybir.ActivationFunctionType
ALU = mybir.AluOpType
AX = mybir.AxisListType

D = 1024
DH = 4096

ENGS = ("sync", "tensor", "vector", "scalar", "gpsimd")


def dm0(m):
    return 256 * (m % 4) + 128 * (m // 4)


class Tok:
    __slots__ = ("eng", "val")

    def __init__(self, eng, val):
        self.eng = eng
        self.val = val


class Sched:

    SELF_DIST = 4

    def __init__(self, nc):
        self.nc = nc
        self.semnames = list(ENGS) + [e + "_dma" for e in ("gpsimd", "scalar", "sync")]
        self.sem = {n: nc.semaphore("sem_" + n).__enter__() for n in self.semnames}
        self.cnt = {n: 0 for n in self.semnames}
        self.events = []
        self.pos = {e: 0 for e in ENGS}
        self.tok_pos = {}
        self.fence_w = []

    def fence(self):
        self.fence_w = [(n, self.cnt[n]) for n in self.semnames if self.cnt[n] > 0]

    def op(self, eng, fn, after=(), dmainc=False):
        sn = eng + "_dma" if (dmainc and eng in ("gpsimd", "scalar", "sync")) else eng
        waits = list(self.fence_w)
        for t in after:
            if t is None:
                continue
            v = self.cnt[t.eng] if t.eng.endswith("_dma") else t.val
            waits.append((t.eng, v))
        n = 16 if dmainc else 1
        self.cnt[sn] += n
        tok = Tok(sn, self.cnt[sn])
        self.pos[eng] += 1
        self.tok_pos[(sn, tok.val)] = (eng, self.pos[eng], dmainc)
        self.events.append((eng, sn, fn, waits, n, self.pos[eng]))
        return tok

    def emit(self):
        ev_by_eng = {e: [] for e in ENGS}
        for eng, sn, fn, waits, n, pos in self.events:
            ev_by_eng[eng].append((sn, fn, waits, n, pos))
        nc = self.nc
        sem = self.sem
        tok_pos = self.tok_pos
        SELF = self.SELF_DIST

        def body(engine, eng_name):
            done = {}
            for sn, fn, waits, n, pos in ev_by_eng[eng_name]:
                for wsn, wval in waits:
                    info = tok_pos.get((wsn, wval))
                    if info is not None:
                        weng, dpos, wdma = info
                        if (weng == eng_name and not wdma
                                and pos - dpos > SELF):
                            continue
                    if done.get(wsn, 0) >= wval:
                        continue
                    engine.wait_ge(sem[wsn], wval)
                    done[wsn] = wval
                ins = fn(engine)
                ins.then_inc(sem[sn], n)

        with nc.Block() as blk:
            for e in ENGS:
                if ev_by_eng[e]:
                    getattr(blk, e)(lambda eng, e=e: body(eng, e))
        self.events = []
        self.pos = {e: 0 for e in ENGS}
        self.tok_pos = {}


def build_program(S=4096):
    PT1 = min(512, S)
    PSC = min(256, S)
    PT4 = min(256, S)
    NT1 = S // PT1
    NSC = S // PSC
    NT4 = S // PT4
    NG1 = PT1 // 128
    NG4 = max(1, PT4 // 128)

    nc = bass.Bass("TRN2", target_bir_lowering=False)

    x_in = nc.dram_tensor("x_in", [S, D], F32, kind="ExternalInput")
    w_in_d = nc.dram_tensor("w_in_d", [2 * D, D], F32, kind="ExternalInput")
    b_in_d = nc.dram_tensor("b_in_d", [2 * D], F32, kind="ExternalInput")
    w_h_d = nc.dram_tensor("w_h_d", [D, D], F32, kind="ExternalInput")
    b_h_d = nc.dram_tensor("b_h_d", [D], F32, kind="ExternalInput")
    w_out_d = nc.dram_tensor("w_out_d", [D, D], F32, kind="ExternalInput")
    b_out_d = nc.dram_tensor("b_out_d", [D], F32, kind="ExternalInput")
    ln_w_d = nc.dram_tensor("ln_w_d", [D], F32, kind="ExternalInput")
    ln_b_d = nc.dram_tensor("ln_b_d", [D], F32, kind="ExternalInput")
    w1_d = nc.dram_tensor("w1_d", [DH, D], F32, kind="ExternalInput")
    cw_d = nc.dram_tensor("cw_d", [DH, 3], F32, kind="ExternalInput")
    cb_d = nc.dram_tensor("cb_d", [DH], F32, kind="ExternalInput")
    w2_d = nc.dram_tensor("w2_d", [D, DH], F32, kind="ExternalInput")
    y_out = nc.dram_tensor("y_out", [S, D], F32, kind="ExternalOutput")

    import os as _os2
    _dbg = _os2.environ.get("KDBG") == "1"
    _kw = {"kind": "ExternalOutput"} if _dbg else {}
    z_scr = nc.dram_tensor("z_scr", [128, 8 * S], F32, **_kw)
    a_scr = nc.dram_tensor("a_scr", [128, 8 * S], F32, **_kw)
    hs_scr = nc.dram_tensor("hs_scr", [128, 8 * S], F32, **_kw)
    y1_scr = nc.dram_tensor("y1_scr", [S, D], F32, **_kw)
    xq_scr = nc.dram_tensor("xq_scr", [128, 8 * (S + 2)], BF16, **_kw)
    xq2_scr = nc.dram_tensor("xq2_scr", [128, 32 * S], BF16, **_kw)
    m1_scr = nc.dram_tensor("m1_scr", [S + 2, 1], F32, **_kw)
    dq2_scr = nc.dram_tensor("dq2_scr", [S, 1], F32, **_kw)

    A = nc.alloc_sbuf_tensor
    arf = A("arf", [128, 8192], F32)
    gen = A("gen", [128, 8192], F32)
    bfa = A("bfa", [128, 8192], BF16)
    bfb = A("bfb", [128, 8192], BF16)
    whT = A("whT", [128, 16384], BF16)
    wbig = A("wbig", [128, 32768], BF16)
    it32 = A("it32", [128, 512], I32)
    lnw_bc = A("lnw_bc", [128, 1024], F32)
    lnb_bc = A("lnb_bc", [128, 1024], F32)
    bu_fm = A("bu_fm", [128, 8], F32)
    zb_fm = A("zb_fm", [128, 8], F32)
    bh_fm = A("bh_fm", [128, 8], F32)
    cw_fm = A("cw_fm", [128, 96], F32)
    cb_fm = A("cb_fm", [128, 32], F32)
    fmat = A("fmat", [128, 4], F32)
    idn = A("idn", [128, 128], F32)
    idnb = A("idnb", [128, 128], BF16)
    ones_col = A("ones_col", [128, 1], F32)
    ones_rf = A("ones_rf", [1, 128], F32)
    ones_rb = A("ones_rb", [1, 128], BF16)
    c1_row = A("c1_row", [1, 128], F32)
    c2_col = A("c2_col", [128, 1], F32)
    thr1 = A("thr1", [128, 1], F32)
    thr2 = A("thr2", [128, 1], F32)
    scs = A("scs", [128, 16], F32)
    vrow = A("vrow", [1, 16], F32)
    zrow = A("zrow", [128, 16], BF16)
    pair = A("pair", [128, 16], BF16)
    dHf = A("dHf", [128, 8], F32)
    dsub = A("dsub", [128, 8], F32)
    dsub2 = A("dsub2", [128, 8], F32)
    dfull = A("dfull", [128, 8], F32)
    qsc = A("qsc", [128, 8], F32)
    cand = A("cand", [128, 8], F32)
    hcur = A("hcur", [128, 8], F32)
    mcol = A("mcol", [128, 8], F32)
    tstage = A("tstage", [128, 1024], BF16)
    bo_bf = wbig[:, 16384:18432]
    Ssc = arf[:, 6144:6400]
    mxt = gen[:, 4096:4096 + PT4]
    dq1b = gen[:, 4608:4608 + PT4 + 2]
    s2b = gen[:, 5632:5632 + PT4]
    mx_tm = gen[:, 6656:6656 + NG4]
    dq2_tm = gen[:, 6720:6720 + NG4]
    mrow = gen[:, 7168:7168 + PT4 + 2]

    p_y = nc.alloc_psum_tensor("p_y", [128, 512], F32)
    p_t0 = nc.alloc_psum_tensor("p_t0", [128, 512], F32)
    p_t1 = nc.alloc_psum_tensor("p_t1", [128, 512], F32)
    p_a = nc.alloc_psum_tensor("p_a", [128, 512], F32)
    p_b = nc.alloc_psum_tensor("p_b", [128, 512], F32)
    p_c = nc.alloc_psum_tensor("p_c", [128, 512], F32)
    p_t = [p_t0, p_t1]

    sch = Sched(nc)
    OP = sch.op
    global_state = {}
    BAR = nc.all_engine_barrier

    def flush():
        sch.emit()
        sch.fence()
        BAR()

    def scr_view(scr, t0, nsteps):
        return scr.rearrange("p (c t) -> p c t", c=8)[:, :, t0:t0 + nsteps]

    ldc = []
    ldc.append(OP("sync", lambda e: e.dma_start(out=lnw_bc[0:1, :], in_=ln_w_d[None, :]), dmainc=True))
    ldc.append(OP("sync", lambda e: e.dma_start(out=lnb_bc[0:1, :], in_=ln_b_d[None, :]), dmainc=True))
    ldc.append(OP("sync", lambda e: e.dma_start(
        out=bu_fm[:, :], in_=b_in_d.rearrange("(c p) -> p c", p=128)[:, 0:8], allow_slow_non_contiguous=True), dmainc=True))
    ldc.append(OP("sync", lambda e: e.dma_start(
        out=zb_fm[:, :], in_=b_in_d.rearrange("(c p) -> p c", p=128)[:, 8:16], allow_slow_non_contiguous=True), dmainc=True))
    ldc.append(OP("sync", lambda e: e.dma_start(
        out=bh_fm[:, :], in_=b_h_d.rearrange("(c p) -> p c", p=128), allow_slow_non_contiguous=True), dmainc=True))
    ldc.append(OP("sync", lambda e: e.dma_start(
        out=cw_fm.rearrange("p (c k) -> p c k", c=32), in_=cw_d.rearrange("(c p) k -> p c k", p=128), allow_slow_non_contiguous=True), dmainc=True))
    ldc.append(OP("sync", lambda e: e.dma_start(
        out=cb_fm[:, :], in_=cb_d.rearrange("(c p) -> p c", p=128), allow_slow_non_contiguous=True), dmainc=True))
    ldwh = []
    for i in range(8):
        ldwh.append(OP("sync", lambda e, i=i: e.dma_start(
            out=arf[:, 1024 * i:1024 * (i + 1)],
            in_=w_h_d[128 * i:128 * (i + 1), :]), dmainc=True))
    t_iota = OP("gpsimd", lambda e: e.iota(idn[:, :], [[1, 128]], channel_multiplier=-1,
                                           allow_small_or_imprecise_dtypes=True))
    for fn in (lambda e: e.memset(zrow[:, :], 0.0),
               lambda e: e.memset(ones_col[:, :], 1.0),
               lambda e: e.memset(ones_rf[:, :], 1.0),
               lambda e: e.memset(ones_rb[:, :], 1.0),
               lambda e: e.memset(hcur[:, :], 0.0),
               lambda e: e.memset(pair[:, :], 0.0),
               lambda e: e.memset(scs[:, :], 0.0),
               lambda e: e.memset(p_y[:, :], 0.0),
               lambda e: e.memset(p_t0[:, :], 0.0),
               lambda e: e.memset(p_t1[:, :], 0.0),
               lambda e: e.memset(fmat[:, :], 0.0)):
        fz = OP("vector", fn)
    for j in range(4):
        fz = OP("vector", lambda e, j=j: e.memset(fmat[32 * j:32 * j + 2, j:j + 1], 1.0),
                after=(fz,))
    flush()

    t1 = OP("vector", lambda e: e.tensor_scalar(out=idn[:, :], in0=idn[:, :],
                                                scalar1=0.0, scalar2=None, op0=ALU.is_equal))
    OP("vector", lambda e: e.tensor_copy(out=idnb[:, :], in_=idn[:, :]), after=(t1,))
    OP("vector", lambda e: e.tensor_tensor(out=zb_fm[:, :], in0=zb_fm[:, :],
                                           in1=bh_fm[:, :], op=ALU.add))
    m1 = OP("tensor", lambda e: e.matmul(p_c[:, 0:512], ones_rf[0:1, :], lnw_bc[0:1, 0:512],
                                         start=True, stop=True))
    m2 = OP("tensor", lambda e: e.matmul(p_b[:, 0:512], ones_rf[0:1, :], lnw_bc[0:1, 512:1024],
                                         start=True, stop=True))
    m3 = OP("tensor", lambda e: e.matmul(p_a[:, 0:512], ones_rf[0:1, :], lnb_bc[0:1, 0:512],
                                         start=True, stop=True))
    m4 = OP("tensor", lambda e: e.matmul(p_t0[:, 0:512], ones_rf[0:1, :], lnb_bc[0:1, 512:1024],
                                         start=True, stop=True))
    OP("scalar", lambda e: e.copy(out=lnw_bc[:, 0:512], in_=p_c[:, 0:512]), after=(m1,))
    OP("scalar", lambda e: e.copy(out=lnw_bc[:, 512:1024], in_=p_b[:, 0:512]), after=(m2,))
    OP("scalar", lambda e: e.copy(out=lnb_bc[:, 0:512], in_=p_a[:, 0:512]), after=(m3,))
    OP("scalar", lambda e: e.copy(out=lnb_bc[:, 512:1024], in_=p_t0[:, 0:512]), after=(m4,))
    flush()

    OP("vector", lambda e: e.memset(p_t0[:, :], 0.0))
    OP("vector", lambda e: e.memset(p_a[:, :], 0.0))
    OP("vector", lambda e: e.memset(p_b[:, :], 0.0))
    OP("vector", lambda e: e.memset(p_c[:, :], 0.0))
    OP("vector", lambda e: e.memset(mrow[0:1, :], 0.0))
    for c in range(8):
        OP("gpsimd", lambda e, c=c: e.dma_start(
            out=xq_scr[:, c * (S + 2):c * (S + 2) + 1], in_=zrow[:, 0:1], allow_slow_non_contiguous=True), dmainc=True)
        OP("gpsimd", lambda e, c=c: e.dma_start(
            out=xq_scr[:, c * (S + 2) + S + 1:c * (S + 2) + S + 2], in_=zrow[:, 0:1], allow_slow_non_contiguous=True), dmainc=True)
    OP("gpsimd", lambda e: e.dma_start(out=m1_scr[0:1, :], in_=scs[0:1, 0:1]), dmainc=True)
    OP("gpsimd", lambda e: e.dma_start(out=m1_scr[S + 1:S + 2, :], in_=scs[0:1, 0:1]), dmainc=True)
    flush()

    def split_hl(src_view, n, tmp0, deps=()):
        h = OP("vector", lambda e: e.tensor_copy(out=bfa[:, 0:n], in_=src_view), after=deps)
        u = OP("vector", lambda e: e.tensor_copy(out=gen[:, tmp0:tmp0 + n], in_=bfa[:, 0:n]),
               after=(h,))
        l0 = OP("vector", lambda e: e.tensor_tensor(out=gen[:, tmp0:tmp0 + n], in0=src_view,
                                                    in1=gen[:, tmp0:tmp0 + n], op=ALU.subtract),
                after=(u,))
        l = OP("vector", lambda e: e.tensor_copy(out=bfa[:, 4096:4096 + n],
                                                 in_=gen[:, tmp0:tmp0 + n]), after=(l0,))
        return h, l

    for rnd in range(2):
        hH, hL = split_hl(arf[:, 4096 * rnd:4096 * (rnd + 1)], 4096, 4096,
                          deps=(ldwh[-1],))
        for iloc in range(4):
            i = 4 * rnd + iloc
            j, u = i // 2, i % 2
            for m in range(8):
                for half, base in ((0, 0), (1, 4096)):
                    dep = hH if half == 0 else hL
                    OP("scalar", lambda e, iloc=iloc, j=j, u=u, m=m, half=half, base=base:
                       e.dma_start(
                           out=whT[:, 8192 * half + 256 * (4 * m + j) + 128 * u:
                                   8192 * half + 256 * (4 * m + j) + 128 * (u + 1)],
                           in_=bfa[:, base + 1024 * iloc + dm0(m):
                                   base + 1024 * iloc + dm0(m) + 128],
                           transpose=True), after=(dep, hL), dmainc=True)
        flush()

    for rnd in range(4):
        lds = []
        for iloc in range(4):
            i = 4 * rnd + iloc
            lds.append(OP("sync", lambda e, iloc=iloc, i=i: e.dma_start(
                out=gen[:, 1024 * iloc:1024 * (iloc + 1)],
                in_=w_in_d[128 * i:128 * (i + 1), :]), dmainc=True))
        hH, hL = split_hl(gen[:, 0:4096], 4096, 4096, deps=tuple(lds))
        for iloc in range(4):
            i = 4 * rnd + iloc
            for dc in range(8):
                for half, base, wb in ((0, 0, 0), (1, 4096, 16384)):
                    OP("scalar", lambda e, i=i, iloc=iloc, dc=dc, base=base, wb=wb:
                       e.dma_start(
                           out=wbig[:, wb + 2048 * dc + 128 * i:wb + 2048 * dc + 128 * (i + 1)],
                           in_=bfa[:, base + 1024 * iloc + 128 * dc:
                                   base + 1024 * iloc + 128 * (dc + 1)],
                           transpose=True), after=(hH, hL), dmainc=True)
        flush()

    for scol, wd, ntile in ((0, w1_d, 32), (8, w2_d, 8)):
        colw = wd.shape[1] if hasattr(wd, "shape") else None
        colw = 1024 if wd is w1_d else 4096
        per = 4096 // colw
        nrnd = ntile // per
        for rnd in range(nrnd):
            lds = []
            for k in range(per):
                i = per * rnd + k
                lds.append(OP("sync", lambda e, k=k, i=i, colw=colw, wd=wd: e.dma_start(
                    out=gen[:, colw * k:colw * (k + 1)],
                    in_=wd[128 * i:128 * (i + 1), :]), dmainc=True))
            OP("vector", lambda e, scol=scol, rnd=rnd: e.reduce_sum(
                out=scs[:, scol + rnd:scol + rnd + 1], in_=gen[:, 0:4096], axis=AX.X,
                apply_absolute_value=True), after=tuple(lds))
            flush()
    mm = OP("tensor", lambda e: e.matmul(p_c[0:1, 0:16], ones_col[:, 0:1], scs[:, 0:16],
                                         start=True, stop=True))
    v0 = OP("vector", lambda e: e.tensor_copy(out=vrow[0:1, 0:16], in_=p_c[0:1, 0:16]),
            after=(mm,))
    v1 = OP("vector", lambda e: e.reduce_sum(out=vrow[0:1, 0:1], in_=vrow[0:1, 0:8], axis=AX.X),
            after=(v0,))
    v2 = OP("vector", lambda e: e.reduce_sum(out=vrow[0:1, 1:2], in_=vrow[0:1, 8:16], axis=AX.X),
            after=(v0, v1))
    v3 = OP("vector", lambda e: e.tensor_scalar(out=vrow[0:1, 2:3], in0=vrow[0:1, 0:1],
                                                scalar1=1.0 / (DH * D), scalar2=1e-5,
                                                op0=ALU.mult, op1=ALU.max), after=(v1,))
    v4 = OP("vector", lambda e: e.tensor_scalar(out=vrow[0:1, 3:4], in0=vrow[0:1, 1:2],
                                                scalar1=1.0 / (DH * D), scalar2=1e-5,
                                                op0=ALU.mult, op1=ALU.max), after=(v2,))
    v5 = OP("vector", lambda e: e.tensor_scalar(out=c1_row[0:1, :], in0=ones_rf[0:1, :],
                                                scalar1=vrow[0:1, 2:3], scalar2=1.0 / 127.0,
                                                op0=ALU.mult, op1=ALU.mult), after=(v3,))
    v6 = OP("vector", lambda e: e.tensor_scalar(out=vrow[0:1, 4:5], in0=vrow[0:1, 2:3],
                                                scalar1=0.5, scalar2=None, op0=ALU.mult),
            after=(v3,))
    v7 = OP("vector", lambda e: e.tensor_scalar(out=vrow[0:1, 5:6], in0=vrow[0:1, 3:4],
                                                scalar1=0.5, scalar2=None, op0=ALU.mult),
            after=(v4,))
    v8 = OP("vector", lambda e: e.tensor_scalar(out=vrow[0:1, 6:7], in0=vrow[0:1, 3:4],
                                                scalar1=1.0 / 127.0, scalar2=None, op0=ALU.mult),
            after=(v4,))
    b1 = OP("tensor", lambda e: e.matmul(p_c[:, 32:33], ones_rf[0:1, :], vrow[0:1, 4:5],
                                         start=True, stop=True), after=(v6,))
    t_thr1 = OP("vector", lambda e: e.tensor_copy(out=thr1[:, :], in_=p_c[:, 32:33]), after=(b1,))
    b2 = OP("tensor", lambda e: e.matmul(p_c[:, 36:37], ones_rf[0:1, :], vrow[0:1, 5:6],
                                         start=True, stop=True), after=(v7, t_thr1))
    t_thr2 = OP("vector", lambda e: e.tensor_copy(out=thr2[:, :], in_=p_c[:, 36:37]), after=(b2,))
    b3 = OP("tensor", lambda e: e.matmul(p_c[:, 40:41], ones_rf[0:1, :], vrow[0:1, 6:7],
                                         start=True, stop=True), after=(v8, t_thr2))
    t_c2 = OP("vector", lambda e: e.tensor_copy(out=c2_col[:, :], in_=p_c[:, 40:41]), after=(b3,))
    if _dbg:
        dbg_scr = nc.dram_tensor("dbg_scr", [128, 64], F32, kind="ExternalOutput")
        OP("gpsimd", lambda e: e.dma_start(out=dbg_scr[:, 0:16], in_=scs[:, :], allow_slow_non_contiguous=True),
           after=(t_thr1,), dmainc=True)
        OP("gpsimd", lambda e: e.dma_start(out=dbg_scr[0:1, 16:32], in_=vrow[0:1, :], allow_slow_non_contiguous=True),
           after=(t_thr1, v5, v8), dmainc=True)
        OP("gpsimd", lambda e: e.dma_start(out=dbg_scr[:, 32:33], in_=thr1[:, :], allow_slow_non_contiguous=True),
           after=(t_thr1,), dmainc=True)
        OP("gpsimd", lambda e: e.dma_start(out=dbg_scr[:, 33:34], in_=thr2[:, :], allow_slow_non_contiguous=True),
           after=(t_thr2,), dmainc=True)
        OP("gpsimd", lambda e: e.dma_start(out=dbg_scr[:, 34:35], in_=c2_col[:, :], allow_slow_non_contiguous=True),
           after=(t_c2,), dmainc=True)
    flush()

    ln_state = {"prev": None}

    def ln_plain(src, dst, t_tmp, sq_tmp, ms, deps=()):
        if ln_state["prev"] is not None:
            deps = tuple(deps) + (ln_state["prev"],)
        a1 = OP("vector", lambda e: e.reduce_sum(out=mcol[:, ms:ms + 1], in_=src, axis=AX.X),
                after=deps)
        a2 = OP("vector", lambda e: e.tensor_scalar(out=mcol[:, ms:ms + 1], in0=mcol[:, ms:ms + 1],
                                                    scalar1=1.0 / D, scalar2=None, op0=ALU.mult),
                after=(a1,))
        a3 = OP("vector", lambda e: e.tensor_scalar(out=t_tmp, in0=src,
                                                    scalar1=mcol[:, ms:ms + 1], scalar2=None,
                                                    op0=ALU.subtract), after=(a2,))
        a4 = OP("vector", lambda e: e.tensor_tensor(out=sq_tmp, in0=t_tmp, in1=t_tmp,
                                                    op=ALU.mult), after=(a3,))
        a5 = OP("vector", lambda e: e.reduce_sum(out=mcol[:, ms + 1:ms + 2], in_=sq_tmp,
                                                 axis=AX.X), after=(a4,))
        a6p = OP("vector", lambda e: e.tensor_scalar(out=mcol[:, ms + 1:ms + 2],
                                                     in0=mcol[:, ms + 1:ms + 2],
                                                     scalar1=1.0 / D, scalar2=1e-5,
                                                     op0=ALU.mult, op1=ALU.add), after=(a5,))
        a6a = OP("scalar", lambda e: e.activation(out=mcol[:, ms + 1:ms + 2],
                                                  in_=mcol[:, ms + 1:ms + 2], func=AF.Sqrt),
                 after=(a6p,))
        a6 = OP("vector", lambda e: e.reciprocal(out=mcol[:, ms + 1:ms + 2],
                                                 in_=mcol[:, ms + 1:ms + 2]), after=(a6a,))
        a7 = OP("vector", lambda e: e.tensor_scalar(out=dst, in0=t_tmp,
                                                    scalar1=mcol[:, ms + 1:ms + 2], scalar2=None,
                                                    op0=ALU.mult), after=(a6, a3))
        ln_state["prev"] = a7
        return a7

    for r in range(NT1):
        ldx = []
        for g in range(NG1):
            ldx.append(OP("sync", lambda e, g=g, r=r: e.dma_start(
                out=arf[:, 1024 * g:1024 * (g + 1)],
                in_=x_in[PT1 * r + 128 * g:PT1 * r + 128 * (g + 1), :]), dmainc=True))
        trT = []
        for g in range(NG1):
            x0v = arf[:, 1024 * g:1024 * (g + 1)]
            n1tok = ln_plain(x0v, gen[:, 1024:2048], gen[:, 0:1024], gen[:, 4096:5120], 0,
                             deps=(ldx[g],))
            xntok = ln_plain(gen[:, 1024:2048], gen[:, 2048:3072], gen[:, 3072:4096],
                             gen[:, 4096:5120], 2, deps=(n1tok,))
            aff1 = OP("vector", lambda e: e.tensor_tensor(out=gen[:, 2048:3072],
                                                          in0=gen[:, 2048:3072],
                                                          in1=lnw_bc[:, :], op=ALU.mult),
                      after=(xntok,))
            aff2 = OP("vector", lambda e: e.tensor_tensor(out=gen[:, 2048:3072],
                                                          in0=gen[:, 2048:3072],
                                                          in1=lnb_bc[:, :], op=ALU.add),
                      after=(aff1,))
            h1 = OP("vector", lambda e, g=g: e.tensor_copy(
                out=bfa[:, 1024 * g:1024 * (g + 1)], in_=gen[:, 2048:3072]), after=(aff2,))
            h2 = OP("vector", lambda e, g=g: e.tensor_copy(out=gen[:, 4096:5120],
                                                           in_=bfa[:, 1024 * g:1024 * (g + 1)]),
                    after=(h1,))
            h3 = OP("vector", lambda e: e.tensor_tensor(out=gen[:, 5120:6144],
                                                        in0=gen[:, 2048:3072],
                                                        in1=gen[:, 4096:5120], op=ALU.subtract),
                    after=(h2,))
            h4 = OP("vector", lambda e, g=g: e.tensor_copy(
                out=bfa[:, 4096 + 1024 * g:4096 + 1024 * (g + 1)], in_=gen[:, 5120:6144]),
                after=(h3,))
            for dc in range(8):
                for half, base, ob, dep in ((0, 0, 0, h1), (1, 4096, 4096, h4)):
                    trT.append(OP("scalar", lambda e, g=g, dc=dc, base=base, ob=ob:
                                  e.dma_start(
                                      out=bfb[:, ob + PT1 * dc + 128 * g:
                                              ob + PT1 * dc + 128 * (g + 1)],
                                      in_=bfa[:, base + 1024 * g + 128 * dc:
                                              base + 1024 * g + 128 * (dc + 1)],
                                      transpose=True), after=(dep, h4), dmainc=True))
        prev_sp = {0: None, 1: None}
        global_state.setdefault("za_last", None)
        for ec in range(16):
            ps = p_a if ec % 2 == 0 else p_b
            deps = tuple(trT) + ((prev_sp[ec % 2],) if prev_sp[ec % 2] else ())
            mmt = None
            for dc in range(8):
                for wofs, xofs in ((0, 0), (0, 4096), (16384, 0)):
                    first = (dc == 0 and wofs == 0 and xofs == 0)
                    last = (dc == 7 and wofs == 16384)
                    mmt = OP("tensor", lambda e, ec=ec, dc=dc, wofs=wofs, xofs=xofs,
                             first=first, last=last, ps=ps:
                             e.matmul(ps[:, 0:PT1],
                                      wbig[:, wofs + 2048 * dc + 128 * ec:
                                           wofs + 2048 * dc + 128 * (ec + 1)],
                                      bfb[:, xofs + PT1 * dc:xofs + PT1 * (dc + 1)],
                                      start=first, stop=last),
                             after=deps if first else ())
            if ec < 8:
                ev = OP("scalar", lambda e, ec=ec, ps=ps: e.activation(
                    out=gen[:, 6144 + 1024 * (ec % 2):6144 + 1024 * (ec % 2) + PT1],
                    in_=ps[:, 0:PT1], func=AF.Sigmoid, bias=bu_fm[:, ec:ec + 1], scale=1.0),
                    after=(mmt,))
            else:
                ev = OP("vector", lambda e, ec=ec, ps=ps: e.tensor_scalar(
                    out=gen[:, 6144 + 1024 * (ec % 2):6144 + 1024 * (ec % 2) + PT1],
                    in0=ps[:, 0:PT1], scalar1=zb_fm[:, ec - 8:ec - 7], scalar2=None,
                    op0=ALU.add), after=(mmt,))
            scr = a_scr if ec < 8 else z_scr
            c = (ec % 8 % 2) * 4 + (ec % 8) // 2
            prev_sp[ec % 2] = OP(
                "gpsimd", lambda e, scr=scr, c=c, r=r, ec=ec: e.dma_start(
                    out=scr[:, S * c + PT1 * r:S * c + PT1 * (r + 1)],
                    in_=gen[:, 6144 + 1024 * (ec % 2):6144 + 1024 * (ec % 2) + PT1]),
                after=(ev,), dmainc=True)
            global_state["za_last"] = prev_sp[ec % 2]
        flush()

    import os as _os
    if _os.environ.get("KSTOP") == "p1":
        return nc

    for rnd in range(2):
        lds = []
        for iloc in range(4):
            i = 4 * rnd + iloc
            lds.append(OP("sync", lambda e, iloc=iloc, i=i: e.dma_start(
                out=gen[:, 1024 * iloc:1024 * (iloc + 1)],
                in_=w_out_d[128 * i:128 * (i + 1), :]), dmainc=True))
        hH, hL = split_hl(gen[:, 0:4096], 4096, 4096, deps=tuple(lds))
        for iloc in range(4):
            i = 4 * rnd + iloc
            for dc in range(8):
                for half, base, wb in ((0, 0, 0), (1, 4096, 8192)):
                    OP("scalar", lambda e, i=i, iloc=iloc, dc=dc, base=base, wb=wb:
                       e.dma_start(
                           out=wbig[:, wb + 1024 * dc + 128 * i:wb + 1024 * dc + 128 * (i + 1)],
                           in_=bfa[:, base + 1024 * iloc + 128 * dc:
                                   base + 1024 * iloc + 128 * (dc + 1)],
                           transpose=True), after=(hH, hL), dmainc=True)
        flush()

    bl = OP("sync", lambda e: e.dma_start(out=gen[0:1, 0:1024], in_=b_out_d[None, :]),
            dmainc=True)
    u1 = OP("vector", lambda e: e.tensor_copy(out=bo_bf[0:1, 0:1024], in_=gen[0:1, 0:1024]),
            after=(bl,))
    u2 = OP("vector", lambda e: e.tensor_copy(out=gen[0:1, 1024:2048], in_=bo_bf[0:1, 0:1024]),
            after=(u1,))
    u3 = OP("vector", lambda e: e.tensor_tensor(out=gen[0:1, 2048:3072], in0=gen[0:1, 0:1024],
                                                in1=gen[0:1, 1024:2048], op=ALU.subtract),
            after=(u2,))
    OP("vector", lambda e: e.tensor_copy(out=bo_bf[0:1, 1024:2048], in_=gen[0:1, 2048:3072]),
       after=(u3,))
    flush()

    whH = lambda m, j: whT[:, 256 * (4 * m + j):256 * (4 * m + j) + 256]
    whL = lambda m, j: whT[:, 8192 + 256 * (4 * m + j):8192 + 256 * (4 * m + j) + 256]
    zr = arf[:, 0:8 * PSC]
    ar = arf[:, 2048:2048 + 8 * PSC]
    hr = arf[:, 4096:4096 + 8 * PSC]
    zh = bfa[:, 0:8 * PSC]
    zl2 = bfa[:, 2048:2048 + 8 * PSC]

    def slot(ring, tl):
        return ring[:, tl:tl + 1 + 7 * PSC:PSC]

    prev_split = None
    prev_bulk = None
    prev_zmm = None
    prev_hs_store = None
    prev_hw = None
    for r in range(NSC):
        dep_r0 = (global_state["za_last"],) if r == 0 else ()
        zl = OP("sync", lambda e, r=r: e.dma_start(
            out=zr.rearrange("p (c t) -> p c t", c=8), in_=scr_view(z_scr, PSC * r, PSC)),
            after=((prev_bulk,) if prev_bulk else ()) + dep_r0, dmainc=True)
        al = OP("sync", lambda e, r=r: e.dma_start(
            out=ar.rearrange("p (c t) -> p c t", c=8), in_=scr_view(a_scr, PSC * r, PSC)),
            after=((prev_split,) if prev_split else ()) + dep_r0, dmainc=True)
        b1 = OP("vector", lambda e: e.tensor_copy(out=zh[:, :], in_=zr[:, :]),
                after=(zl,) + ((prev_zmm,) if prev_zmm else ()))
        b2 = OP("vector", lambda e: e.tensor_copy(out=gen[:, 0:8 * PSC], in_=zh[:, :]),
                after=(b1,))
        b3 = OP("vector", lambda e: e.tensor_tensor(out=gen[:, 0:8 * PSC], in0=zr[:, :],
                                                    in1=gen[:, 0:8 * PSC],
                                                    op=ALU.subtract), after=(b2,))
        b4 = OP("vector", lambda e: e.tensor_copy(out=zl2[:, :], in_=gen[:, 0:8 * PSC]),
                after=(b3,))
        prev_bulk = b4
        for tloc in range(PSC):
            t = PSC * r + tloc
            mmt = None
            dep0 = (prev_split,) if prev_split else ()
            for m in range(8):
                for j in range(4):
                    pr = pair[:, m:m + 9:8]
                    mmt = OP("tensor", lambda e, m=m, j=j, pr=pr: e.matmul(
                        p_y[32 * j:32 * j + 2, 0:256], pr, whH(m, j),
                        start=False, stop=False, tile_position=(0, 32 * j),
                        skip_group_check=True),
                        after=dep0 if (m == 0 and j == 0) else ())
                    mmt = OP("tensor", lambda e, m=m, j=j, pr=pr: e.matmul(
                        p_y[32 * j:32 * j + 2, 0:256], pr, whL(m, j),
                        start=False, stop=False, tile_position=(0, 32 * j),
                        skip_group_check=True))
            ev = OP("scalar", lambda e: e.activation(out=Ssc[:, :], in_=p_y[0:128, 0:256],
                                                     func=AF.Copy), after=(mmt,))
            f1 = OP("tensor", lambda e, t=t: e.matmul(p_t[t % 2][0:128, 0:4],
                                                      Ssc[0:128, 0:128], fmat[:, :],
                                                      start=True, stop=False), after=(ev,))
            f2 = OP("tensor", lambda e, t=t: e.matmul(p_t[t % 2][0:128, 4:8],
                                                      Ssc[0:128, 128:256], fmat[:, :],
                                                      start=False, stop=False))
            z1 = OP("tensor", lambda e, t=t, tloc=tloc: e.matmul(
                p_t[t % 2][0:128, 0:8], idnb[:, :], slot(zh, tloc),
                start=False, stop=False), after=(b4,) if tloc == 0 else ())
            z2 = OP("tensor", lambda e, t=t, tloc=tloc: e.matmul(
                p_t[t % 2][0:128, 0:8], idnb[:, :], slot(zl2, tloc),
                start=False, stop=True))
            prev_zmm = z2
            th = OP("scalar", lambda e, t=t: e.activation(
                out=cand[:, :], in_=p_t[t % 2][0:128, 0:8], func=AF.Tanh), after=(z2,))
            hprev = hcur[:, :] if t == 0 else slot(hr, (tloc - 1) % PSC)
            s1 = OP("vector", lambda e, hprev=hprev: e.tensor_tensor(
                out=dsub2[:, :], in0=cand[:, :], in1=hprev, op=ALU.subtract),
                after=(th,) + ((prev_hw,) if prev_hw else ()))
            s2 = OP("vector", lambda e, tloc=tloc: e.tensor_tensor(
                out=dfull[:, :], in0=dsub2[:, :], in1=slot(ar, tloc), op=ALU.mult),
                after=(s1, al))
            p1 = OP("scalar", lambda e: e.activation(out=pair[:, 0:8], in_=dfull[:, :],
                                                     func=AF.Copy), after=(s2,))
            p2 = OP("vector", lambda e: e.tensor_copy(out=dHf[:, :], in_=pair[:, 0:8]),
                    after=(p1,))
            p3 = OP("vector", lambda e: e.tensor_tensor(out=dsub[:, :], in0=dfull[:, :],
                                                        in1=dHf[:, :], op=ALU.subtract),
                    after=(p2,))
            p4 = OP("scalar", lambda e: e.activation(out=pair[:, 8:16], in_=dsub[:, :],
                                                     func=AF.Copy), after=(p3,))
            prev_split = p4
            prev_hw = OP("vector", lambda e, tloc=tloc, hprev=hprev: e.tensor_tensor(
                out=slot(hr, tloc), in0=hprev, in1=dfull[:, :], op=ALU.add),
                after=(s2,) + ((prev_hs_store,) if (tloc == 0 and prev_hs_store) else ()))
        prev_hs_store = OP("gpsimd", lambda e, r=r: e.dma_start(
            out=scr_view(hs_scr, PSC * r, PSC), in_=hr.rearrange("p (c t) -> p c t", c=8)),
            after=(prev_split, prev_hw), dmainc=True)
        sch.emit()
    flush()

    global_state["hs_last"] = prev_hs_store
    import os as _os
    if _os.environ.get("KSTOP") != "p2":
        build_rest(nc, sch, locals())
    return nc


def build_rest(nc, sch, env):
    OP = sch.op
    BAR = nc.all_engine_barrier

    def flush():
        sch.emit()
        sch.fence()
        BAR()

    S = env["S"]
    PT1, PT4 = env["PT1"], env["PT4"]
    NT1, NT4 = env["NT1"], env["NT4"]
    NG1, NG4 = env["NG1"], env["NG4"]
    names = ("arf gen bfa bfb whT wbig it32 lnw_bc lnb_bc bo_bf cw_fm cb_fm "
             "idn ones_col ones_rf ones_rb c1_row c2_col thr1 thr2 scs vrow "
             "mcol mrow dq1b s2b mx_tm dq2_tm mxt tstage p_a p_b p_c p_t0 "
             "p_t1 p_y x_in w1_d w2_d y_out z_scr a_scr hs_scr y1_scr xq_scr "
             "xq2_scr m1_scr dq2_scr").split()
    (arf, gen, bfa, bfb, whT, wbig, it32, lnw_bc, lnb_bc, bo_bf, cw_fm, cb_fm,
     idn, ones_col, ones_rf, ones_rb, c1_row, c2_col, thr1, thr2, scs, vrow,
     mcol, mrow, dq1b, s2b, mx_tm, dq2_tm, mxt, tstage, p_a, p_b, p_c, p_t0,
     p_t1, p_y, x_in, w1_d, w2_d, y_out, z_scr, a_scr, hs_scr, y1_scr, xq_scr,
     xq2_scr, m1_scr, dq2_scr) = (env[k] for k in names)
    ln_plain = env["ln_plain"]
    scr_view = env["scr_view"]
    global_state = env["global_state"]

    def nch(m):
        return dm0(m) // 128

    for r in range(NT1):
        ldx = []
        for g in range(NG1):
            ldx.append(OP("sync", lambda e, g=g, r=r: e.dma_start(
                out=arf[:, 1024 * g:1024 * (g + 1)],
                in_=x_in[PT1 * r + 128 * g:PT1 * r + 128 * (g + 1), :]), dmainc=True))
        hsl = OP("sync", lambda e, r=r: e.dma_start(
            out=arf[:, 4096:4096 + 8 * PT1].rearrange("p (c t) -> p c t", c=8),
            in_=scr_view(hs_scr, PT1 * r, PT1)),
            after=(global_state["hs_last"],) if r == 0 else (), dmainc=True)
        hH = OP("vector", lambda e: e.tensor_copy(out=bfb[:, 0:4096],
                                                  in_=arf[:, 4096:8192]), after=(hsl,))
        hU = OP("vector", lambda e: e.tensor_copy(out=gen[:, 4096:8192], in_=bfb[:, 0:4096]),
                after=(hH,))
        hS = OP("vector", lambda e: e.tensor_tensor(out=gen[:, 4096:8192],
                                                    in0=arf[:, 4096:8192],
                                                    in1=gen[:, 4096:8192], op=ALU.subtract),
                after=(hU,))
        hL = OP("vector", lambda e: e.tensor_copy(out=bfb[:, 4096:8192],
                                                  in_=gen[:, 4096:8192]), after=(hS,))
        prev_q = None
        for g in range(NG1):
            x0v = arf[:, 1024 * g:1024 * (g + 1)]
            mm_done = []
            for eh, ps in ((0, p_a), (1, p_b)):
                mmt = None
                first = True
                for m in range(8):
                    for sofs, wofs in ((0, 0), (0, 8192), (4096, 0)):
                        fdeps = (hH, hL) if first else ()
                        if first and prev_q is not None:
                            fdeps = fdeps + (prev_q,)
                        mmt = OP("tensor", lambda e, g=g, m=m, sofs=sofs, wofs=wofs,
                                 eh=eh, ps=ps, first=first: e.matmul(
                                     ps[:, 0:512],
                                     bfb[:, sofs + PT1 * m + 128 * g:
                                         sofs + PT1 * m + 128 * (g + 1)],
                                     wbig[:, wofs + 1024 * nch(m) + 512 * eh:
                                          wofs + 1024 * nch(m) + 512 * (eh + 1)],
                                     start=first, stop=False), after=fdeps)
                        first = False
                mmt = OP("tensor", lambda e, eh=eh, ps=ps: e.matmul(
                    ps[:, 0:512], ones_rb[0:1, :], bo_bf[0:1, 512 * eh:512 * (eh + 1)],
                    start=False, stop=False))
                mmt = OP("tensor", lambda e, eh=eh, ps=ps: e.matmul(
                    ps[:, 0:512], ones_rb[0:1, :],
                    bo_bf[0:1, 1024 + 512 * eh:1024 + 512 * (eh + 1)],
                    start=False, stop=True))
                mm_done.append(mmt)
            n1tok = ln_plain(x0v, gen[:, 1024:2048], gen[:, 0:1024], gen[:, 2048:3072], 0,
                             deps=(ldx[g],))
            aA = OP("vector", lambda e: e.tensor_tensor(out=gen[:, 2048:2560],
                                                        in0=p_a[:, 0:512],
                                                        in1=gen[:, 1024:1536], op=ALU.add),
                    after=(mm_done[0], n1tok))
            aB = OP("vector", lambda e: e.tensor_tensor(out=gen[:, 2560:3072],
                                                        in0=p_b[:, 0:512],
                                                        in1=gen[:, 1536:2048], op=ALU.add),
                    after=(mm_done[1], n1tok))
            ln3 = ln_plain(gen[:, 2048:3072], gen[:, 3072:4096], gen[:, 0:1024],
                           gen[:, 1024:2048], 2, deps=(aA, aB))
            y1v = OP("vector", lambda e, x0v=x0v: e.tensor_tensor(
                out=gen[:, 2048:3072], in0=x0v,
                in1=gen[:, 3072:4096], op=ALU.add), after=(ln3,))
            global_state["y1_last"] = OP("gpsimd", lambda e, r=r, g=g: e.dma_start(
                out=y1_scr[PT1 * r + 128 * g:PT1 * r + 128 * (g + 1), :],
                in_=gen[:, 2048:3072]), after=(y1v,), dmainc=True)
            n2tok = ln_plain(gen[:, 2048:3072], gen[:, 3072:4096], gen[:, 0:1024],
                             gen[:, 1024:2048], 4, deps=(y1v,))
            q1 = OP("vector", lambda e: e.reduce_max(out=mcol[:, 6:7], in_=gen[:, 3072:4096],
                                                     axis=AX.X, apply_absolute_value=True),
                    after=(n2tok,))
            q2 = OP("vector", lambda e: e.tensor_scalar(out=mcol[:, 6:7], in0=mcol[:, 6:7],
                                                        scalar1=1e-5, scalar2=None,
                                                        op0=ALU.max), after=(q1,))
            global_state["m1_last"] = OP("gpsimd", lambda e, r=r, g=g: e.dma_start(
                out=m1_scr[1 + PT1 * r + 128 * g:1 + PT1 * r + 128 * (g + 1), :],
                in_=mcol[:, 6:7]), after=(q2,), dmainc=True)
            q3 = OP("vector", lambda e: e.reciprocal(out=mcol[:, 7:8], in_=mcol[:, 6:7]),
                    after=(q2,))
            q4 = OP("vector", lambda e: e.tensor_scalar(out=gen[:, 0:1024],
                                                        in0=gen[:, 3072:4096],
                                                        scalar1=mcol[:, 7:8], scalar2=127.0,
                                                        op0=ALU.mult, op1=ALU.mult),
                    after=(q3,))
            q5a = OP("gpsimd", lambda e: e.tensor_copy(out=it32[:, :], in_=gen[:, 0:512]),
                     after=(q4,))
            q5b = OP("gpsimd", lambda e: e.tensor_copy(out=gen[:, 1024:1536], in_=it32[:, :]),
                     after=(q5a,))
            q5c = OP("gpsimd", lambda e: e.tensor_copy(out=it32[:, :], in_=gen[:, 512:1024]),
                     after=(q5b,))
            q6 = OP("gpsimd", lambda e: e.tensor_copy(out=gen[:, 1536:2048], in_=it32[:, :]),
                    after=(q5c,))
            q7 = OP("vector", lambda e: e.tensor_tensor(out=gen[:, 2048:3072],
                                                        in0=gen[:, 0:1024],
                                                        in1=gen[:, 1024:2048],
                                                        op=ALU.subtract),
                    after=(q6, global_state["y1_last"]))
            q8 = OP("vector", lambda e: e.scalar_tensor_tensor(out=gen[:, 3072:4096],
                                                               in0=gen[:, 2048:3072],
                                                               scalar=0.5,
                                                               in1=gen[:, 1024:2048],
                                                               op0=ALU.is_gt, op1=ALU.add),
                    after=(q7,))
            q9 = OP("vector", lambda e: e.tensor_scalar(out=gen[:, 0:1024],
                                                        in0=gen[:, 2048:3072],
                                                        scalar1=-0.5, scalar2=None,
                                                        op0=ALU.is_lt), after=(q7,))
            qa_ = OP("vector", lambda e: e.scalar_tensor_tensor(out=gen[:, 1024:2048],
                                                                in0=gen[:, 0:1024],
                                                                scalar=-1.0,
                                                                in1=gen[:, 3072:4096],
                                                                op0=ALU.mult, op1=ALU.add),
                     after=(q8, q9))
            qb_ = OP("vector", lambda e: e.tensor_copy(out=bfa[:, 0:1024],
                                                       in_=gen[:, 1024:2048]), after=(qa_,))
            prev_q = qb_
            prev_xq = global_state.get("xq_last")
            trs = None
            for dc in range(8):
                trs = OP("scalar", lambda e, dc=dc: e.dma_start(
                    out=tstage[:, 128 * dc:128 * (dc + 1)],
                    in_=bfa[:, 128 * dc:128 * (dc + 1)], transpose=True),
                    after=(qb_,) + ((prev_xq,) if prev_xq else ()), dmainc=True)
            global_state["xq_last"] = OP(
                "gpsimd", lambda e, r=r, g=g: e.dma_start(
                    out=xq_scr.rearrange("p (c t) -> p c t", c=8)[
                        :, :, 1 + PT1 * r + 128 * g:1 + PT1 * r + 128 * (g + 1)],
                    in_=tstage.rearrange("p (c t) -> p c t", c=8)),
                after=(trs,), dmainc=True)
        flush()

    for rnd in range(DH // 512):
        lds = []
        for k in range(4):
            i = 4 * rnd + k
            lds.append(OP("sync", lambda e, k=k, i=i: e.dma_start(
                out=gen[:, 1024 * k:1024 * (k + 1)],
                in_=w1_d[128 * i:128 * (i + 1), :]), dmainc=True))
        g1a = OP("vector", lambda e: e.scalar_tensor_tensor(
            out=gen[:, 4096:8192], in0=gen[:, 0:4096], scalar=-1.0,
            in1=gen[:, 0:4096], op0=ALU.mult, op1=ALU.max), after=tuple(lds))
        g1 = OP("vector", lambda e: e.tensor_scalar(out=gen[:, 4096:8192],
                                                    in0=gen[:, 4096:8192],
                                                    scalar1=thr1[:, 0:1], scalar2=None,
                                                    op0=ALU.is_ge), after=(g1a,))
        g2 = OP("scalar", lambda e: e.activation(out=arf[:, 0:4096], in_=gen[:, 0:4096],
                                                 func=AF.Sign), after=tuple(lds))
        g3 = OP("vector", lambda e: e.tensor_tensor(out=arf[:, 4096:8192],
                                                    in0=gen[:, 4096:8192],
                                                    in1=arf[:, 0:4096], op=ALU.mult),
                after=(g1, g2))
        g4 = OP("vector", lambda e: e.tensor_copy(out=bfa[:, 0:4096], in_=arf[:, 4096:8192]),
                after=(g3,))
        for k in range(4):
            ht = 4 * rnd + k
            for dc in range(8):
                OP("scalar", lambda e, k=k, ht=ht, dc=dc: e.dma_start(
                    out=wbig[:, 4096 * dc + 128 * ht:4096 * dc + 128 * (ht + 1)],
                    in_=bfa[:, 1024 * k + 128 * dc:1024 * k + 128 * (dc + 1)],
                    transpose=True), after=(g4,), dmainc=True)
        flush()

    for r in range(NT4):
        t0 = PT4 * r
        bdep = (global_state["xq_last"], global_state["m1_last"]) if r == 0 else ()
        xl = OP("sync", lambda e, t0=t0: e.dma_start(
            out=bfb[:, 0:8 * (PT4 + 2)].rearrange("p (c t) -> p c t", c=8),
            in_=xq_scr.rearrange("p (c t) -> p c t", c=8)[:, :, t0:t0 + PT4 + 2]),
            after=bdep, dmainc=True)
        ml = OP("sync", lambda e, t0=t0: e.dma_start(
            out=mrow[0:1, :], in_=m1_scr[t0:t0 + PT4 + 2, :].rearrange("t a -> a t"), allow_slow_non_contiguous=True),
            after=bdep, dmainc=True)
        bq = OP("tensor", lambda e: e.matmul(p_c[:, 0:PT4 + 2], c1_row[0:1, :], mrow[0:1, :],
                                             start=True, stop=True), after=(ml,))
        bqe = OP("vector", lambda e: e.tensor_copy(out=dq1b[:, :], in_=p_c[:, 0:PT4 + 2]),
                 after=(bq,))
        mxprev = OP("vector", lambda e: e.memset(mxt[:, :], 0.0))
        prev_ev = {0: None, 1: None}
        sil_ev = []
        for hc in range(32):
            ps = p_a if hc % 2 == 0 else p_b
            mmt = None
            for dc in range(8):
                fdeps = ()
                if dc == 0:
                    fdeps = (xl,)
                    if prev_ev[hc % 2] is not None:
                        fdeps = fdeps + (prev_ev[hc % 2],)
                mmt = OP("tensor", lambda e, hc=hc, dc=dc, ps=ps: e.matmul(
                    ps[:, 0:PT4 + 2],
                    wbig[:, 4096 * dc + 128 * hc:4096 * dc + 128 * (hc + 1)],
                    bfb[:, (PT4 + 2) * dc:(PT4 + 2) * (dc + 1)],
                    start=(dc == 0), stop=(dc == 7)), after=fdeps)
            dv = OP("vector", lambda e, ps=ps: e.tensor_tensor(out=gen[:, 0:PT4 + 2],
                                                               in0=ps[:, 0:PT4 + 2],
                                                               in1=dq1b[:, :], op=ALU.mult),
                    after=(mmt, bqe))
            prev_ev[hc % 2] = dv
            c1v = OP("vector", lambda e, hc=hc: e.tensor_scalar(
                out=gen[:, 1024:1024 + PT4], in0=gen[:, 1:1 + PT4],
                scalar1=cw_fm[:, 3 * hc + 1:3 * hc + 2], scalar2=cb_fm[:, hc:hc + 1],
                op0=ALU.mult, op1=ALU.add), after=(dv,))
            c2v = OP("vector", lambda e, hc=hc: e.scalar_tensor_tensor(
                out=gen[:, 2048:2048 + PT4], in0=gen[:, 0:PT4],
                scalar=cw_fm[:, 3 * hc:3 * hc + 1], in1=gen[:, 1024:1024 + PT4],
                op0=ALU.mult, op1=ALU.add), after=(c1v,))
            c3v = OP("vector", lambda e, hc=hc: e.scalar_tensor_tensor(
                out=gen[:, 1024:1024 + PT4], in0=gen[:, 2:2 + PT4],
                scalar=cw_fm[:, 3 * hc + 2:3 * hc + 3], in1=gen[:, 2048:2048 + PT4],
                op0=ALU.mult, op1=ALU.add), after=(c2v,))
            sl = OP("scalar", lambda e, hc=hc: e.activation(
                out=arf[:, PT4 * hc:PT4 * (hc + 1)], in_=gen[:, 1024:1024 + PT4],
                func=AF.Silu), after=(c3v,))
            mxa = OP("vector", lambda e, hc=hc: e.scalar_tensor_tensor(
                out=gen[:, 3072:3072 + PT4], in0=arf[:, PT4 * hc:PT4 * (hc + 1)],
                scalar=-1.0, in1=arf[:, PT4 * hc:PT4 * (hc + 1)],
                op0=ALU.mult, op1=ALU.max), after=(sl, mxprev))
            mxprev = OP("vector", lambda e: e.tensor_tensor(
                out=mxt[:, :], in0=mxt[:, :], in1=gen[:, 3072:3072 + PT4],
                op=ALU.max), after=(mxa,))
            sil_ev.append(sl)
        rds = []
        trprev = mxprev
        for g in range(NG4):
            tr = OP("tensor", lambda e, g=g: e.transpose(
                p_c[0:128, 128:256], mxt[:, 128 * g:128 * (g + 1)], idn[:, :]),
                after=(trprev, bqe))
            rd = OP("vector", lambda e, g=g: e.reduce_max(
                out=mx_tm[:, g:g + 1], in_=p_c[0:128, 128:256], axis=AX.X), after=(tr,))
            trprev = rd
            rds.append(rd)
        cl = OP("vector", lambda e: e.tensor_scalar(out=mx_tm[:, :], in0=mx_tm[:, :],
                                                    scalar1=1e-5, scalar2=None, op0=ALU.max),
                after=tuple(rds))
        dq2v = OP("vector", lambda e: e.tensor_scalar(out=dq2_tm[:, :], in0=mx_tm[:, :],
                                                      scalar1=c2_col[:, 0:1], scalar2=None,
                                                      op0=ALU.mult), after=(cl,))
        global_state["dq2_last"] = OP("gpsimd", lambda e, t0=t0: e.dma_start(
            out=dq2_scr[t0:t0 + PT4, :].rearrange("(g p) a -> p (g a)", p=128),
            in_=dq2_tm[:, :], allow_slow_non_contiguous=True), after=(dq2v,), dmainc=True)
        rc = OP("vector", lambda e: e.reciprocal(out=mx_tm[:, :], in_=mx_tm[:, :]),
                after=(cl, dq2v))
        s2toks = []
        prevrow = rc
        for g in range(NG4):
            trr = OP("tensor", lambda e, g=g: e.matmul(
                p_c[0:1, 256 + 128 * g:256 + 128 * (g + 1)], mx_tm[:, g:g + 1], idn[:, :],
                start=(g == 0), stop=(g == NG4 - 1)), after=(prevrow,))
            prevrow = trr
            s2toks.append(trr)
        s2r_ = OP("vector", lambda e: e.tensor_scalar(
            out=gen[0:1, 0:PT4], in0=p_c[0:1, 256:256 + PT4], scalar1=127.0,
            scalar2=None, op0=ALU.mult), after=tuple(s2toks))
        bq2 = OP("tensor", lambda e: e.matmul(p_c[:, 256:256 + PT4], ones_rf[0:1, :],
                                              gen[0:1, 0:PT4], start=True, stop=True),
                 after=(s2r_,))
        s2be = OP("vector", lambda e: e.tensor_copy(out=s2b[:, :], in_=p_c[:, 256:256 + PT4]),
                  after=(bq2,))
        prev_st = None
        st_hist = []
        for hc in range(32):
            u = OP("vector", lambda e, hc=hc: e.tensor_tensor(
                out=gen[:, 0:PT4], in0=arf[:, PT4 * hc:PT4 * (hc + 1)], in1=s2b[:, :],
                op=ALU.mult), after=(s2be, sil_ev[hc]) + ((prev_st,) if prev_st else ()))
            i1 = OP("gpsimd", lambda e: e.tensor_copy(out=it32[:, 0:PT4], in_=gen[:, 0:PT4]),
                    after=(u,))
            i2 = OP("gpsimd", lambda e: e.tensor_copy(out=gen[:, 1024:1024 + PT4],
                                                      in_=it32[:, 0:PT4]), after=(i1,))
            d1 = OP("vector", lambda e: e.tensor_tensor(out=gen[:, 2048:2048 + PT4],
                                                        in0=gen[:, 0:PT4],
                                                        in1=gen[:, 1024:1024 + PT4],
                                                        op=ALU.subtract), after=(i2,))
            e1 = OP("vector", lambda e: e.scalar_tensor_tensor(out=gen[:, 3072:3072 + PT4],
                                                               in0=gen[:, 2048:2048 + PT4],
                                                               scalar=0.5,
                                                               in1=gen[:, 1024:1024 + PT4],
                                                               op0=ALU.is_gt, op1=ALU.add),
                    after=(d1,))
            e2 = OP("vector", lambda e: e.tensor_scalar(out=gen[:, 0:PT4],
                                                        in0=gen[:, 2048:2048 + PT4],
                                                        scalar1=-0.5, scalar2=None,
                                                        op0=ALU.is_lt), after=(d1,))
            e3 = OP("vector", lambda e: e.scalar_tensor_tensor(out=gen[:, 1024:1024 + PT4],
                                                               in0=gen[:, 0:PT4],
                                                               scalar=-1.0,
                                                               in1=gen[:, 3072:3072 + PT4],
                                                               op0=ALU.mult, op1=ALU.add),
                    after=(e1, e2))
            e4deps = (e3,)
            if len(st_hist) >= 8:
                e4deps = e4deps + (st_hist[-8],)
            e4 = OP("vector", lambda e, hc=hc: e.tensor_copy(
                out=bfa[:, PT4 * (hc % 8):PT4 * (hc % 8 + 1)], in_=gen[:, 1024:1024 + PT4]),
                after=e4deps)
            prev_st = OP("gpsimd", lambda e, hc=hc, t0=t0: e.dma_start(
                out=xq2_scr[:, S * hc + t0:S * hc + t0 + PT4],
                in_=bfa[:, PT4 * (hc % 8):PT4 * (hc % 8 + 1)]), after=(e4,), dmainc=True)
            st_hist.append(prev_st)
            global_state["xq2_last"] = prev_st
        flush()

    for i in range(8):
        ld = OP("sync", lambda e, i=i: e.dma_start(
            out=gen[:, 0:4096], in_=w2_d[128 * i:128 * (i + 1), :]), dmainc=True)
        g1a = OP("vector", lambda e: e.scalar_tensor_tensor(
            out=gen[:, 4096:8192], in0=gen[:, 0:4096], scalar=-1.0,
            in1=gen[:, 0:4096], op0=ALU.mult, op1=ALU.max), after=(ld,))
        g1 = OP("vector", lambda e: e.tensor_scalar(out=gen[:, 4096:8192],
                                                    in0=gen[:, 4096:8192],
                                                    scalar1=thr2[:, 0:1], scalar2=None,
                                                    op0=ALU.is_ge), after=(g1a,))
        g2 = OP("scalar", lambda e: e.activation(out=arf[:, 0:4096], in_=gen[:, 0:4096],
                                                 func=AF.Sign), after=(ld,))
        g3 = OP("vector", lambda e: e.tensor_tensor(out=arf[:, 4096:8192],
                                                    in0=gen[:, 4096:8192],
                                                    in1=arf[:, 0:4096], op=ALU.mult),
                after=(g1, g2))
        g4 = OP("vector", lambda e: e.tensor_copy(out=bfa[:, 0:4096], in_=arf[:, 4096:8192]),
                after=(g3,))
        for hc in range(32):
            OP("scalar", lambda e, hc=hc, i=i: e.dma_start(
                out=wbig[:, 1024 * hc + 128 * i:1024 * hc + 128 * (i + 1)],
                in_=bfa[:, 128 * hc:128 * (hc + 1)], transpose=True),
               after=(g4,), dmainc=True)
        flush()

    ystores = []
    for r in range(NT4):
        t0 = PT4 * r
        bdep5 = ((global_state["xq2_last"], global_state["dq2_last"],
                  global_state["y1_last"]) if r == 0 else ())
        x2l = OP("sync", lambda e, t0=t0: e.dma_start(
            out=whT[:, 0:32 * PT4].rearrange("p (c t) -> p c t", c=32),
            in_=xq2_scr.rearrange("p (c t) -> p c t", c=32)[:, :, t0:t0 + PT4]),
            after=bdep5, dmainc=True)
        y1l = []
        for g in range(NG4):
            y1l.append(OP("sync", lambda e, t0=t0, g=g: e.dma_start(
                out=arf[:, 1024 * g:1024 * (g + 1)],
                in_=y1_scr[t0 + 128 * g:t0 + 128 * (g + 1), :]), after=bdep5, dmainc=True))
        dql = OP("sync", lambda e, t0=t0: e.dma_start(
            out=dq2_tm[:, :],
            in_=dq2_scr[t0:t0 + PT4, :].rearrange("(g p) a -> p (g a)", p=128), allow_slow_non_contiguous=True),
            after=bdep5, dmainc=True)
        prev_f = {0: None, 1: None}
        for g in range(NG4):
            mm_done = []
            for eh, ps in ((0, p_a), (1, p_b)):
                mmt = None
                for hc in range(32):
                    fdeps = ()
                    if hc == 0:
                        fdeps = (x2l,)
                        if prev_f[eh] is not None:
                            fdeps = fdeps + (prev_f[eh],)
                    mmt = OP("tensor", lambda e, hc=hc, g=g, eh=eh, ps=ps: e.matmul(
                        ps[:, 0:512],
                        whT[:, PT4 * hc + 128 * g:PT4 * hc + 128 * (g + 1)],
                        wbig[:, 1024 * hc + 512 * eh:1024 * hc + 512 * (eh + 1)],
                        start=(hc == 0), stop=(hc == 31)), after=fdeps)
                mm_done.append(mmt)
            f1 = OP("vector", lambda e, g=g: e.tensor_scalar(
                out=gen[:, 0:512], in0=p_a[:, 0:512], scalar1=dq2_tm[:, g:g + 1],
                scalar2=None, op0=ALU.mult), after=(mm_done[0], dql))
            f2 = OP("vector", lambda e, g=g: e.tensor_scalar(
                out=gen[:, 512:1024], in0=p_b[:, 0:512], scalar1=dq2_tm[:, g:g + 1],
                scalar2=None, op0=ALU.mult), after=(mm_done[1], dql))
            prev_f[0], prev_f[1] = f1, f2
            ln4 = ln_plain(gen[:, 0:1024], gen[:, 1024:2048], gen[:, 2048:3072],
                           gen[:, 3072:4096], 0,
                           deps=(f1, f2) + ((ystores[-1],) if ystores else ()))
            y2v = OP("vector", lambda e, g=g: e.tensor_tensor(
                out=gen[:, 2048:3072], in0=arf[:, 1024 * g:1024 * (g + 1)],
                in1=gen[:, 1024:2048], op=ALU.add), after=(ln4, y1l[g]))
            ystores.append(OP("gpsimd", lambda e, t0=t0, g=g: e.dma_start(
                out=y_out[t0 + 128 * g:t0 + 128 * (g + 1), :], in_=gen[:, 2048:3072]),
                after=(y2v,), dmainc=True))
        flush()
    OP("vector", lambda e: e.memset(scs[:, 0:1], 0.0), after=tuple(ystores))
    sch.emit()


def kernel(L0_attn_x, L0_attn_W_in, L0_attn_b_in, L0_attn_W_h, L0_attn_b_h,
           L0_attn_W_out, L0_attn_b_out, L0_attn_ln_w, L0_attn_ln_b,
           L0_ffn_w1, L0_ffn_conv_w, L0_ffn_conv_b, L0_ffn_w2):
    L0_attn_x = np.asarray(L0_attn_x)
    B, S, D_ = L0_attn_x.shape
    nc = build_program(S=S)
    base = {
        "w_in_d": np.ascontiguousarray(L0_attn_W_in, dtype=np.float32),
        "b_in_d": np.ascontiguousarray(L0_attn_b_in, dtype=np.float32),
        "w_h_d": np.ascontiguousarray(L0_attn_W_h, dtype=np.float32),
        "b_h_d": np.ascontiguousarray(L0_attn_b_h, dtype=np.float32),
        "w_out_d": np.ascontiguousarray(L0_attn_W_out, dtype=np.float32),
        "b_out_d": np.ascontiguousarray(L0_attn_b_out, dtype=np.float32),
        "ln_w_d": np.ascontiguousarray(L0_attn_ln_w, dtype=np.float32),
        "ln_b_d": np.ascontiguousarray(L0_attn_ln_b, dtype=np.float32),
        "w1_d": np.ascontiguousarray(L0_ffn_w1, dtype=np.float32),
        "cw_d": np.ascontiguousarray(np.asarray(L0_ffn_conv_w).reshape(DH, 3),
                                     dtype=np.float32),
        "cb_d": np.ascontiguousarray(L0_ffn_conv_b, dtype=np.float32),
        "w2_d": np.ascontiguousarray(L0_ffn_w2, dtype=np.float32),
    }
    in_maps = []
    for b in range(B):
        m = dict(base)
        m["x_in"] = np.ascontiguousarray(L0_attn_x[b], dtype=np.float32)
        in_maps.append(m)
    res = run_bass_kernel_spmd(nc, in_maps, core_ids=list(range(B)))
    return np.stack([res.results[b]["y_out"] for b in range(B)],
                    axis=0).astype(np.float32)


# version 05
# speedup vs baseline: 1.0745x; 1.0732x over previous
import numpy as np
import concourse.bass as bass
import concourse.mybir as mybir
from concourse.bass_utils import run_bass_kernel_spmd

F32 = mybir.dt.float32
BF16 = mybir.dt.bfloat16
I32 = mybir.dt.int32
AF = mybir.ActivationFunctionType
ALU = mybir.AluOpType
AX = mybir.AxisListType

D = 1024
DH = 4096

ENGS = ("sync", "tensor", "vector", "scalar", "gpsimd")


def dm0(m):
    return 256 * (m % 4) + 128 * (m // 4)


class Tok:
    __slots__ = ("eng", "val")

    def __init__(self, eng, val):
        self.eng = eng
        self.val = val


class Sched:

    SELF_DIST = 4

    def __init__(self, nc):
        self.nc = nc
        self.semnames = list(ENGS) + [e + "_dma" for e in ("gpsimd", "scalar", "sync")]
        self.sem = {n: nc.semaphore("sem_" + n).__enter__() for n in self.semnames}
        self.cnt = {n: 0 for n in self.semnames}
        self.events = []
        self.pos = {e: 0 for e in ENGS}
        self.tok_pos = {}
        self.fence_w = []

    def fence(self):
        self.fence_w = [(n, self.cnt[n]) for n in self.semnames if self.cnt[n] > 0]

    def op(self, eng, fn, after=(), dmainc=False):
        sn = eng + "_dma" if (dmainc and eng in ("gpsimd", "scalar", "sync")) else eng
        waits = list(self.fence_w)
        for t in after:
            if t is None:
                continue
            v = self.cnt[t.eng] if t.eng.endswith("_dma") else t.val
            waits.append((t.eng, v))
        n = 16 if dmainc else 1
        self.cnt[sn] += n
        tok = Tok(sn, self.cnt[sn])
        self.pos[eng] += 1
        self.tok_pos[(sn, tok.val)] = (eng, self.pos[eng], dmainc)
        self.events.append((eng, sn, fn, waits, n, self.pos[eng]))
        return tok

    def emit(self):
        ev_by_eng = {e: [] for e in ENGS}
        for eng, sn, fn, waits, n, pos in self.events:
            ev_by_eng[eng].append((sn, fn, waits, n, pos))
        nc = self.nc
        sem = self.sem
        tok_pos = self.tok_pos
        SELF = self.SELF_DIST

        def body(engine, eng_name):
            done = {}
            for sn, fn, waits, n, pos in ev_by_eng[eng_name]:
                for wsn, wval in waits:
                    info = tok_pos.get((wsn, wval))
                    if info is not None:
                        weng, dpos, wdma = info
                        if (weng == eng_name and not wdma
                                and pos - dpos > SELF):
                            continue
                    if done.get(wsn, 0) >= wval:
                        continue
                    engine.wait_ge(sem[wsn], wval)
                    done[wsn] = wval
                ins = fn(engine)
                ins.then_inc(sem[sn], n)

        with nc.Block() as blk:
            for e in ENGS:
                if ev_by_eng[e]:
                    getattr(blk, e)(lambda eng, e=e: body(eng, e))
        self.events = []
        self.pos = {e: 0 for e in ENGS}
        self.tok_pos = {}


def build_program(S=4096):
    PT1 = min(512, S)
    PSC = min(256, S)
    PT4 = min(256, S)
    NT1 = S // PT1
    NSC = S // PSC
    NT4 = S // PT4
    NG1 = PT1 // 128
    NG4 = max(1, PT4 // 128)

    nc = bass.Bass("TRN2", target_bir_lowering=False)

    x_in = nc.dram_tensor("x_in", [S, D], F32, kind="ExternalInput")
    w_in_d = nc.dram_tensor("w_in_d", [2 * D, D], F32, kind="ExternalInput")
    b_in_d = nc.dram_tensor("b_in_d", [2 * D], F32, kind="ExternalInput")
    w_h_d = nc.dram_tensor("w_h_d", [D, D], F32, kind="ExternalInput")
    b_h_d = nc.dram_tensor("b_h_d", [D], F32, kind="ExternalInput")
    w_out_d = nc.dram_tensor("w_out_d", [D, D], F32, kind="ExternalInput")
    b_out_d = nc.dram_tensor("b_out_d", [D], F32, kind="ExternalInput")
    ln_w_d = nc.dram_tensor("ln_w_d", [D], F32, kind="ExternalInput")
    ln_b_d = nc.dram_tensor("ln_b_d", [D], F32, kind="ExternalInput")
    w1_d = nc.dram_tensor("w1_d", [DH, D], F32, kind="ExternalInput")
    cw_d = nc.dram_tensor("cw_d", [DH, 3], F32, kind="ExternalInput")
    cb_d = nc.dram_tensor("cb_d", [DH], F32, kind="ExternalInput")
    w2_d = nc.dram_tensor("w2_d", [D, DH], F32, kind="ExternalInput")
    y_out = nc.dram_tensor("y_out", [S, D], F32, kind="ExternalOutput")

    import os as _os2
    _dbg = _os2.environ.get("KDBG") == "1"
    _kw = {"kind": "ExternalOutput"} if _dbg else {}
    z_scr = nc.dram_tensor("z_scr", [128, 8 * S], F32, **_kw)
    a_scr = nc.dram_tensor("a_scr", [128, 8 * S], F32, **_kw)
    hs_scr = nc.dram_tensor("hs_scr", [128, 8 * S], F32, **_kw)
    y1_scr = nc.dram_tensor("y1_scr", [S, D], F32, **_kw)
    xq_scr = nc.dram_tensor("xq_scr", [128, 8 * (S + 2)], BF16, **_kw)
    xq2_scr = nc.dram_tensor("xq2_scr", [128, 32 * S], BF16, **_kw)
    m1_scr = nc.dram_tensor("m1_scr", [S + 2, 1], F32, **_kw)
    dq2_scr = nc.dram_tensor("dq2_scr", [S, 1], F32, **_kw)

    A = nc.alloc_sbuf_tensor
    arf = A("arf", [128, 8192], F32)
    gen = A("gen", [128, 8192], F32)
    bfa = A("bfa", [128, 8192], BF16)
    bfb = A("bfb", [128, 8192], BF16)
    whT = A("whT", [128, 16384], BF16)
    wbig = A("wbig", [128, 32768], BF16)
    it32 = A("it32", [128, 512], I32)
    lnw_bc = A("lnw_bc", [128, 1024], F32)
    lnb_bc = A("lnb_bc", [128, 1024], F32)
    bu_fm = A("bu_fm", [128, 8], F32)
    zb_fm = A("zb_fm", [128, 8], F32)
    bh_fm = A("bh_fm", [128, 8], F32)
    cw_fm = A("cw_fm", [128, 96], F32)
    cb_fm = A("cb_fm", [128, 32], F32)
    fmat = A("fmat", [128, 4], F32)
    idn = A("idn", [128, 128], F32)
    idnb = A("idnb", [128, 128], BF16)
    ones_col = A("ones_col", [128, 1], F32)
    ones_rf = A("ones_rf", [1, 128], F32)
    ones_rb = A("ones_rb", [1, 128], BF16)
    c1_row = A("c1_row", [1, 128], F32)
    c2_col = A("c2_col", [128, 1], F32)
    thr1 = A("thr1", [128, 1], F32)
    thr2 = A("thr2", [128, 1], F32)
    scs = A("scs", [128, 16], F32)
    vrow = A("vrow", [1, 16], F32)
    zrow = A("zrow", [128, 16], BF16)
    pair = A("pair", [128, 16], BF16)
    dHf = A("dHf", [128, 8], F32)
    dsub = A("dsub", [128, 8], F32)
    dsub2 = A("dsub2", [128, 8], F32)
    dfull = A("dfull", [128, 8], F32)
    qsc = A("qsc", [128, 8], F32)
    cand = A("cand", [128, 8], F32)
    hcur = A("hcur", [128, 8], F32)
    mcol = A("mcol", [128, 8], F32)
    tstage = A("tstage", [128, 1024], BF16)
    bo_bf = wbig[:, 16384:18432]
    Ssc = arf[:, 6144:6400]
    mxt = gen[:, 4096:4096 + PT4]
    dq1b = gen[:, 4608:4608 + PT4 + 2]
    s2b = gen[:, 5632:5632 + PT4]
    mx_tm = gen[:, 6656:6656 + NG4]
    dq2_tm = gen[:, 6720:6720 + NG4]
    mrow = gen[:, 7168:7168 + PT4 + 2]

    p_y = nc.alloc_psum_tensor("p_y", [128, 512], F32)
    p_t0 = nc.alloc_psum_tensor("p_t0", [128, 512], F32)
    p_t1 = nc.alloc_psum_tensor("p_t1", [128, 512], F32)
    p_a = nc.alloc_psum_tensor("p_a", [128, 512], F32)
    p_b = nc.alloc_psum_tensor("p_b", [128, 512], F32)
    p_c = nc.alloc_psum_tensor("p_c", [128, 512], F32)
    p_t = [p_t0, p_t1]

    sch = Sched(nc)
    OP = sch.op
    global_state = {}
    BAR = nc.all_engine_barrier

    def flush():
        sch.emit()
        sch.fence()
        BAR()

    def scr_view(scr, t0, nsteps):
        return scr.rearrange("p (c t) -> p c t", c=8)[:, :, t0:t0 + nsteps]

    ldc = []
    ldc.append(OP("sync", lambda e: e.dma_start(out=lnw_bc[0:1, :], in_=ln_w_d[None, :]), dmainc=True))
    ldc.append(OP("sync", lambda e: e.dma_start(out=lnb_bc[0:1, :], in_=ln_b_d[None, :]), dmainc=True))
    ldc.append(OP("sync", lambda e: e.dma_start(
        out=bu_fm[:, :], in_=b_in_d.rearrange("(c p) -> p c", p=128)[:, 0:8], allow_slow_non_contiguous=True), dmainc=True))
    ldc.append(OP("sync", lambda e: e.dma_start(
        out=zb_fm[:, :], in_=b_in_d.rearrange("(c p) -> p c", p=128)[:, 8:16], allow_slow_non_contiguous=True), dmainc=True))
    ldc.append(OP("sync", lambda e: e.dma_start(
        out=bh_fm[:, :], in_=b_h_d.rearrange("(c p) -> p c", p=128), allow_slow_non_contiguous=True), dmainc=True))
    ldc.append(OP("sync", lambda e: e.dma_start(
        out=cw_fm.rearrange("p (c k) -> p c k", c=32), in_=cw_d.rearrange("(c p) k -> p c k", p=128), allow_slow_non_contiguous=True), dmainc=True))
    ldc.append(OP("sync", lambda e: e.dma_start(
        out=cb_fm[:, :], in_=cb_d.rearrange("(c p) -> p c", p=128), allow_slow_non_contiguous=True), dmainc=True))
    ldwh = []
    for i in range(8):
        ldwh.append(OP("sync", lambda e, i=i: e.dma_start(
            out=arf[:, 1024 * i:1024 * (i + 1)],
            in_=w_h_d[128 * i:128 * (i + 1), :]), dmainc=True))
    t_iota = OP("gpsimd", lambda e: e.iota(idn[:, :], [[1, 128]], channel_multiplier=-1,
                                           allow_small_or_imprecise_dtypes=True))
    for fn in (lambda e: e.memset(zrow[:, :], 0.0),
               lambda e: e.memset(ones_col[:, :], 1.0),
               lambda e: e.memset(ones_rf[:, :], 1.0),
               lambda e: e.memset(ones_rb[:, :], 1.0),
               lambda e: e.memset(hcur[:, :], 0.0),
               lambda e: e.memset(pair[:, :], 0.0),
               lambda e: e.memset(scs[:, :], 0.0),
               lambda e: e.memset(p_y[:, :], 0.0),
               lambda e: e.memset(p_t0[:, :], 0.0),
               lambda e: e.memset(p_t1[:, :], 0.0),
               lambda e: e.memset(fmat[:, :], 0.0)):
        fz = OP("vector", fn)
    for j in range(4):
        fz = OP("vector", lambda e, j=j: e.memset(fmat[32 * j:32 * j + 2, j:j + 1], 1.0),
                after=(fz,))
    flush()

    t1 = OP("vector", lambda e: e.tensor_scalar(out=idn[:, :], in0=idn[:, :],
                                                scalar1=0.0, scalar2=None, op0=ALU.is_equal))
    OP("vector", lambda e: e.tensor_copy(out=idnb[:, :], in_=idn[:, :]), after=(t1,))
    OP("vector", lambda e: e.tensor_tensor(out=zb_fm[:, :], in0=zb_fm[:, :],
                                           in1=bh_fm[:, :], op=ALU.add))
    m1 = OP("tensor", lambda e: e.matmul(p_c[:, 0:512], ones_rf[0:1, :], lnw_bc[0:1, 0:512],
                                         start=True, stop=True))
    m2 = OP("tensor", lambda e: e.matmul(p_b[:, 0:512], ones_rf[0:1, :], lnw_bc[0:1, 512:1024],
                                         start=True, stop=True))
    m3 = OP("tensor", lambda e: e.matmul(p_a[:, 0:512], ones_rf[0:1, :], lnb_bc[0:1, 0:512],
                                         start=True, stop=True))
    m4 = OP("tensor", lambda e: e.matmul(p_t0[:, 0:512], ones_rf[0:1, :], lnb_bc[0:1, 512:1024],
                                         start=True, stop=True))
    OP("scalar", lambda e: e.copy(out=lnw_bc[:, 0:512], in_=p_c[:, 0:512]), after=(m1,))
    OP("scalar", lambda e: e.copy(out=lnw_bc[:, 512:1024], in_=p_b[:, 0:512]), after=(m2,))
    OP("scalar", lambda e: e.copy(out=lnb_bc[:, 0:512], in_=p_a[:, 0:512]), after=(m3,))
    OP("scalar", lambda e: e.copy(out=lnb_bc[:, 512:1024], in_=p_t0[:, 0:512]), after=(m4,))
    flush()

    OP("vector", lambda e: e.memset(p_t0[:, :], 0.0))
    OP("vector", lambda e: e.memset(p_a[:, :], 0.0))
    OP("vector", lambda e: e.memset(p_b[:, :], 0.0))
    OP("vector", lambda e: e.memset(p_c[:, :], 0.0))
    OP("vector", lambda e: e.memset(mrow[0:1, :], 0.0))
    for c in range(8):
        OP("gpsimd", lambda e, c=c: e.dma_start(
            out=xq_scr[:, c * (S + 2):c * (S + 2) + 1], in_=zrow[:, 0:1], allow_slow_non_contiguous=True), dmainc=True)
        OP("gpsimd", lambda e, c=c: e.dma_start(
            out=xq_scr[:, c * (S + 2) + S + 1:c * (S + 2) + S + 2], in_=zrow[:, 0:1], allow_slow_non_contiguous=True), dmainc=True)
    OP("gpsimd", lambda e: e.dma_start(out=m1_scr[0:1, :], in_=scs[0:1, 0:1]), dmainc=True)
    OP("gpsimd", lambda e: e.dma_start(out=m1_scr[S + 1:S + 2, :], in_=scs[0:1, 0:1]), dmainc=True)
    flush()

    def split_hl(src_view, n, tmp0, deps=()):
        h = OP("vector", lambda e: e.tensor_copy(out=bfa[:, 0:n], in_=src_view), after=deps)
        u = OP("vector", lambda e: e.tensor_copy(out=gen[:, tmp0:tmp0 + n], in_=bfa[:, 0:n]),
               after=(h,))
        l0 = OP("vector", lambda e: e.tensor_tensor(out=gen[:, tmp0:tmp0 + n], in0=src_view,
                                                    in1=gen[:, tmp0:tmp0 + n], op=ALU.subtract),
                after=(u,))
        l = OP("vector", lambda e: e.tensor_copy(out=bfa[:, 4096:4096 + n],
                                                 in_=gen[:, tmp0:tmp0 + n]), after=(l0,))
        return h, l

    for rnd in range(2):
        hH, hL = split_hl(arf[:, 4096 * rnd:4096 * (rnd + 1)], 4096, 4096,
                          deps=(ldwh[-1],))
        for iloc in range(4):
            i = 4 * rnd + iloc
            j, u = i // 2, i % 2
            for m in range(8):
                for half, base in ((0, 0), (1, 4096)):
                    dep = hH if half == 0 else hL
                    OP("scalar", lambda e, iloc=iloc, j=j, u=u, m=m, half=half, base=base:
                       e.dma_start(
                           out=whT[:, 8192 * half + 256 * (4 * m + j) + 128 * u:
                                   8192 * half + 256 * (4 * m + j) + 128 * (u + 1)],
                           in_=bfa[:, base + 1024 * iloc + dm0(m):
                                   base + 1024 * iloc + dm0(m) + 128],
                           transpose=True), after=(dep, hL), dmainc=True)
        flush()

    for rnd in range(4):
        lds = []
        for iloc in range(4):
            i = 4 * rnd + iloc
            lds.append(OP("sync", lambda e, iloc=iloc, i=i: e.dma_start(
                out=gen[:, 1024 * iloc:1024 * (iloc + 1)],
                in_=w_in_d[128 * i:128 * (i + 1), :]), dmainc=True))
        hH, hL = split_hl(gen[:, 0:4096], 4096, 4096, deps=tuple(lds))
        for iloc in range(4):
            i = 4 * rnd + iloc
            for dc in range(8):
                for half, base, wb in ((0, 0, 0), (1, 4096, 16384)):
                    OP("scalar", lambda e, i=i, iloc=iloc, dc=dc, base=base, wb=wb:
                       e.dma_start(
                           out=wbig[:, wb + 2048 * dc + 128 * i:wb + 2048 * dc + 128 * (i + 1)],
                           in_=bfa[:, base + 1024 * iloc + 128 * dc:
                                   base + 1024 * iloc + 128 * (dc + 1)],
                           transpose=True), after=(hH, hL), dmainc=True)
        flush()

    for scol, wd, ntile in ((0, w1_d, 32), (8, w2_d, 8)):
        colw = wd.shape[1] if hasattr(wd, "shape") else None
        colw = 1024 if wd is w1_d else 4096
        per = 4096 // colw
        nrnd = ntile // per
        for rnd in range(nrnd):
            lds = []
            for k in range(per):
                i = per * rnd + k
                lds.append(OP("sync", lambda e, k=k, i=i, colw=colw, wd=wd: e.dma_start(
                    out=gen[:, colw * k:colw * (k + 1)],
                    in_=wd[128 * i:128 * (i + 1), :]), dmainc=True))
            OP("vector", lambda e, scol=scol, rnd=rnd: e.reduce_sum(
                out=scs[:, scol + rnd:scol + rnd + 1], in_=gen[:, 0:4096], axis=AX.X,
                apply_absolute_value=True), after=tuple(lds))
            flush()
    mm = OP("tensor", lambda e: e.matmul(p_c[0:1, 0:16], ones_col[:, 0:1], scs[:, 0:16],
                                         start=True, stop=True))
    v0 = OP("vector", lambda e: e.tensor_copy(out=vrow[0:1, 0:16], in_=p_c[0:1, 0:16]),
            after=(mm,))
    v1 = OP("vector", lambda e: e.reduce_sum(out=vrow[0:1, 0:1], in_=vrow[0:1, 0:8], axis=AX.X),
            after=(v0,))
    v2 = OP("vector", lambda e: e.reduce_sum(out=vrow[0:1, 1:2], in_=vrow[0:1, 8:16], axis=AX.X),
            after=(v0, v1))
    v3 = OP("vector", lambda e: e.tensor_scalar(out=vrow[0:1, 2:3], in0=vrow[0:1, 0:1],
                                                scalar1=1.0 / (DH * D), scalar2=1e-5,
                                                op0=ALU.mult, op1=ALU.max), after=(v1,))
    v4 = OP("vector", lambda e: e.tensor_scalar(out=vrow[0:1, 3:4], in0=vrow[0:1, 1:2],
                                                scalar1=1.0 / (DH * D), scalar2=1e-5,
                                                op0=ALU.mult, op1=ALU.max), after=(v2,))
    v5 = OP("vector", lambda e: e.tensor_scalar(out=c1_row[0:1, :], in0=ones_rf[0:1, :],
                                                scalar1=vrow[0:1, 2:3], scalar2=1.0 / 127.0,
                                                op0=ALU.mult, op1=ALU.mult), after=(v3,))
    v6 = OP("vector", lambda e: e.tensor_scalar(out=vrow[0:1, 4:5], in0=vrow[0:1, 2:3],
                                                scalar1=0.5, scalar2=None, op0=ALU.mult),
            after=(v3,))
    v7 = OP("vector", lambda e: e.tensor_scalar(out=vrow[0:1, 5:6], in0=vrow[0:1, 3:4],
                                                scalar1=0.5, scalar2=None, op0=ALU.mult),
            after=(v4,))
    v8 = OP("vector", lambda e: e.tensor_scalar(out=vrow[0:1, 6:7], in0=vrow[0:1, 3:4],
                                                scalar1=1.0 / 127.0, scalar2=None, op0=ALU.mult),
            after=(v4,))
    b1 = OP("tensor", lambda e: e.matmul(p_c[:, 32:33], ones_rf[0:1, :], vrow[0:1, 4:5],
                                         start=True, stop=True), after=(v6,))
    t_thr1 = OP("vector", lambda e: e.tensor_copy(out=thr1[:, :], in_=p_c[:, 32:33]), after=(b1,))
    b2 = OP("tensor", lambda e: e.matmul(p_c[:, 36:37], ones_rf[0:1, :], vrow[0:1, 5:6],
                                         start=True, stop=True), after=(v7, t_thr1))
    t_thr2 = OP("vector", lambda e: e.tensor_copy(out=thr2[:, :], in_=p_c[:, 36:37]), after=(b2,))
    b3 = OP("tensor", lambda e: e.matmul(p_c[:, 40:41], ones_rf[0:1, :], vrow[0:1, 6:7],
                                         start=True, stop=True), after=(v8, t_thr2))
    t_c2 = OP("vector", lambda e: e.tensor_copy(out=c2_col[:, :], in_=p_c[:, 40:41]), after=(b3,))
    if _dbg:
        dbg_scr = nc.dram_tensor("dbg_scr", [128, 64], F32, kind="ExternalOutput")
        OP("gpsimd", lambda e: e.dma_start(out=dbg_scr[:, 0:16], in_=scs[:, :], allow_slow_non_contiguous=True),
           after=(t_thr1,), dmainc=True)
        OP("gpsimd", lambda e: e.dma_start(out=dbg_scr[0:1, 16:32], in_=vrow[0:1, :], allow_slow_non_contiguous=True),
           after=(t_thr1, v5, v8), dmainc=True)
        OP("gpsimd", lambda e: e.dma_start(out=dbg_scr[:, 32:33], in_=thr1[:, :], allow_slow_non_contiguous=True),
           after=(t_thr1,), dmainc=True)
        OP("gpsimd", lambda e: e.dma_start(out=dbg_scr[:, 33:34], in_=thr2[:, :], allow_slow_non_contiguous=True),
           after=(t_thr2,), dmainc=True)
        OP("gpsimd", lambda e: e.dma_start(out=dbg_scr[:, 34:35], in_=c2_col[:, :], allow_slow_non_contiguous=True),
           after=(t_c2,), dmainc=True)
    flush()

    ln_state = {"prev": None}

    def ln_plain(src, dst, t_tmp, sq_tmp, ms, deps=()):
        if ln_state["prev"] is not None:
            deps = tuple(deps) + (ln_state["prev"],)
        a1 = OP("vector", lambda e: e.reduce_sum(out=mcol[:, ms:ms + 1], in_=src, axis=AX.X),
                after=deps)
        a2 = OP("vector", lambda e: e.tensor_scalar(out=mcol[:, ms:ms + 1], in0=mcol[:, ms:ms + 1],
                                                    scalar1=1.0 / D, scalar2=None, op0=ALU.mult),
                after=(a1,))
        a3 = OP("vector", lambda e: e.tensor_scalar(out=t_tmp, in0=src,
                                                    scalar1=mcol[:, ms:ms + 1], scalar2=None,
                                                    op0=ALU.subtract), after=(a2,))
        a4 = OP("vector", lambda e: e.tensor_tensor(out=sq_tmp, in0=t_tmp, in1=t_tmp,
                                                    op=ALU.mult), after=(a3,))
        a5 = OP("vector", lambda e: e.reduce_sum(out=mcol[:, ms + 1:ms + 2], in_=sq_tmp,
                                                 axis=AX.X), after=(a4,))
        a6p = OP("vector", lambda e: e.tensor_scalar(out=mcol[:, ms + 1:ms + 2],
                                                     in0=mcol[:, ms + 1:ms + 2],
                                                     scalar1=1.0 / D, scalar2=1e-5,
                                                     op0=ALU.mult, op1=ALU.add), after=(a5,))
        a6a = OP("scalar", lambda e: e.activation(out=mcol[:, ms + 1:ms + 2],
                                                  in_=mcol[:, ms + 1:ms + 2], func=AF.Sqrt),
                 after=(a6p,))
        a6 = OP("vector", lambda e: e.reciprocal(out=mcol[:, ms + 1:ms + 2],
                                                 in_=mcol[:, ms + 1:ms + 2]), after=(a6a,))
        a7 = OP("vector", lambda e: e.tensor_scalar(out=dst, in0=t_tmp,
                                                    scalar1=mcol[:, ms + 1:ms + 2], scalar2=None,
                                                    op0=ALU.mult), after=(a6, a3))
        ln_state["prev"] = a7
        return a7

    for r in range(NT1):
        ldx = []
        for g in range(NG1):
            ldx.append(OP("sync", lambda e, g=g, r=r: e.dma_start(
                out=arf[:, 1024 * g:1024 * (g + 1)],
                in_=x_in[PT1 * r + 128 * g:PT1 * r + 128 * (g + 1), :]), dmainc=True))
        trT = []
        for g in range(NG1):
            x0v = arf[:, 1024 * g:1024 * (g + 1)]
            n1tok = ln_plain(x0v, gen[:, 1024:2048], gen[:, 0:1024], gen[:, 4096:5120], 0,
                             deps=(ldx[g],))
            xntok = ln_plain(gen[:, 1024:2048], gen[:, 2048:3072], gen[:, 3072:4096],
                             gen[:, 4096:5120], 2, deps=(n1tok,))
            aff1 = OP("vector", lambda e: e.tensor_tensor(out=gen[:, 2048:3072],
                                                          in0=gen[:, 2048:3072],
                                                          in1=lnw_bc[:, :], op=ALU.mult),
                      after=(xntok,))
            aff2 = OP("vector", lambda e: e.tensor_tensor(out=gen[:, 2048:3072],
                                                          in0=gen[:, 2048:3072],
                                                          in1=lnb_bc[:, :], op=ALU.add),
                      after=(aff1,))
            h1 = OP("vector", lambda e, g=g: e.tensor_copy(
                out=bfa[:, 1024 * g:1024 * (g + 1)], in_=gen[:, 2048:3072]), after=(aff2,))
            h2 = OP("vector", lambda e, g=g: e.tensor_copy(out=gen[:, 4096:5120],
                                                           in_=bfa[:, 1024 * g:1024 * (g + 1)]),
                    after=(h1,))
            h3 = OP("vector", lambda e: e.tensor_tensor(out=gen[:, 5120:6144],
                                                        in0=gen[:, 2048:3072],
                                                        in1=gen[:, 4096:5120], op=ALU.subtract),
                    after=(h2,))
            h4 = OP("vector", lambda e, g=g: e.tensor_copy(
                out=bfa[:, 4096 + 1024 * g:4096 + 1024 * (g + 1)], in_=gen[:, 5120:6144]),
                after=(h3,))
            for dc in range(8):
                for half, base, ob, dep in ((0, 0, 0, h1), (1, 4096, 4096, h4)):
                    trT.append(OP("scalar", lambda e, g=g, dc=dc, base=base, ob=ob:
                                  e.dma_start(
                                      out=bfb[:, ob + PT1 * dc + 128 * g:
                                              ob + PT1 * dc + 128 * (g + 1)],
                                      in_=bfa[:, base + 1024 * g + 128 * dc:
                                              base + 1024 * g + 128 * (dc + 1)],
                                      transpose=True), after=(dep, h4), dmainc=True))
        prev_sp = {0: None, 1: None}
        global_state.setdefault("za_last", None)
        for ec in range(16):
            ps = p_a if ec % 2 == 0 else p_b
            deps = tuple(trT) + ((prev_sp[ec % 2],) if prev_sp[ec % 2] else ())
            mmt = None
            for dc in range(8):
                for wofs, xofs in ((0, 0), (0, 4096), (16384, 0)):
                    first = (dc == 0 and wofs == 0 and xofs == 0)
                    last = (dc == 7 and wofs == 16384)
                    mmt = OP("tensor", lambda e, ec=ec, dc=dc, wofs=wofs, xofs=xofs,
                             first=first, last=last, ps=ps:
                             e.matmul(ps[:, 0:PT1],
                                      wbig[:, wofs + 2048 * dc + 128 * ec:
                                           wofs + 2048 * dc + 128 * (ec + 1)],
                                      bfb[:, xofs + PT1 * dc:xofs + PT1 * (dc + 1)],
                                      start=first, stop=last),
                             after=deps if first else ())
            if ec < 8:
                ev = OP("scalar", lambda e, ec=ec, ps=ps: e.activation(
                    out=gen[:, 6144 + 1024 * (ec % 2):6144 + 1024 * (ec % 2) + PT1],
                    in_=ps[:, 0:PT1], func=AF.Sigmoid, bias=bu_fm[:, ec:ec + 1], scale=1.0),
                    after=(mmt,))
            else:
                ev = OP("vector", lambda e, ec=ec, ps=ps: e.tensor_scalar(
                    out=gen[:, 6144 + 1024 * (ec % 2):6144 + 1024 * (ec % 2) + PT1],
                    in0=ps[:, 0:PT1], scalar1=zb_fm[:, ec - 8:ec - 7], scalar2=None,
                    op0=ALU.add), after=(mmt,))
            scr = a_scr if ec < 8 else z_scr
            c = (ec % 8 % 2) * 4 + (ec % 8) // 2
            prev_sp[ec % 2] = OP(
                "gpsimd", lambda e, scr=scr, c=c, r=r, ec=ec: e.dma_start(
                    out=scr[:, S * c + PT1 * r:S * c + PT1 * (r + 1)],
                    in_=gen[:, 6144 + 1024 * (ec % 2):6144 + 1024 * (ec % 2) + PT1]),
                after=(ev,), dmainc=True)
            global_state["za_last"] = prev_sp[ec % 2]
        flush()

    import os as _os
    if _os.environ.get("KSTOP") == "p1":
        return nc

    for rnd in range(2):
        lds = []
        for iloc in range(4):
            i = 4 * rnd + iloc
            lds.append(OP("sync", lambda e, iloc=iloc, i=i: e.dma_start(
                out=gen[:, 1024 * iloc:1024 * (iloc + 1)],
                in_=w_out_d[128 * i:128 * (i + 1), :]), dmainc=True))
        hH, hL = split_hl(gen[:, 0:4096], 4096, 4096, deps=tuple(lds))
        for iloc in range(4):
            i = 4 * rnd + iloc
            for dc in range(8):
                for half, base, wb in ((0, 0, 0), (1, 4096, 8192)):
                    OP("scalar", lambda e, i=i, iloc=iloc, dc=dc, base=base, wb=wb:
                       e.dma_start(
                           out=wbig[:, wb + 1024 * dc + 128 * i:wb + 1024 * dc + 128 * (i + 1)],
                           in_=bfa[:, base + 1024 * iloc + 128 * dc:
                                   base + 1024 * iloc + 128 * (dc + 1)],
                           transpose=True), after=(hH, hL), dmainc=True)
        flush()

    bl = OP("sync", lambda e: e.dma_start(out=gen[0:1, 0:1024], in_=b_out_d[None, :]),
            dmainc=True)
    u1 = OP("vector", lambda e: e.tensor_copy(out=bo_bf[0:1, 0:1024], in_=gen[0:1, 0:1024]),
            after=(bl,))
    u2 = OP("vector", lambda e: e.tensor_copy(out=gen[0:1, 1024:2048], in_=bo_bf[0:1, 0:1024]),
            after=(u1,))
    u3 = OP("vector", lambda e: e.tensor_tensor(out=gen[0:1, 2048:3072], in0=gen[0:1, 0:1024],
                                                in1=gen[0:1, 1024:2048], op=ALU.subtract),
            after=(u2,))
    OP("vector", lambda e: e.tensor_copy(out=bo_bf[0:1, 1024:2048], in_=gen[0:1, 2048:3072]),
       after=(u3,))
    flush()

    whH = lambda m, j: whT[:, 256 * (4 * m + j):256 * (4 * m + j) + 256]
    whL = lambda m, j: whT[:, 8192 + 256 * (4 * m + j):8192 + 256 * (4 * m + j) + 256]
    zr = arf[:, 0:8 * PSC]
    ar = arf[:, 2048:2048 + 8 * PSC]
    hr = arf[:, 4096:4096 + 8 * PSC]
    zh = bfa[:, 0:8 * PSC]
    zl2 = bfa[:, 2048:2048 + 8 * PSC]

    def slot(ring, tl):
        return ring[:, tl:tl + 1 + 7 * PSC:PSC]

    prev_split = None
    prev_bulk = None
    prev_zmm = None
    prev_hs_store = None
    prev_hw = None
    for r in range(NSC):
        dep_r0 = (global_state["za_last"],) if r == 0 else ()
        zl = OP("sync", lambda e, r=r: e.dma_start(
            out=zr.rearrange("p (c t) -> p c t", c=8), in_=scr_view(z_scr, PSC * r, PSC)),
            after=((prev_bulk,) if prev_bulk else ()) + dep_r0, dmainc=True)
        al = OP("sync", lambda e, r=r: e.dma_start(
            out=ar.rearrange("p (c t) -> p c t", c=8), in_=scr_view(a_scr, PSC * r, PSC)),
            after=((prev_split,) if prev_split else ()) + dep_r0, dmainc=True)
        b1 = OP("vector", lambda e: e.tensor_copy(out=zh[:, :], in_=zr[:, :]),
                after=(zl,) + ((prev_zmm,) if prev_zmm else ()))
        b2 = OP("vector", lambda e: e.tensor_copy(out=gen[:, 0:8 * PSC], in_=zh[:, :]),
                after=(b1,))
        b3 = OP("vector", lambda e: e.tensor_tensor(out=gen[:, 0:8 * PSC], in0=zr[:, :],
                                                    in1=gen[:, 0:8 * PSC],
                                                    op=ALU.subtract), after=(b2,))
        b4 = OP("vector", lambda e: e.tensor_copy(out=zl2[:, :], in_=gen[:, 0:8 * PSC]),
                after=(b3,))
        prev_bulk = b4
        for tloc in range(PSC):
            t = PSC * r + tloc
            mmt = None
            dep0 = (prev_split,) if prev_split else ()
            for m in range(8):
                for j in range(4):
                    pr = pair[:, m:m + 9:8]
                    mmt = OP("tensor", lambda e, m=m, j=j, pr=pr: e.matmul(
                        p_y[32 * j:32 * j + 2, 0:256], pr, whH(m, j),
                        start=False, stop=False, tile_position=(0, 32 * j),
                        skip_group_check=True),
                        after=dep0 if (m == 0 and j == 0) else ())
                    mmt = OP("tensor", lambda e, m=m, j=j, pr=pr: e.matmul(
                        p_y[32 * j:32 * j + 2, 0:256], pr, whL(m, j),
                        start=False, stop=False, tile_position=(0, 32 * j),
                        skip_group_check=True))
            ev1 = OP("vector", lambda e: e.tensor_copy(out=Ssc[:, 0:128],
                                                       in_=p_y[0:128, 0:128]),
                     after=(mmt,))
            ev2 = OP("vector", lambda e: e.tensor_copy(out=Ssc[:, 128:256],
                                                       in_=p_y[0:128, 128:256]))
            f1 = OP("tensor", lambda e, t=t: e.matmul(p_t[t % 2][0:128, 0:4],
                                                      Ssc[0:128, 0:128], fmat[:, :],
                                                      start=True, stop=False), after=(ev1,))
            f2 = OP("tensor", lambda e, t=t: e.matmul(p_t[t % 2][0:128, 4:8],
                                                      Ssc[0:128, 128:256], fmat[:, :],
                                                      start=False, stop=False), after=(ev2,))
            z1 = OP("tensor", lambda e, t=t, tloc=tloc: e.matmul(
                p_t[t % 2][0:128, 0:8], idnb[:, :], slot(zh, tloc),
                start=False, stop=False), after=(b4,) if tloc == 0 else ())
            z2 = OP("tensor", lambda e, t=t, tloc=tloc: e.matmul(
                p_t[t % 2][0:128, 0:8], idnb[:, :], slot(zl2, tloc),
                start=False, stop=True))
            prev_zmm = z2
            th = OP("scalar", lambda e, t=t: e.activation(
                out=cand[:, :], in_=p_t[t % 2][0:128, 0:8], func=AF.Tanh), after=(z2,))
            hprev = hcur[:, :] if t == 0 else slot(hr, (tloc - 1) % PSC)
            s1 = OP("vector", lambda e, hprev=hprev: e.tensor_tensor(
                out=dsub2[:, :], in0=cand[:, :], in1=hprev, op=ALU.subtract),
                after=(th,) + ((prev_hw,) if prev_hw else ()))
            s2 = OP("vector", lambda e, tloc=tloc: e.tensor_tensor(
                out=dfull[:, :], in0=dsub2[:, :], in1=slot(ar, tloc), op=ALU.mult),
                after=(s1, al))
            p1 = OP("scalar", lambda e: e.activation(out=pair[:, 0:8], in_=dfull[:, :],
                                                     func=AF.Copy), after=(s2,))
            p3 = OP("vector", lambda e: e.tensor_tensor(out=dsub[:, :], in0=dfull[:, :],
                                                        in1=pair[:, 0:8], op=ALU.subtract),
                    after=(p1,))
            p4 = OP("scalar", lambda e: e.activation(out=pair[:, 8:16], in_=dsub[:, :],
                                                     func=AF.Copy), after=(p3,))
            prev_split = p4
            prev_hw = OP("vector", lambda e, tloc=tloc, hprev=hprev: e.tensor_tensor(
                out=slot(hr, tloc), in0=hprev, in1=dfull[:, :], op=ALU.add),
                after=(s2,) + ((prev_hs_store,) if (tloc == 0 and prev_hs_store) else ()))
        prev_hs_store = OP("gpsimd", lambda e, r=r: e.dma_start(
            out=scr_view(hs_scr, PSC * r, PSC), in_=hr.rearrange("p (c t) -> p c t", c=8)),
            after=(prev_split, prev_hw), dmainc=True)
        sch.emit()
    flush()

    global_state["hs_last"] = prev_hs_store
    import os as _os
    if _os.environ.get("KSTOP") != "p2":
        build_rest(nc, sch, locals())
    return nc


def build_rest(nc, sch, env):
    OP = sch.op
    BAR = nc.all_engine_barrier

    def flush():
        sch.emit()
        sch.fence()
        BAR()

    S = env["S"]
    PT1, PT4 = env["PT1"], env["PT4"]
    NT1, NT4 = env["NT1"], env["NT4"]
    NG1, NG4 = env["NG1"], env["NG4"]
    names = ("arf gen bfa bfb whT wbig it32 lnw_bc lnb_bc bo_bf cw_fm cb_fm "
             "idn ones_col ones_rf ones_rb c1_row c2_col thr1 thr2 scs vrow "
             "mcol mrow dq1b s2b mx_tm dq2_tm mxt tstage p_a p_b p_c p_t0 "
             "p_t1 p_y x_in w1_d w2_d y_out z_scr a_scr hs_scr y1_scr xq_scr "
             "xq2_scr m1_scr dq2_scr").split()
    (arf, gen, bfa, bfb, whT, wbig, it32, lnw_bc, lnb_bc, bo_bf, cw_fm, cb_fm,
     idn, ones_col, ones_rf, ones_rb, c1_row, c2_col, thr1, thr2, scs, vrow,
     mcol, mrow, dq1b, s2b, mx_tm, dq2_tm, mxt, tstage, p_a, p_b, p_c, p_t0,
     p_t1, p_y, x_in, w1_d, w2_d, y_out, z_scr, a_scr, hs_scr, y1_scr, xq_scr,
     xq2_scr, m1_scr, dq2_scr) = (env[k] for k in names)
    ln_plain = env["ln_plain"]
    scr_view = env["scr_view"]
    global_state = env["global_state"]

    def nch(m):
        return dm0(m) // 128

    for r in range(NT1):
        ldx = []
        for g in range(NG1):
            ldx.append(OP("sync", lambda e, g=g, r=r: e.dma_start(
                out=arf[:, 1024 * g:1024 * (g + 1)],
                in_=x_in[PT1 * r + 128 * g:PT1 * r + 128 * (g + 1), :]), dmainc=True))
        hsl = OP("sync", lambda e, r=r: e.dma_start(
            out=arf[:, 4096:4096 + 8 * PT1].rearrange("p (c t) -> p c t", c=8),
            in_=scr_view(hs_scr, PT1 * r, PT1)),
            after=(global_state["hs_last"],) if r == 0 else (), dmainc=True)
        hH = OP("vector", lambda e: e.tensor_copy(out=bfb[:, 0:4096],
                                                  in_=arf[:, 4096:8192]), after=(hsl,))
        hU = OP("vector", lambda e: e.tensor_copy(out=gen[:, 4096:8192], in_=bfb[:, 0:4096]),
                after=(hH,))
        hS = OP("vector", lambda e: e.tensor_tensor(out=gen[:, 4096:8192],
                                                    in0=arf[:, 4096:8192],
                                                    in1=gen[:, 4096:8192], op=ALU.subtract),
                after=(hU,))
        hL = OP("vector", lambda e: e.tensor_copy(out=bfb[:, 4096:8192],
                                                  in_=gen[:, 4096:8192]), after=(hS,))
        prev_q = None
        for g in range(NG1):
            x0v = arf[:, 1024 * g:1024 * (g + 1)]
            mm_done = []
            for eh, ps in ((0, p_a), (1, p_b)):
                mmt = None
                first = True
                for m in range(8):
                    for sofs, wofs in ((0, 0), (0, 8192), (4096, 0)):
                        fdeps = (hH, hL) if first else ()
                        if first and prev_q is not None:
                            fdeps = fdeps + (prev_q,)
                        mmt = OP("tensor", lambda e, g=g, m=m, sofs=sofs, wofs=wofs,
                                 eh=eh, ps=ps, first=first: e.matmul(
                                     ps[:, 0:512],
                                     bfb[:, sofs + PT1 * m + 128 * g:
                                         sofs + PT1 * m + 128 * (g + 1)],
                                     wbig[:, wofs + 1024 * nch(m) + 512 * eh:
                                          wofs + 1024 * nch(m) + 512 * (eh + 1)],
                                     start=first, stop=False), after=fdeps)
                        first = False
                mmt = OP("tensor", lambda e, eh=eh, ps=ps: e.matmul(
                    ps[:, 0:512], ones_rb[0:1, :], bo_bf[0:1, 512 * eh:512 * (eh + 1)],
                    start=False, stop=False))
                mmt = OP("tensor", lambda e, eh=eh, ps=ps: e.matmul(
                    ps[:, 0:512], ones_rb[0:1, :],
                    bo_bf[0:1, 1024 + 512 * eh:1024 + 512 * (eh + 1)],
                    start=False, stop=True))
                mm_done.append(mmt)
            n1tok = ln_plain(x0v, gen[:, 1024:2048], gen[:, 0:1024], gen[:, 2048:3072], 0,
                             deps=(ldx[g],))
            aA = OP("vector", lambda e: e.tensor_tensor(out=gen[:, 2048:2560],
                                                        in0=p_a[:, 0:512],
                                                        in1=gen[:, 1024:1536], op=ALU.add),
                    after=(mm_done[0], n1tok))
            aB = OP("vector", lambda e: e.tensor_tensor(out=gen[:, 2560:3072],
                                                        in0=p_b[:, 0:512],
                                                        in1=gen[:, 1536:2048], op=ALU.add),
                    after=(mm_done[1], n1tok))
            ln3 = ln_plain(gen[:, 2048:3072], gen[:, 3072:4096], gen[:, 0:1024],
                           gen[:, 1024:2048], 2, deps=(aA, aB))
            y1v = OP("vector", lambda e, x0v=x0v: e.tensor_tensor(
                out=gen[:, 2048:3072], in0=x0v,
                in1=gen[:, 3072:4096], op=ALU.add), after=(ln3,))
            global_state["y1_last"] = OP("gpsimd", lambda e, r=r, g=g: e.dma_start(
                out=y1_scr[PT1 * r + 128 * g:PT1 * r + 128 * (g + 1), :],
                in_=gen[:, 2048:3072]), after=(y1v,), dmainc=True)
            n2tok = ln_plain(gen[:, 2048:3072], gen[:, 3072:4096], gen[:, 0:1024],
                             gen[:, 1024:2048], 4, deps=(y1v,))
            q1 = OP("vector", lambda e: e.reduce_max(out=mcol[:, 6:7], in_=gen[:, 3072:4096],
                                                     axis=AX.X, apply_absolute_value=True),
                    after=(n2tok,))
            q2 = OP("vector", lambda e: e.tensor_scalar(out=mcol[:, 6:7], in0=mcol[:, 6:7],
                                                        scalar1=1e-5, scalar2=None,
                                                        op0=ALU.max), after=(q1,))
            global_state["m1_last"] = OP("gpsimd", lambda e, r=r, g=g: e.dma_start(
                out=m1_scr[1 + PT1 * r + 128 * g:1 + PT1 * r + 128 * (g + 1), :],
                in_=mcol[:, 6:7]), after=(q2,), dmainc=True)
            q3 = OP("vector", lambda e: e.reciprocal(out=mcol[:, 7:8], in_=mcol[:, 6:7]),
                    after=(q2,))
            q4 = OP("vector", lambda e: e.tensor_scalar(out=gen[:, 0:1024],
                                                        in0=gen[:, 3072:4096],
                                                        scalar1=mcol[:, 7:8], scalar2=127.0,
                                                        op0=ALU.mult, op1=ALU.mult),
                    after=(q3,))
            q5a = OP("gpsimd", lambda e: e.tensor_copy(out=it32[:, :], in_=gen[:, 0:512]),
                     after=(q4,))
            q5b = OP("gpsimd", lambda e: e.tensor_copy(out=gen[:, 1024:1536], in_=it32[:, :]),
                     after=(q5a,))
            q5c = OP("gpsimd", lambda e: e.tensor_copy(out=it32[:, :], in_=gen[:, 512:1024]),
                     after=(q5b,))
            q6 = OP("gpsimd", lambda e: e.tensor_copy(out=gen[:, 1536:2048], in_=it32[:, :]),
                    after=(q5c,))
            q7 = OP("vector", lambda e: e.tensor_tensor(out=gen[:, 2048:3072],
                                                        in0=gen[:, 0:1024],
                                                        in1=gen[:, 1024:2048],
                                                        op=ALU.subtract),
                    after=(q6, global_state["y1_last"]))
            q8 = OP("vector", lambda e: e.scalar_tensor_tensor(out=gen[:, 3072:4096],
                                                               in0=gen[:, 2048:3072],
                                                               scalar=0.5,
                                                               in1=gen[:, 1024:2048],
                                                               op0=ALU.is_gt, op1=ALU.add),
                    after=(q7,))
            q9 = OP("vector", lambda e: e.tensor_scalar(out=gen[:, 0:1024],
                                                        in0=gen[:, 2048:3072],
                                                        scalar1=-0.5, scalar2=None,
                                                        op0=ALU.is_lt), after=(q7,))
            qa_ = OP("vector", lambda e: e.scalar_tensor_tensor(out=gen[:, 1024:2048],
                                                                in0=gen[:, 0:1024],
                                                                scalar=-1.0,
                                                                in1=gen[:, 3072:4096],
                                                                op0=ALU.mult, op1=ALU.add),
                     after=(q8, q9))
            qb_ = OP("vector", lambda e: e.tensor_copy(out=bfa[:, 0:1024],
                                                       in_=gen[:, 1024:2048]), after=(qa_,))
            prev_q = qb_
            prev_xq = global_state.get("xq_last")
            trs = None
            for dc in range(8):
                trs = OP("scalar", lambda e, dc=dc: e.dma_start(
                    out=tstage[:, 128 * dc:128 * (dc + 1)],
                    in_=bfa[:, 128 * dc:128 * (dc + 1)], transpose=True),
                    after=(qb_,) + ((prev_xq,) if prev_xq else ()), dmainc=True)
            global_state["xq_last"] = OP(
                "gpsimd", lambda e, r=r, g=g: e.dma_start(
                    out=xq_scr.rearrange("p (c t) -> p c t", c=8)[
                        :, :, 1 + PT1 * r + 128 * g:1 + PT1 * r + 128 * (g + 1)],
                    in_=tstage.rearrange("p (c t) -> p c t", c=8)),
                after=(trs,), dmainc=True)
        flush()

    for rnd in range(DH // 512):
        lds = []
        for k in range(4):
            i = 4 * rnd + k
            lds.append(OP("sync", lambda e, k=k, i=i: e.dma_start(
                out=gen[:, 1024 * k:1024 * (k + 1)],
                in_=w1_d[128 * i:128 * (i + 1), :]), dmainc=True))
        g1a = OP("vector", lambda e: e.scalar_tensor_tensor(
            out=gen[:, 4096:8192], in0=gen[:, 0:4096], scalar=-1.0,
            in1=gen[:, 0:4096], op0=ALU.mult, op1=ALU.max), after=tuple(lds))
        g1 = OP("vector", lambda e: e.tensor_scalar(out=gen[:, 4096:8192],
                                                    in0=gen[:, 4096:8192],
                                                    scalar1=thr1[:, 0:1], scalar2=None,
                                                    op0=ALU.is_ge), after=(g1a,))
        g2 = OP("scalar", lambda e: e.activation(out=arf[:, 0:4096], in_=gen[:, 0:4096],
                                                 func=AF.Sign), after=tuple(lds))
        g3 = OP("vector", lambda e: e.tensor_tensor(out=arf[:, 4096:8192],
                                                    in0=gen[:, 4096:8192],
                                                    in1=arf[:, 0:4096], op=ALU.mult),
                after=(g1, g2))
        g4 = OP("vector", lambda e: e.tensor_copy(out=bfa[:, 0:4096], in_=arf[:, 4096:8192]),
                after=(g3,))
        for k in range(4):
            ht = 4 * rnd + k
            for dc in range(8):
                OP("scalar", lambda e, k=k, ht=ht, dc=dc: e.dma_start(
                    out=wbig[:, 4096 * dc + 128 * ht:4096 * dc + 128 * (ht + 1)],
                    in_=bfa[:, 1024 * k + 128 * dc:1024 * k + 128 * (dc + 1)],
                    transpose=True), after=(g4,), dmainc=True)
        flush()

    for r in range(NT4):
        t0 = PT4 * r
        bdep = (global_state["xq_last"], global_state["m1_last"]) if r == 0 else ()
        xl = OP("sync", lambda e, t0=t0: e.dma_start(
            out=bfb[:, 0:8 * (PT4 + 2)].rearrange("p (c t) -> p c t", c=8),
            in_=xq_scr.rearrange("p (c t) -> p c t", c=8)[:, :, t0:t0 + PT4 + 2]),
            after=bdep, dmainc=True)
        ml = OP("sync", lambda e, t0=t0: e.dma_start(
            out=mrow[0:1, :], in_=m1_scr[t0:t0 + PT4 + 2, :].rearrange("t a -> a t"), allow_slow_non_contiguous=True),
            after=bdep, dmainc=True)
        bq = OP("tensor", lambda e: e.matmul(p_c[:, 0:PT4 + 2], c1_row[0:1, :], mrow[0:1, :],
                                             start=True, stop=True), after=(ml,))
        bqe = OP("vector", lambda e: e.tensor_copy(out=dq1b[:, :], in_=p_c[:, 0:PT4 + 2]),
                 after=(bq,))
        mxprev = OP("vector", lambda e: e.memset(mxt[:, :], 0.0))
        prev_ev = {0: None, 1: None}
        sil_ev = []
        for hc in range(32):
            ps = p_a if hc % 2 == 0 else p_b
            mmt = None
            for dc in range(8):
                fdeps = ()
                if dc == 0:
                    fdeps = (xl,)
                    if prev_ev[hc % 2] is not None:
                        fdeps = fdeps + (prev_ev[hc % 2],)
                mmt = OP("tensor", lambda e, hc=hc, dc=dc, ps=ps: e.matmul(
                    ps[:, 0:PT4 + 2],
                    wbig[:, 4096 * dc + 128 * hc:4096 * dc + 128 * (hc + 1)],
                    bfb[:, (PT4 + 2) * dc:(PT4 + 2) * (dc + 1)],
                    start=(dc == 0), stop=(dc == 7)), after=fdeps)
            dv = OP("vector", lambda e, ps=ps: e.tensor_tensor(out=gen[:, 0:PT4 + 2],
                                                               in0=ps[:, 0:PT4 + 2],
                                                               in1=dq1b[:, :], op=ALU.mult),
                    after=(mmt, bqe))
            prev_ev[hc % 2] = dv
            c1v = OP("vector", lambda e, hc=hc: e.tensor_scalar(
                out=gen[:, 1024:1024 + PT4], in0=gen[:, 1:1 + PT4],
                scalar1=cw_fm[:, 3 * hc + 1:3 * hc + 2], scalar2=cb_fm[:, hc:hc + 1],
                op0=ALU.mult, op1=ALU.add), after=(dv,))
            c2v = OP("vector", lambda e, hc=hc: e.scalar_tensor_tensor(
                out=gen[:, 2048:2048 + PT4], in0=gen[:, 0:PT4],
                scalar=cw_fm[:, 3 * hc:3 * hc + 1], in1=gen[:, 1024:1024 + PT4],
                op0=ALU.mult, op1=ALU.add), after=(c1v,))
            c3v = OP("vector", lambda e, hc=hc: e.scalar_tensor_tensor(
                out=gen[:, 1024:1024 + PT4], in0=gen[:, 2:2 + PT4],
                scalar=cw_fm[:, 3 * hc + 2:3 * hc + 3], in1=gen[:, 2048:2048 + PT4],
                op0=ALU.mult, op1=ALU.add), after=(c2v,))
            sl = OP("scalar", lambda e, hc=hc: e.activation(
                out=arf[:, PT4 * hc:PT4 * (hc + 1)], in_=gen[:, 1024:1024 + PT4],
                func=AF.Silu), after=(c3v,))
            mxa = OP("vector", lambda e, hc=hc: e.scalar_tensor_tensor(
                out=gen[:, 3072:3072 + PT4], in0=arf[:, PT4 * hc:PT4 * (hc + 1)],
                scalar=-1.0, in1=arf[:, PT4 * hc:PT4 * (hc + 1)],
                op0=ALU.mult, op1=ALU.max), after=(sl, mxprev))
            mxprev = OP("vector", lambda e: e.tensor_tensor(
                out=mxt[:, :], in0=mxt[:, :], in1=gen[:, 3072:3072 + PT4],
                op=ALU.max), after=(mxa,))
            sil_ev.append(sl)
        rds = []
        trprev = mxprev
        for g in range(NG4):
            tr = OP("tensor", lambda e, g=g: e.transpose(
                p_c[0:128, 128:256], mxt[:, 128 * g:128 * (g + 1)], idn[:, :]),
                after=(trprev, bqe))
            rd = OP("vector", lambda e, g=g: e.reduce_max(
                out=mx_tm[:, g:g + 1], in_=p_c[0:128, 128:256], axis=AX.X), after=(tr,))
            trprev = rd
            rds.append(rd)
        cl = OP("vector", lambda e: e.tensor_scalar(out=mx_tm[:, :], in0=mx_tm[:, :],
                                                    scalar1=1e-5, scalar2=None, op0=ALU.max),
                after=tuple(rds))
        dq2v = OP("vector", lambda e: e.tensor_scalar(out=dq2_tm[:, :], in0=mx_tm[:, :],
                                                      scalar1=c2_col[:, 0:1], scalar2=None,
                                                      op0=ALU.mult), after=(cl,))
        global_state["dq2_last"] = OP("gpsimd", lambda e, t0=t0: e.dma_start(
            out=dq2_scr[t0:t0 + PT4, :].rearrange("(g p) a -> p (g a)", p=128),
            in_=dq2_tm[:, :], allow_slow_non_contiguous=True), after=(dq2v,), dmainc=True)
        rc = OP("vector", lambda e: e.reciprocal(out=mx_tm[:, :], in_=mx_tm[:, :]),
                after=(cl, dq2v))
        s2toks = []
        prevrow = rc
        for g in range(NG4):
            trr = OP("tensor", lambda e, g=g: e.matmul(
                p_c[0:1, 256 + 128 * g:256 + 128 * (g + 1)], mx_tm[:, g:g + 1], idn[:, :],
                start=(g == 0), stop=(g == NG4 - 1)), after=(prevrow,))
            prevrow = trr
            s2toks.append(trr)
        s2r_ = OP("vector", lambda e: e.tensor_scalar(
            out=gen[0:1, 0:PT4], in0=p_c[0:1, 256:256 + PT4], scalar1=127.0,
            scalar2=None, op0=ALU.mult), after=tuple(s2toks))
        bq2 = OP("tensor", lambda e: e.matmul(p_c[:, 256:256 + PT4], ones_rf[0:1, :],
                                              gen[0:1, 0:PT4], start=True, stop=True),
                 after=(s2r_,))
        s2be = OP("vector", lambda e: e.tensor_copy(out=s2b[:, :], in_=p_c[:, 256:256 + PT4]),
                  after=(bq2,))
        prev_st = None
        st_hist = []
        for hc in range(32):
            u = OP("vector", lambda e, hc=hc: e.tensor_tensor(
                out=gen[:, 0:PT4], in0=arf[:, PT4 * hc:PT4 * (hc + 1)], in1=s2b[:, :],
                op=ALU.mult), after=(s2be, sil_ev[hc]) + ((prev_st,) if prev_st else ()))
            i1 = OP("gpsimd", lambda e: e.tensor_copy(out=it32[:, 0:PT4], in_=gen[:, 0:PT4]),
                    after=(u,))
            i2 = OP("gpsimd", lambda e: e.tensor_copy(out=gen[:, 1024:1024 + PT4],
                                                      in_=it32[:, 0:PT4]), after=(i1,))
            d1 = OP("vector", lambda e: e.tensor_tensor(out=gen[:, 2048:2048 + PT4],
                                                        in0=gen[:, 0:PT4],
                                                        in1=gen[:, 1024:1024 + PT4],
                                                        op=ALU.subtract), after=(i2,))
            e1 = OP("vector", lambda e: e.scalar_tensor_tensor(out=gen[:, 3072:3072 + PT4],
                                                               in0=gen[:, 2048:2048 + PT4],
                                                               scalar=0.5,
                                                               in1=gen[:, 1024:1024 + PT4],
                                                               op0=ALU.is_gt, op1=ALU.add),
                    after=(d1,))
            e2 = OP("vector", lambda e: e.tensor_scalar(out=gen[:, 0:PT4],
                                                        in0=gen[:, 2048:2048 + PT4],
                                                        scalar1=-0.5, scalar2=None,
                                                        op0=ALU.is_lt), after=(d1,))
            e3 = OP("vector", lambda e: e.scalar_tensor_tensor(out=gen[:, 1024:1024 + PT4],
                                                               in0=gen[:, 0:PT4],
                                                               scalar=-1.0,
                                                               in1=gen[:, 3072:3072 + PT4],
                                                               op0=ALU.mult, op1=ALU.add),
                    after=(e1, e2))
            e4deps = (e3,)
            if len(st_hist) >= 8:
                e4deps = e4deps + (st_hist[-8],)
            e4 = OP("vector", lambda e, hc=hc: e.tensor_copy(
                out=bfa[:, PT4 * (hc % 8):PT4 * (hc % 8 + 1)], in_=gen[:, 1024:1024 + PT4]),
                after=e4deps)
            prev_st = OP("gpsimd", lambda e, hc=hc, t0=t0: e.dma_start(
                out=xq2_scr[:, S * hc + t0:S * hc + t0 + PT4],
                in_=bfa[:, PT4 * (hc % 8):PT4 * (hc % 8 + 1)]), after=(e4,), dmainc=True)
            st_hist.append(prev_st)
            global_state["xq2_last"] = prev_st
        flush()

    for i in range(8):
        ld = OP("sync", lambda e, i=i: e.dma_start(
            out=gen[:, 0:4096], in_=w2_d[128 * i:128 * (i + 1), :]), dmainc=True)
        g1a = OP("vector", lambda e: e.scalar_tensor_tensor(
            out=gen[:, 4096:8192], in0=gen[:, 0:4096], scalar=-1.0,
            in1=gen[:, 0:4096], op0=ALU.mult, op1=ALU.max), after=(ld,))
        g1 = OP("vector", lambda e: e.tensor_scalar(out=gen[:, 4096:8192],
                                                    in0=gen[:, 4096:8192],
                                                    scalar1=thr2[:, 0:1], scalar2=None,
                                                    op0=ALU.is_ge), after=(g1a,))
        g2 = OP("scalar", lambda e: e.activation(out=arf[:, 0:4096], in_=gen[:, 0:4096],
                                                 func=AF.Sign), after=(ld,))
        g3 = OP("vector", lambda e: e.tensor_tensor(out=arf[:, 4096:8192],
                                                    in0=gen[:, 4096:8192],
                                                    in1=arf[:, 0:4096], op=ALU.mult),
                after=(g1, g2))
        g4 = OP("vector", lambda e: e.tensor_copy(out=bfa[:, 0:4096], in_=arf[:, 4096:8192]),
                after=(g3,))
        for hc in range(32):
            OP("scalar", lambda e, hc=hc, i=i: e.dma_start(
                out=wbig[:, 1024 * hc + 128 * i:1024 * hc + 128 * (i + 1)],
                in_=bfa[:, 128 * hc:128 * (hc + 1)], transpose=True),
               after=(g4,), dmainc=True)
        flush()

    ystores = []
    PT5 = min(512, S)
    NT5 = S // PT5
    NG5 = PT5 // 128
    dq5_tm = gen[:, 6656:6656 + NG5]
    for r in range(NT5):
        t0 = PT5 * r
        bdep5 = ((global_state["xq2_last"], global_state["dq2_last"],
                  global_state["y1_last"]) if r == 0 else ())
        x2l = OP("sync", lambda e, t0=t0: e.dma_start(
            out=whT[:, 0:32 * PT5].rearrange("p (c t) -> p c t", c=32),
            in_=xq2_scr.rearrange("p (c t) -> p c t", c=32)[:, :, t0:t0 + PT5]),
            after=bdep5, dmainc=True)
        y1l = []
        for g in range(NG5):
            y1l.append(OP("sync", lambda e, t0=t0, g=g: e.dma_start(
                out=arf[:, 1024 * g:1024 * (g + 1)],
                in_=y1_scr[t0 + 128 * g:t0 + 128 * (g + 1), :]), after=bdep5, dmainc=True))
        dql = OP("sync", lambda e, t0=t0: e.dma_start(
            out=dq5_tm[:, :],
            in_=dq2_scr[t0:t0 + PT5, :].rearrange("(g p) a -> p (g a)", p=128),
            allow_slow_non_contiguous=True), after=bdep5, dmainc=True)
        prev_f = {0: None, 1: None}
        for g in range(NG5):
            mm_done = []
            for eh, ps in ((0, p_a), (1, p_b)):
                mmt = None
                for hc in range(32):
                    fdeps = ()
                    if hc == 0:
                        fdeps = (x2l,)
                        if prev_f[eh] is not None:
                            fdeps = fdeps + (prev_f[eh],)
                    mmt = OP("tensor", lambda e, hc=hc, g=g, eh=eh, ps=ps: e.matmul(
                        ps[:, 0:512],
                        whT[:, PT5 * hc + 128 * g:PT5 * hc + 128 * (g + 1)],
                        wbig[:, 1024 * hc + 512 * eh:1024 * hc + 512 * (eh + 1)],
                        start=(hc == 0), stop=(hc == 31)), after=fdeps)
                mm_done.append(mmt)
            f1 = OP("vector", lambda e, g=g: e.tensor_scalar(
                out=gen[:, 0:512], in0=p_a[:, 0:512], scalar1=dq5_tm[:, g:g + 1],
                scalar2=None, op0=ALU.mult), after=(mm_done[0], dql))
            f2 = OP("vector", lambda e, g=g: e.tensor_scalar(
                out=gen[:, 512:1024], in0=p_b[:, 0:512], scalar1=dq5_tm[:, g:g + 1],
                scalar2=None, op0=ALU.mult), after=(mm_done[1], dql))
            prev_f[0], prev_f[1] = f1, f2
            ln4 = ln_plain(gen[:, 0:1024], gen[:, 1024:2048], gen[:, 2048:3072],
                           gen[:, 3072:4096], 0,
                           deps=(f1, f2) + ((ystores[-1],) if ystores else ()))
            y2v = OP("vector", lambda e, g=g: e.tensor_tensor(
                out=gen[:, 2048:3072], in0=arf[:, 1024 * g:1024 * (g + 1)],
                in1=gen[:, 1024:2048], op=ALU.add), after=(ln4, y1l[g]))
            ystores.append(OP("gpsimd", lambda e, t0=t0, g=g: e.dma_start(
                out=y_out[t0 + 128 * g:t0 + 128 * (g + 1), :], in_=gen[:, 2048:3072]),
                after=(y2v,), dmainc=True))
        flush()
    OP("vector", lambda e: e.memset(scs[:, 0:1], 0.0), after=tuple(ystores))
    sch.emit()


def kernel(L0_attn_x, L0_attn_W_in, L0_attn_b_in, L0_attn_W_h, L0_attn_b_h,
           L0_attn_W_out, L0_attn_b_out, L0_attn_ln_w, L0_attn_ln_b,
           L0_ffn_w1, L0_ffn_conv_w, L0_ffn_conv_b, L0_ffn_w2):
    L0_attn_x = np.asarray(L0_attn_x)
    B, S, D_ = L0_attn_x.shape
    nc = build_program(S=S)
    base = {
        "w_in_d": np.ascontiguousarray(L0_attn_W_in, dtype=np.float32),
        "b_in_d": np.ascontiguousarray(L0_attn_b_in, dtype=np.float32),
        "w_h_d": np.ascontiguousarray(L0_attn_W_h, dtype=np.float32),
        "b_h_d": np.ascontiguousarray(L0_attn_b_h, dtype=np.float32),
        "w_out_d": np.ascontiguousarray(L0_attn_W_out, dtype=np.float32),
        "b_out_d": np.ascontiguousarray(L0_attn_b_out, dtype=np.float32),
        "ln_w_d": np.ascontiguousarray(L0_attn_ln_w, dtype=np.float32),
        "ln_b_d": np.ascontiguousarray(L0_attn_ln_b, dtype=np.float32),
        "w1_d": np.ascontiguousarray(L0_ffn_w1, dtype=np.float32),
        "cw_d": np.ascontiguousarray(np.asarray(L0_ffn_conv_w).reshape(DH, 3),
                                     dtype=np.float32),
        "cb_d": np.ascontiguousarray(L0_ffn_conv_b, dtype=np.float32),
        "w2_d": np.ascontiguousarray(L0_ffn_w2, dtype=np.float32),
    }
    in_maps = []
    for b in range(B):
        m = dict(base)
        m["x_in"] = np.ascontiguousarray(L0_attn_x[b], dtype=np.float32)
        in_maps.append(m)
    res = run_bass_kernel_spmd(nc, in_maps, core_ids=list(range(B)))
    return np.stack([res.results[b]["y_out"] for b in range(B)],
                    axis=0).astype(np.float32)


# version 06 (final)
# speedup vs baseline: 1.0890x; 1.0135x over previous
import numpy as np
import concourse.bass as bass
import concourse.mybir as mybir
from concourse.bass_utils import run_bass_kernel_spmd

F32 = mybir.dt.float32
BF16 = mybir.dt.bfloat16
I32 = mybir.dt.int32
AF = mybir.ActivationFunctionType
ALU = mybir.AluOpType
AX = mybir.AxisListType

D = 1024
DH = 4096

ENGS = ("sync", "tensor", "vector", "scalar", "gpsimd")


def dm0(m):
    return 256 * (m % 4) + 128 * (m // 4)


class Tok:
    __slots__ = ("eng", "val")

    def __init__(self, eng, val):
        self.eng = eng
        self.val = val


class Sched:

    SELF_DIST = 4

    def __init__(self, nc):
        self.nc = nc
        self.semnames = list(ENGS) + [e + "_dma" for e in ("gpsimd", "scalar", "sync")]
        self.sem = {n: nc.semaphore("sem_" + n).__enter__() for n in self.semnames}
        self.cnt = {n: 0 for n in self.semnames}
        self.events = []
        self.pos = {e: 0 for e in ENGS}
        self.tok_pos = {}
        self.fence_w = []

    def fence(self):
        self.fence_w = [(n, self.cnt[n]) for n in self.semnames if self.cnt[n] > 0]

    def op(self, eng, fn, after=(), dmainc=False):
        sn = eng + "_dma" if (dmainc and eng in ("gpsimd", "scalar", "sync")) else eng
        waits = list(self.fence_w)
        for t in after:
            if t is None:
                continue
            v = self.cnt[t.eng] if t.eng.endswith("_dma") else t.val
            waits.append((t.eng, v))
        n = 16 if dmainc else 1
        self.cnt[sn] += n
        tok = Tok(sn, self.cnt[sn])
        self.pos[eng] += 1
        self.tok_pos[(sn, tok.val)] = (eng, self.pos[eng], dmainc)
        self.events.append((eng, sn, fn, waits, n, self.pos[eng]))
        return tok

    def emit(self):
        ev_by_eng = {e: [] for e in ENGS}
        for eng, sn, fn, waits, n, pos in self.events:
            ev_by_eng[eng].append((sn, fn, waits, n, pos))
        nc = self.nc
        sem = self.sem
        tok_pos = self.tok_pos
        SELF = self.SELF_DIST

        def body(engine, eng_name):
            done = {}
            for sn, fn, waits, n, pos in ev_by_eng[eng_name]:
                for wsn, wval in waits:
                    info = tok_pos.get((wsn, wval))
                    if info is not None:
                        weng, dpos, wdma = info
                        if (weng == eng_name and not wdma
                                and pos - dpos > SELF):
                            continue
                    if done.get(wsn, 0) >= wval:
                        continue
                    engine.wait_ge(sem[wsn], wval)
                    done[wsn] = wval
                ins = fn(engine)
                ins.then_inc(sem[sn], n)

        with nc.Block() as blk:
            for e in ENGS:
                if ev_by_eng[e]:
                    getattr(blk, e)(lambda eng, e=e: body(eng, e))
        self.events = []
        self.pos = {e: 0 for e in ENGS}
        self.tok_pos = {}


def build_program(S=4096):
    PT1 = min(512, S)
    PSC = min(256, S)
    PT4 = min(256, S)
    NT1 = S // PT1
    NSC = S // PSC
    NT4 = S // PT4
    NG1 = PT1 // 128
    NG4 = max(1, PT4 // 128)

    nc = bass.Bass("TRN2", target_bir_lowering=False)

    x_in = nc.dram_tensor("x_in", [S, D], F32, kind="ExternalInput")
    w_in_d = nc.dram_tensor("w_in_d", [2 * D, D], F32, kind="ExternalInput")
    b_in_d = nc.dram_tensor("b_in_d", [2 * D], F32, kind="ExternalInput")
    w_h_d = nc.dram_tensor("w_h_d", [D, D], F32, kind="ExternalInput")
    b_h_d = nc.dram_tensor("b_h_d", [D], F32, kind="ExternalInput")
    w_out_d = nc.dram_tensor("w_out_d", [D, D], F32, kind="ExternalInput")
    b_out_d = nc.dram_tensor("b_out_d", [D], F32, kind="ExternalInput")
    ln_w_d = nc.dram_tensor("ln_w_d", [D], F32, kind="ExternalInput")
    ln_b_d = nc.dram_tensor("ln_b_d", [D], F32, kind="ExternalInput")
    w1_d = nc.dram_tensor("w1_d", [DH, D], F32, kind="ExternalInput")
    cw_d = nc.dram_tensor("cw_d", [DH, 3], F32, kind="ExternalInput")
    cb_d = nc.dram_tensor("cb_d", [DH], F32, kind="ExternalInput")
    w2_d = nc.dram_tensor("w2_d", [D, DH], F32, kind="ExternalInput")
    y_out = nc.dram_tensor("y_out", [S, D], F32, kind="ExternalOutput")

    import os as _os2
    _dbg = _os2.environ.get("KDBG") == "1"
    _kw = {"kind": "ExternalOutput"} if _dbg else {}
    z_scr = nc.dram_tensor("z_scr", [128, 8 * S], F32, **_kw)
    a_scr = nc.dram_tensor("a_scr", [128, 8 * S], F32, **_kw)
    hs_scr = nc.dram_tensor("hs_scr", [128, 8 * S], F32, **_kw)
    y1_scr = nc.dram_tensor("y1_scr", [S, D], F32, **_kw)
    xq_scr = nc.dram_tensor("xq_scr", [128, 8 * (S + 2)], BF16, **_kw)
    xq2_scr = nc.dram_tensor("xq2_scr", [128, 32 * S], BF16, **_kw)
    m1_scr = nc.dram_tensor("m1_scr", [S + 2, 1], F32, **_kw)
    dq2_scr = nc.dram_tensor("dq2_scr", [S, 1], F32, **_kw)

    A = nc.alloc_sbuf_tensor
    arf = A("arf", [128, 8192], F32)
    gen = A("gen", [128, 8192], F32)
    bfa = A("bfa", [128, 8192], BF16)
    bfb = A("bfb", [128, 8192], BF16)
    whT = A("whT", [128, 16384], BF16)
    wbig = A("wbig", [128, 32768], BF16)
    it32 = A("it32", [128, 512], I32)
    lnw_bc = A("lnw_bc", [128, 1024], F32)
    lnb_bc = A("lnb_bc", [128, 1024], F32)
    bu_fm = A("bu_fm", [128, 8], F32)
    zb_fm = A("zb_fm", [128, 8], F32)
    bh_fm = A("bh_fm", [128, 8], F32)
    cw_fm = A("cw_fm", [128, 96], F32)
    cb_fm = A("cb_fm", [128, 32], F32)
    fmat = A("fmat", [128, 4], F32)
    idn = A("idn", [128, 128], F32)
    idnb = A("idnb", [128, 128], BF16)
    ones_col = A("ones_col", [128, 1], F32)
    ones_rf = A("ones_rf", [1, 128], F32)
    ones_rb = A("ones_rb", [1, 128], BF16)
    c1_row = A("c1_row", [1, 128], F32)
    c2_col = A("c2_col", [128, 1], F32)
    thr1 = A("thr1", [128, 1], F32)
    thr2 = A("thr2", [128, 1], F32)
    scs = A("scs", [128, 16], F32)
    vrow = A("vrow", [1, 16], F32)
    zrow = A("zrow", [128, 16], BF16)
    pair = A("pair", [128, 16], BF16)
    dHf = A("dHf", [128, 8], F32)
    dsub = A("dsub", [128, 8], F32)
    dsub2 = A("dsub2", [128, 8], F32)
    dfull = A("dfull", [128, 8], F32)
    qsc = A("qsc", [128, 8], F32)
    cand = A("cand", [128, 8], F32)
    hcur = A("hcur", [128, 8], F32)
    mcol = A("mcol", [128, 8], F32)
    tstage = A("tstage", [128, 1024], BF16)
    bo_bf = wbig[:, 16384:18432]
    Ssc = arf[:, 6144:6400]
    mxt = gen[:, 4096:4096 + PT4]
    dq1b = gen[:, 4608:4608 + PT4 + 2]
    s2b = gen[:, 5632:5632 + PT4]
    mx_tm = gen[:, 6656:6656 + NG4]
    dq2_tm = gen[:, 6720:6720 + NG4]
    mrow = gen[:, 7168:7168 + PT4 + 2]

    p_y = nc.alloc_psum_tensor("p_y", [128, 512], F32)
    p_t0 = nc.alloc_psum_tensor("p_t0", [128, 512], F32)
    p_t1 = nc.alloc_psum_tensor("p_t1", [128, 512], F32)
    p_a = nc.alloc_psum_tensor("p_a", [128, 512], F32)
    p_b = nc.alloc_psum_tensor("p_b", [128, 512], F32)
    p_c = nc.alloc_psum_tensor("p_c", [128, 512], F32)
    p_t = [p_t0, p_t1]

    sch = Sched(nc)
    OP = sch.op
    global_state = {}
    BAR = nc.all_engine_barrier

    def flush():
        sch.emit()
        sch.fence()
        BAR()

    def scr_view(scr, t0, nsteps):
        return scr.rearrange("p (c t) -> p c t", c=8)[:, :, t0:t0 + nsteps]

    ldc = []
    ldc.append(OP("sync", lambda e: e.dma_start(out=lnw_bc[0:1, :], in_=ln_w_d[None, :]), dmainc=True))
    ldc.append(OP("sync", lambda e: e.dma_start(out=lnb_bc[0:1, :], in_=ln_b_d[None, :]), dmainc=True))
    ldc.append(OP("sync", lambda e: e.dma_start(
        out=bu_fm[:, :], in_=b_in_d.rearrange("(c p) -> p c", p=128)[:, 0:8], allow_slow_non_contiguous=True), dmainc=True))
    ldc.append(OP("sync", lambda e: e.dma_start(
        out=zb_fm[:, :], in_=b_in_d.rearrange("(c p) -> p c", p=128)[:, 8:16], allow_slow_non_contiguous=True), dmainc=True))
    ldc.append(OP("sync", lambda e: e.dma_start(
        out=bh_fm[:, :], in_=b_h_d.rearrange("(c p) -> p c", p=128), allow_slow_non_contiguous=True), dmainc=True))
    ldc.append(OP("sync", lambda e: e.dma_start(
        out=cw_fm.rearrange("p (c k) -> p c k", c=32), in_=cw_d.rearrange("(c p) k -> p c k", p=128), allow_slow_non_contiguous=True), dmainc=True))
    ldc.append(OP("sync", lambda e: e.dma_start(
        out=cb_fm[:, :], in_=cb_d.rearrange("(c p) -> p c", p=128), allow_slow_non_contiguous=True), dmainc=True))
    ldwh = []
    for i in range(8):
        ldwh.append(OP("sync", lambda e, i=i: e.dma_start(
            out=arf[:, 1024 * i:1024 * (i + 1)],
            in_=w_h_d[128 * i:128 * (i + 1), :]), dmainc=True))
    t_iota = OP("gpsimd", lambda e: e.iota(idn[:, :], [[1, 128]], channel_multiplier=-1,
                                           allow_small_or_imprecise_dtypes=True))
    for fn in (lambda e: e.memset(zrow[:, :], 0.0),
               lambda e: e.memset(ones_col[:, :], 1.0),
               lambda e: e.memset(ones_rf[:, :], 1.0),
               lambda e: e.memset(ones_rb[:, :], 1.0),
               lambda e: e.memset(hcur[:, :], 0.0),
               lambda e: e.memset(pair[:, :], 0.0),
               lambda e: e.memset(scs[:, :], 0.0),
               lambda e: e.memset(p_y[:, :], 0.0),
               lambda e: e.memset(p_t0[:, :], 0.0),
               lambda e: e.memset(p_t1[:, :], 0.0),
               lambda e: e.memset(fmat[:, :], 0.0)):
        fz = OP("vector", fn)
    for j in range(4):
        fz = OP("vector", lambda e, j=j: e.memset(fmat[32 * j:32 * j + 2, j:j + 1], 1.0),
                after=(fz,))
    flush()

    t1 = OP("vector", lambda e: e.tensor_scalar(out=idn[:, :], in0=idn[:, :],
                                                scalar1=0.0, scalar2=None, op0=ALU.is_equal))
    OP("vector", lambda e: e.tensor_copy(out=idnb[:, :], in_=idn[:, :]), after=(t1,))
    OP("vector", lambda e: e.tensor_tensor(out=zb_fm[:, :], in0=zb_fm[:, :],
                                           in1=bh_fm[:, :], op=ALU.add))
    m1 = OP("tensor", lambda e: e.matmul(p_c[:, 0:512], ones_rf[0:1, :], lnw_bc[0:1, 0:512],
                                         start=True, stop=True))
    m2 = OP("tensor", lambda e: e.matmul(p_b[:, 0:512], ones_rf[0:1, :], lnw_bc[0:1, 512:1024],
                                         start=True, stop=True))
    m3 = OP("tensor", lambda e: e.matmul(p_a[:, 0:512], ones_rf[0:1, :], lnb_bc[0:1, 0:512],
                                         start=True, stop=True))
    m4 = OP("tensor", lambda e: e.matmul(p_t0[:, 0:512], ones_rf[0:1, :], lnb_bc[0:1, 512:1024],
                                         start=True, stop=True))
    OP("scalar", lambda e: e.copy(out=lnw_bc[:, 0:512], in_=p_c[:, 0:512]), after=(m1,))
    OP("scalar", lambda e: e.copy(out=lnw_bc[:, 512:1024], in_=p_b[:, 0:512]), after=(m2,))
    OP("scalar", lambda e: e.copy(out=lnb_bc[:, 0:512], in_=p_a[:, 0:512]), after=(m3,))
    OP("scalar", lambda e: e.copy(out=lnb_bc[:, 512:1024], in_=p_t0[:, 0:512]), after=(m4,))
    flush()

    OP("vector", lambda e: e.memset(p_t0[:, :], 0.0))
    OP("vector", lambda e: e.memset(p_a[:, :], 0.0))
    OP("vector", lambda e: e.memset(p_b[:, :], 0.0))
    OP("vector", lambda e: e.memset(p_c[:, :], 0.0))
    OP("vector", lambda e: e.memset(mrow[0:1, :], 0.0))
    for c in range(8):
        OP("gpsimd", lambda e, c=c: e.dma_start(
            out=xq_scr[:, c * (S + 2):c * (S + 2) + 1], in_=zrow[:, 0:1], allow_slow_non_contiguous=True), dmainc=True)
        OP("gpsimd", lambda e, c=c: e.dma_start(
            out=xq_scr[:, c * (S + 2) + S + 1:c * (S + 2) + S + 2], in_=zrow[:, 0:1], allow_slow_non_contiguous=True), dmainc=True)
    OP("gpsimd", lambda e: e.dma_start(out=m1_scr[0:1, :], in_=scs[0:1, 0:1]), dmainc=True)
    OP("gpsimd", lambda e: e.dma_start(out=m1_scr[S + 1:S + 2, :], in_=scs[0:1, 0:1]), dmainc=True)
    flush()

    def split_hl(src_view, n, tmp0, deps=()):
        h = OP("vector", lambda e: e.tensor_copy(out=bfa[:, 0:n], in_=src_view), after=deps)
        u = OP("vector", lambda e: e.tensor_copy(out=gen[:, tmp0:tmp0 + n], in_=bfa[:, 0:n]),
               after=(h,))
        l0 = OP("vector", lambda e: e.tensor_tensor(out=gen[:, tmp0:tmp0 + n], in0=src_view,
                                                    in1=gen[:, tmp0:tmp0 + n], op=ALU.subtract),
                after=(u,))
        l = OP("vector", lambda e: e.tensor_copy(out=bfa[:, 4096:4096 + n],
                                                 in_=gen[:, tmp0:tmp0 + n]), after=(l0,))
        return h, l

    for rnd in range(2):
        hH, hL = split_hl(arf[:, 4096 * rnd:4096 * (rnd + 1)], 4096, 4096,
                          deps=(ldwh[-1],))
        for iloc in range(4):
            i = 4 * rnd + iloc
            j, u = i // 2, i % 2
            for m in range(8):
                for half, base in ((0, 0), (1, 4096)):
                    dep = hH if half == 0 else hL
                    OP("scalar", lambda e, iloc=iloc, j=j, u=u, m=m, half=half, base=base:
                       e.dma_start(
                           out=whT[:, 8192 * half + 256 * (4 * m + j) + 128 * u:
                                   8192 * half + 256 * (4 * m + j) + 128 * (u + 1)],
                           in_=bfa[:, base + 1024 * iloc + dm0(m):
                                   base + 1024 * iloc + dm0(m) + 128],
                           transpose=True), after=(dep, hL), dmainc=True)
        flush()

    for rnd in range(4):
        lds = []
        for iloc in range(4):
            i = 4 * rnd + iloc
            lds.append(OP("sync", lambda e, iloc=iloc, i=i: e.dma_start(
                out=gen[:, 1024 * iloc:1024 * (iloc + 1)],
                in_=w_in_d[128 * i:128 * (i + 1), :]), dmainc=True))
        hH, hL = split_hl(gen[:, 0:4096], 4096, 4096, deps=tuple(lds))
        for iloc in range(4):
            i = 4 * rnd + iloc
            for dc in range(8):
                for half, base, wb in ((0, 0, 0), (1, 4096, 16384)):
                    OP("scalar", lambda e, i=i, iloc=iloc, dc=dc, base=base, wb=wb:
                       e.dma_start(
                           out=wbig[:, wb + 2048 * dc + 128 * i:wb + 2048 * dc + 128 * (i + 1)],
                           in_=bfa[:, base + 1024 * iloc + 128 * dc:
                                   base + 1024 * iloc + 128 * (dc + 1)],
                           transpose=True), after=(hH, hL), dmainc=True)
        flush()

    for scol, wd, ntile in ((0, w1_d, 32), (8, w2_d, 8)):
        colw = wd.shape[1] if hasattr(wd, "shape") else None
        colw = 1024 if wd is w1_d else 4096
        per = 4096 // colw
        nrnd = ntile // per
        for rnd in range(nrnd):
            lds = []
            for k in range(per):
                i = per * rnd + k
                lds.append(OP("sync", lambda e, k=k, i=i, colw=colw, wd=wd: e.dma_start(
                    out=gen[:, colw * k:colw * (k + 1)],
                    in_=wd[128 * i:128 * (i + 1), :]), dmainc=True))
            OP("vector", lambda e, scol=scol, rnd=rnd: e.reduce_sum(
                out=scs[:, scol + rnd:scol + rnd + 1], in_=gen[:, 0:4096], axis=AX.X,
                apply_absolute_value=True), after=tuple(lds))
            flush()
    mm = OP("tensor", lambda e: e.matmul(p_c[0:1, 0:16], ones_col[:, 0:1], scs[:, 0:16],
                                         start=True, stop=True))
    v0 = OP("vector", lambda e: e.tensor_copy(out=vrow[0:1, 0:16], in_=p_c[0:1, 0:16]),
            after=(mm,))
    v1 = OP("vector", lambda e: e.reduce_sum(out=vrow[0:1, 0:1], in_=vrow[0:1, 0:8], axis=AX.X),
            after=(v0,))
    v2 = OP("vector", lambda e: e.reduce_sum(out=vrow[0:1, 1:2], in_=vrow[0:1, 8:16], axis=AX.X),
            after=(v0, v1))
    v3 = OP("vector", lambda e: e.tensor_scalar(out=vrow[0:1, 2:3], in0=vrow[0:1, 0:1],
                                                scalar1=1.0 / (DH * D), scalar2=1e-5,
                                                op0=ALU.mult, op1=ALU.max), after=(v1,))
    v4 = OP("vector", lambda e: e.tensor_scalar(out=vrow[0:1, 3:4], in0=vrow[0:1, 1:2],
                                                scalar1=1.0 / (DH * D), scalar2=1e-5,
                                                op0=ALU.mult, op1=ALU.max), after=(v2,))
    v5 = OP("vector", lambda e: e.tensor_scalar(out=c1_row[0:1, :], in0=ones_rf[0:1, :],
                                                scalar1=vrow[0:1, 2:3], scalar2=1.0 / 127.0,
                                                op0=ALU.mult, op1=ALU.mult), after=(v3,))
    v6 = OP("vector", lambda e: e.tensor_scalar(out=vrow[0:1, 4:5], in0=vrow[0:1, 2:3],
                                                scalar1=0.5, scalar2=None, op0=ALU.mult),
            after=(v3,))
    v7 = OP("vector", lambda e: e.tensor_scalar(out=vrow[0:1, 5:6], in0=vrow[0:1, 3:4],
                                                scalar1=0.5, scalar2=None, op0=ALU.mult),
            after=(v4,))
    v8 = OP("vector", lambda e: e.tensor_scalar(out=vrow[0:1, 6:7], in0=vrow[0:1, 3:4],
                                                scalar1=1.0 / 127.0, scalar2=None, op0=ALU.mult),
            after=(v4,))
    b1 = OP("tensor", lambda e: e.matmul(p_c[:, 32:33], ones_rf[0:1, :], vrow[0:1, 4:5],
                                         start=True, stop=True), after=(v6,))
    t_thr1 = OP("vector", lambda e: e.tensor_copy(out=thr1[:, :], in_=p_c[:, 32:33]), after=(b1,))
    b2 = OP("tensor", lambda e: e.matmul(p_c[:, 36:37], ones_rf[0:1, :], vrow[0:1, 5:6],
                                         start=True, stop=True), after=(v7, t_thr1))
    t_thr2 = OP("vector", lambda e: e.tensor_copy(out=thr2[:, :], in_=p_c[:, 36:37]), after=(b2,))
    b3 = OP("tensor", lambda e: e.matmul(p_c[:, 40:41], ones_rf[0:1, :], vrow[0:1, 6:7],
                                         start=True, stop=True), after=(v8, t_thr2))
    t_c2 = OP("vector", lambda e: e.tensor_copy(out=c2_col[:, :], in_=p_c[:, 40:41]), after=(b3,))
    if _dbg:
        dbg_scr = nc.dram_tensor("dbg_scr", [128, 64], F32, kind="ExternalOutput")
        OP("gpsimd", lambda e: e.dma_start(out=dbg_scr[:, 0:16], in_=scs[:, :], allow_slow_non_contiguous=True),
           after=(t_thr1,), dmainc=True)
        OP("gpsimd", lambda e: e.dma_start(out=dbg_scr[0:1, 16:32], in_=vrow[0:1, :], allow_slow_non_contiguous=True),
           after=(t_thr1, v5, v8), dmainc=True)
        OP("gpsimd", lambda e: e.dma_start(out=dbg_scr[:, 32:33], in_=thr1[:, :], allow_slow_non_contiguous=True),
           after=(t_thr1,), dmainc=True)
        OP("gpsimd", lambda e: e.dma_start(out=dbg_scr[:, 33:34], in_=thr2[:, :], allow_slow_non_contiguous=True),
           after=(t_thr2,), dmainc=True)
        OP("gpsimd", lambda e: e.dma_start(out=dbg_scr[:, 34:35], in_=c2_col[:, :], allow_slow_non_contiguous=True),
           after=(t_c2,), dmainc=True)
    flush()

    ln_state = {"prev": None}

    def ln_plain(src, dst, t_tmp, sq_tmp, ms, deps=()):
        if ln_state["prev"] is not None:
            deps = tuple(deps) + (ln_state["prev"],)
        a1 = OP("vector", lambda e: e.reduce_sum(out=mcol[:, ms:ms + 1], in_=src, axis=AX.X),
                after=deps)
        a2 = OP("vector", lambda e: e.tensor_scalar(out=mcol[:, ms:ms + 1], in0=mcol[:, ms:ms + 1],
                                                    scalar1=1.0 / D, scalar2=None, op0=ALU.mult),
                after=(a1,))
        a3 = OP("vector", lambda e: e.tensor_scalar(out=t_tmp, in0=src,
                                                    scalar1=mcol[:, ms:ms + 1], scalar2=None,
                                                    op0=ALU.subtract), after=(a2,))
        a4 = OP("vector", lambda e: e.tensor_tensor(out=sq_tmp, in0=t_tmp, in1=t_tmp,
                                                    op=ALU.mult), after=(a3,))
        a5 = OP("vector", lambda e: e.reduce_sum(out=mcol[:, ms + 1:ms + 2], in_=sq_tmp,
                                                 axis=AX.X), after=(a4,))
        a6p = OP("vector", lambda e: e.tensor_scalar(out=mcol[:, ms + 1:ms + 2],
                                                     in0=mcol[:, ms + 1:ms + 2],
                                                     scalar1=1.0 / D, scalar2=1e-5,
                                                     op0=ALU.mult, op1=ALU.add), after=(a5,))
        a6a = OP("scalar", lambda e: e.activation(out=mcol[:, ms + 1:ms + 2],
                                                  in_=mcol[:, ms + 1:ms + 2], func=AF.Sqrt),
                 after=(a6p,))
        a6 = OP("vector", lambda e: e.reciprocal(out=mcol[:, ms + 1:ms + 2],
                                                 in_=mcol[:, ms + 1:ms + 2]), after=(a6a,))
        a7 = OP("vector", lambda e: e.tensor_scalar(out=dst, in0=t_tmp,
                                                    scalar1=mcol[:, ms + 1:ms + 2], scalar2=None,
                                                    op0=ALU.mult), after=(a6, a3))
        ln_state["prev"] = a7
        return a7

    for r in range(NT1):
        ldx = []
        for g in range(NG1):
            ldx.append(OP("sync", lambda e, g=g, r=r: e.dma_start(
                out=arf[:, 1024 * g:1024 * (g + 1)],
                in_=x_in[PT1 * r + 128 * g:PT1 * r + 128 * (g + 1), :]), dmainc=True))
        trT = []
        for g in range(NG1):
            x0v = arf[:, 1024 * g:1024 * (g + 1)]
            n1tok = ln_plain(x0v, gen[:, 1024:2048], gen[:, 0:1024], gen[:, 4096:5120], 0,
                             deps=(ldx[g],))
            xntok = ln_plain(gen[:, 1024:2048], gen[:, 2048:3072], gen[:, 3072:4096],
                             gen[:, 4096:5120], 2, deps=(n1tok,))
            aff1 = OP("vector", lambda e: e.tensor_tensor(out=gen[:, 2048:3072],
                                                          in0=gen[:, 2048:3072],
                                                          in1=lnw_bc[:, :], op=ALU.mult),
                      after=(xntok,))
            aff2 = OP("vector", lambda e: e.tensor_tensor(out=gen[:, 2048:3072],
                                                          in0=gen[:, 2048:3072],
                                                          in1=lnb_bc[:, :], op=ALU.add),
                      after=(aff1,))
            h1 = OP("vector", lambda e, g=g: e.tensor_copy(
                out=bfa[:, 1024 * g:1024 * (g + 1)], in_=gen[:, 2048:3072]), after=(aff2,))
            h2 = OP("vector", lambda e, g=g: e.tensor_copy(out=gen[:, 4096:5120],
                                                           in_=bfa[:, 1024 * g:1024 * (g + 1)]),
                    after=(h1,))
            h3 = OP("vector", lambda e: e.tensor_tensor(out=gen[:, 5120:6144],
                                                        in0=gen[:, 2048:3072],
                                                        in1=gen[:, 4096:5120], op=ALU.subtract),
                    after=(h2,))
            h4 = OP("vector", lambda e, g=g: e.tensor_copy(
                out=bfa[:, 4096 + 1024 * g:4096 + 1024 * (g + 1)], in_=gen[:, 5120:6144]),
                after=(h3,))
            for dc in range(8):
                for half, base, ob, dep in ((0, 0, 0, h1), (1, 4096, 4096, h4)):
                    trT.append(OP("scalar", lambda e, g=g, dc=dc, base=base, ob=ob:
                                  e.dma_start(
                                      out=bfb[:, ob + PT1 * dc + 128 * g:
                                              ob + PT1 * dc + 128 * (g + 1)],
                                      in_=bfa[:, base + 1024 * g + 128 * dc:
                                              base + 1024 * g + 128 * (dc + 1)],
                                      transpose=True), after=(dep, h4), dmainc=True))
        prev_sp = {0: None, 1: None}
        prev_ev = {0: None, 1: None}
        global_state.setdefault("za_last", ())
        for ec in range(16):
            ps = p_a if ec % 2 == 0 else p_b
            deps = tuple(trT) + ((prev_ev[ec % 2],) if prev_ev[ec % 2] else ())
            mmt = None
            for dc in range(8):
                for wofs, xofs in ((0, 0), (0, 4096), (16384, 0)):
                    first = (dc == 0 and wofs == 0 and xofs == 0)
                    last = (dc == 7 and wofs == 16384)
                    mmt = OP("tensor", lambda e, ec=ec, dc=dc, wofs=wofs, xofs=xofs,
                             first=first, last=last, ps=ps:
                             e.matmul(ps[:, 0:PT1],
                                      wbig[:, wofs + 2048 * dc + 128 * ec:
                                           wofs + 2048 * dc + 128 * (ec + 1)],
                                      bfb[:, xofs + PT1 * dc:xofs + PT1 * (dc + 1)],
                                      start=first, stop=last),
                             after=deps if first else ())
            wdep = (prev_sp[ec % 2],) if prev_sp[ec % 2] else ()
            if ec < 8:
                ev = OP("scalar", lambda e, ec=ec, ps=ps: e.activation(
                    out=gen[:, 6144 + 1024 * (ec % 2):6144 + 1024 * (ec % 2) + PT1],
                    in_=ps[:, 0:PT1], func=AF.Sigmoid, bias=bu_fm[:, ec:ec + 1], scale=1.0),
                    after=(mmt,) + wdep)
            else:
                ev = OP("vector", lambda e, ec=ec, ps=ps: e.tensor_scalar(
                    out=gen[:, 6144 + 1024 * (ec % 2):6144 + 1024 * (ec % 2) + PT1],
                    in0=ps[:, 0:PT1], scalar1=zb_fm[:, ec - 8:ec - 7], scalar2=None,
                    op0=ALU.add), after=(mmt,) + wdep)
            prev_ev[ec % 2] = ev
            scr = a_scr if ec < 8 else z_scr
            c = (ec % 8 % 2) * 4 + (ec % 8) // 2
            prev_sp[ec % 2] = OP(
                "gpsimd" if ec % 2 == 0 else "sync",
                lambda e, scr=scr, c=c, r=r, ec=ec: e.dma_start(
                    out=scr[:, S * c + PT1 * r:S * c + PT1 * (r + 1)],
                    in_=gen[:, 6144 + 1024 * (ec % 2):6144 + 1024 * (ec % 2) + PT1]),
                after=(ev,), dmainc=True)
            if ec >= 14:
                global_state["za_last"] = tuple(global_state["za_last"]) + (prev_sp[ec % 2],)
        flush()

    import os as _os
    if _os.environ.get("KSTOP") == "p1":
        return nc

    for rnd in range(2):
        lds = []
        for iloc in range(4):
            i = 4 * rnd + iloc
            lds.append(OP("sync", lambda e, iloc=iloc, i=i: e.dma_start(
                out=gen[:, 1024 * iloc:1024 * (iloc + 1)],
                in_=w_out_d[128 * i:128 * (i + 1), :]), dmainc=True))
        hH, hL = split_hl(gen[:, 0:4096], 4096, 4096, deps=tuple(lds))
        for iloc in range(4):
            i = 4 * rnd + iloc
            for dc in range(8):
                for half, base, wb in ((0, 0, 0), (1, 4096, 8192)):
                    OP("scalar", lambda e, i=i, iloc=iloc, dc=dc, base=base, wb=wb:
                       e.dma_start(
                           out=wbig[:, wb + 1024 * dc + 128 * i:wb + 1024 * dc + 128 * (i + 1)],
                           in_=bfa[:, base + 1024 * iloc + 128 * dc:
                                   base + 1024 * iloc + 128 * (dc + 1)],
                           transpose=True), after=(hH, hL), dmainc=True)
        flush()

    bl = OP("sync", lambda e: e.dma_start(out=gen[0:1, 0:1024], in_=b_out_d[None, :]),
            dmainc=True)
    u1 = OP("vector", lambda e: e.tensor_copy(out=bo_bf[0:1, 0:1024], in_=gen[0:1, 0:1024]),
            after=(bl,))
    u2 = OP("vector", lambda e: e.tensor_copy(out=gen[0:1, 1024:2048], in_=bo_bf[0:1, 0:1024]),
            after=(u1,))
    u3 = OP("vector", lambda e: e.tensor_tensor(out=gen[0:1, 2048:3072], in0=gen[0:1, 0:1024],
                                                in1=gen[0:1, 1024:2048], op=ALU.subtract),
            after=(u2,))
    OP("vector", lambda e: e.tensor_copy(out=bo_bf[0:1, 1024:2048], in_=gen[0:1, 2048:3072]),
       after=(u3,))
    flush()

    whH = lambda m, j: whT[:, 256 * (4 * m + j):256 * (4 * m + j) + 256]
    whL = lambda m, j: whT[:, 8192 + 256 * (4 * m + j):8192 + 256 * (4 * m + j) + 256]
    zr = arf[:, 0:8 * PSC]
    ar = arf[:, 2048:2048 + 8 * PSC]
    hr = arf[:, 4096:4096 + 8 * PSC]
    zh = bfa[:, 0:8 * PSC]
    zl2 = bfa[:, 2048:2048 + 8 * PSC]

    def slot(ring, tl):
        return ring[:, tl:tl + 1 + 7 * PSC:PSC]

    prev_split = None
    prev_bulk = None
    prev_zmm = None
    prev_hs_store = None
    prev_hw = None
    for r in range(NSC):
        dep_r0 = tuple(global_state["za_last"]) if r == 0 else ()
        zl = OP("sync", lambda e, r=r: e.dma_start(
            out=zr.rearrange("p (c t) -> p c t", c=8), in_=scr_view(z_scr, PSC * r, PSC)),
            after=((prev_bulk,) if prev_bulk else ()) + dep_r0, dmainc=True)
        al = OP("sync", lambda e, r=r: e.dma_start(
            out=ar.rearrange("p (c t) -> p c t", c=8), in_=scr_view(a_scr, PSC * r, PSC)),
            after=((prev_split,) if prev_split else ()) + dep_r0, dmainc=True)
        b1 = OP("vector", lambda e: e.tensor_copy(out=zh[:, :], in_=zr[:, :]),
                after=(zl,) + ((prev_zmm,) if prev_zmm else ()))
        b2 = OP("vector", lambda e: e.tensor_copy(out=gen[:, 0:8 * PSC], in_=zh[:, :]),
                after=(b1,))
        b3 = OP("vector", lambda e: e.tensor_tensor(out=gen[:, 0:8 * PSC], in0=zr[:, :],
                                                    in1=gen[:, 0:8 * PSC],
                                                    op=ALU.subtract), after=(b2,))
        b4 = OP("vector", lambda e: e.tensor_copy(out=zl2[:, :], in_=gen[:, 0:8 * PSC]),
                after=(b3,))
        prev_bulk = b4
        for tloc in range(PSC):
            t = PSC * r + tloc
            mmt = None
            dep0 = (prev_split,) if prev_split else ()
            for m in range(8):
                for j in range(4):
                    pr = pair[:, m:m + 9:8]
                    mmt = OP("tensor", lambda e, m=m, j=j, pr=pr: e.matmul(
                        p_y[32 * j:32 * j + 2, 0:256], pr, whH(m, j),
                        start=False, stop=False, tile_position=(0, 32 * j),
                        skip_group_check=True),
                        after=dep0 if (m == 0 and j == 0) else ())
                    mmt = OP("tensor", lambda e, m=m, j=j, pr=pr: e.matmul(
                        p_y[32 * j:32 * j + 2, 0:256], pr, whL(m, j),
                        start=False, stop=False, tile_position=(0, 32 * j),
                        skip_group_check=True))
            ev1 = OP("vector", lambda e: e.tensor_copy(out=Ssc[:, 0:128],
                                                       in_=p_y[0:128, 0:128]),
                     after=(mmt,))
            ev2 = OP("vector", lambda e: e.tensor_copy(out=Ssc[:, 128:256],
                                                       in_=p_y[0:128, 128:256]))
            f1 = OP("tensor", lambda e, t=t: e.matmul(p_t[t % 2][0:128, 0:4],
                                                      Ssc[0:128, 0:128], fmat[:, :],
                                                      start=True, stop=False), after=(ev1,))
            f2 = OP("tensor", lambda e, t=t: e.matmul(p_t[t % 2][0:128, 4:8],
                                                      Ssc[0:128, 128:256], fmat[:, :],
                                                      start=False, stop=False), after=(ev2,))
            z1 = OP("tensor", lambda e, t=t, tloc=tloc: e.matmul(
                p_t[t % 2][0:128, 0:8], idnb[:, :], slot(zh, tloc),
                start=False, stop=False), after=(b4,) if tloc == 0 else ())
            z2 = OP("tensor", lambda e, t=t, tloc=tloc: e.matmul(
                p_t[t % 2][0:128, 0:8], idnb[:, :], slot(zl2, tloc),
                start=False, stop=True))
            prev_zmm = z2
            th = OP("scalar", lambda e, t=t: e.activation(
                out=cand[:, :], in_=p_t[t % 2][0:128, 0:8], func=AF.Tanh), after=(z2,))
            hprev = hcur[:, :] if t == 0 else slot(hr, (tloc - 1) % PSC)
            s1 = OP("vector", lambda e, hprev=hprev: e.tensor_tensor(
                out=dsub2[:, :], in0=cand[:, :], in1=hprev, op=ALU.subtract),
                after=(th,) + ((prev_hw,) if prev_hw else ()))
            s2 = OP("vector", lambda e, tloc=tloc: e.tensor_tensor(
                out=dfull[:, :], in0=dsub2[:, :], in1=slot(ar, tloc), op=ALU.mult),
                after=(s1, al))
            p1 = OP("scalar", lambda e: e.activation(out=pair[:, 0:8], in_=dfull[:, :],
                                                     func=AF.Copy), after=(s2,))
            p3 = OP("vector", lambda e: e.tensor_tensor(out=dsub[:, :], in0=dfull[:, :],
                                                        in1=pair[:, 0:8], op=ALU.subtract),
                    after=(p1,))
            p4 = OP("scalar", lambda e: e.activation(out=pair[:, 8:16], in_=dsub[:, :],
                                                     func=AF.Copy), after=(p3,))
            prev_split = p4
            prev_hw = OP("vector", lambda e, tloc=tloc, hprev=hprev: e.tensor_tensor(
                out=slot(hr, tloc), in0=hprev, in1=dfull[:, :], op=ALU.add),
                after=(s2,) + ((prev_hs_store,) if (tloc == 0 and prev_hs_store) else ()))
        prev_hs_store = OP("gpsimd", lambda e, r=r: e.dma_start(
            out=scr_view(hs_scr, PSC * r, PSC), in_=hr.rearrange("p (c t) -> p c t", c=8)),
            after=(prev_split, prev_hw), dmainc=True)
        sch.emit()
    flush()

    global_state["hs_last"] = prev_hs_store
    import os as _os
    if _os.environ.get("KSTOP") != "p2":
        build_rest(nc, sch, locals())
    return nc


def build_rest(nc, sch, env):
    OP = sch.op
    BAR = nc.all_engine_barrier

    def flush():
        sch.emit()
        sch.fence()
        BAR()

    S = env["S"]
    PT1, PT4 = env["PT1"], env["PT4"]
    NT1, NT4 = env["NT1"], env["NT4"]
    NG1, NG4 = env["NG1"], env["NG4"]
    names = ("arf gen bfa bfb whT wbig it32 lnw_bc lnb_bc bo_bf cw_fm cb_fm "
             "idn ones_col ones_rf ones_rb c1_row c2_col thr1 thr2 scs vrow "
             "mcol mrow dq1b s2b mx_tm dq2_tm mxt tstage p_a p_b p_c p_t0 "
             "p_t1 p_y x_in w1_d w2_d y_out z_scr a_scr hs_scr y1_scr xq_scr "
             "xq2_scr m1_scr dq2_scr").split()
    (arf, gen, bfa, bfb, whT, wbig, it32, lnw_bc, lnb_bc, bo_bf, cw_fm, cb_fm,
     idn, ones_col, ones_rf, ones_rb, c1_row, c2_col, thr1, thr2, scs, vrow,
     mcol, mrow, dq1b, s2b, mx_tm, dq2_tm, mxt, tstage, p_a, p_b, p_c, p_t0,
     p_t1, p_y, x_in, w1_d, w2_d, y_out, z_scr, a_scr, hs_scr, y1_scr, xq_scr,
     xq2_scr, m1_scr, dq2_scr) = (env[k] for k in names)
    ln_plain = env["ln_plain"]
    scr_view = env["scr_view"]
    global_state = env["global_state"]

    def nch(m):
        return dm0(m) // 128

    for r in range(NT1):
        ldx = []
        for g in range(NG1):
            ldx.append(OP("sync", lambda e, g=g, r=r: e.dma_start(
                out=arf[:, 1024 * g:1024 * (g + 1)],
                in_=x_in[PT1 * r + 128 * g:PT1 * r + 128 * (g + 1), :]), dmainc=True))
        hsl = OP("sync", lambda e, r=r: e.dma_start(
            out=arf[:, 4096:4096 + 8 * PT1].rearrange("p (c t) -> p c t", c=8),
            in_=scr_view(hs_scr, PT1 * r, PT1)),
            after=(global_state["hs_last"],) if r == 0 else (), dmainc=True)
        hH = OP("vector", lambda e: e.tensor_copy(out=bfb[:, 0:4096],
                                                  in_=arf[:, 4096:8192]), after=(hsl,))
        hU = OP("vector", lambda e: e.tensor_copy(out=gen[:, 4096:8192], in_=bfb[:, 0:4096]),
                after=(hH,))
        hS = OP("vector", lambda e: e.tensor_tensor(out=gen[:, 4096:8192],
                                                    in0=arf[:, 4096:8192],
                                                    in1=gen[:, 4096:8192], op=ALU.subtract),
                after=(hU,))
        hL = OP("vector", lambda e: e.tensor_copy(out=bfb[:, 4096:8192],
                                                  in_=gen[:, 4096:8192]), after=(hS,))
        prev_q = None
        for g in range(NG1):
            x0v = arf[:, 1024 * g:1024 * (g + 1)]
            mm_done = []
            for eh, ps in ((0, p_a), (1, p_b)):
                mmt = None
                first = True
                for m in range(8):
                    for sofs, wofs in ((0, 0), (0, 8192), (4096, 0)):
                        fdeps = (hH, hL) if first else ()
                        if first and prev_q is not None:
                            fdeps = fdeps + (prev_q,)
                        mmt = OP("tensor", lambda e, g=g, m=m, sofs=sofs, wofs=wofs,
                                 eh=eh, ps=ps, first=first: e.matmul(
                                     ps[:, 0:512],
                                     bfb[:, sofs + PT1 * m + 128 * g:
                                         sofs + PT1 * m + 128 * (g + 1)],
                                     wbig[:, wofs + 1024 * nch(m) + 512 * eh:
                                          wofs + 1024 * nch(m) + 512 * (eh + 1)],
                                     start=first, stop=False), after=fdeps)
                        first = False
                mmt = OP("tensor", lambda e, eh=eh, ps=ps: e.matmul(
                    ps[:, 0:512], ones_rb[0:1, :], bo_bf[0:1, 512 * eh:512 * (eh + 1)],
                    start=False, stop=False))
                mmt = OP("tensor", lambda e, eh=eh, ps=ps: e.matmul(
                    ps[:, 0:512], ones_rb[0:1, :],
                    bo_bf[0:1, 1024 + 512 * eh:1024 + 512 * (eh + 1)],
                    start=False, stop=True))
                mm_done.append(mmt)
            n1tok = ln_plain(x0v, gen[:, 1024:2048], gen[:, 0:1024], gen[:, 2048:3072], 0,
                             deps=(ldx[g],))
            aA = OP("vector", lambda e: e.tensor_tensor(out=gen[:, 2048:2560],
                                                        in0=p_a[:, 0:512],
                                                        in1=gen[:, 1024:1536], op=ALU.add),
                    after=(mm_done[0], n1tok))
            aB = OP("vector", lambda e: e.tensor_tensor(out=gen[:, 2560:3072],
                                                        in0=p_b[:, 0:512],
                                                        in1=gen[:, 1536:2048], op=ALU.add),
                    after=(mm_done[1], n1tok))
            ln3 = ln_plain(gen[:, 2048:3072], gen[:, 3072:4096], gen[:, 0:1024],
                           gen[:, 1024:2048], 2, deps=(aA, aB))
            y1v = OP("vector", lambda e, x0v=x0v: e.tensor_tensor(
                out=gen[:, 2048:3072], in0=x0v,
                in1=gen[:, 3072:4096], op=ALU.add), after=(ln3,))
            global_state["y1_last"] = OP("gpsimd", lambda e, r=r, g=g: e.dma_start(
                out=y1_scr[PT1 * r + 128 * g:PT1 * r + 128 * (g + 1), :],
                in_=gen[:, 2048:3072]), after=(y1v,), dmainc=True)
            n2tok = ln_plain(gen[:, 2048:3072], gen[:, 3072:4096], gen[:, 0:1024],
                             gen[:, 1024:2048], 4, deps=(y1v,))
            q1 = OP("vector", lambda e: e.reduce_max(out=mcol[:, 6:7], in_=gen[:, 3072:4096],
                                                     axis=AX.X, apply_absolute_value=True),
                    after=(n2tok,))
            q2 = OP("vector", lambda e: e.tensor_scalar(out=mcol[:, 6:7], in0=mcol[:, 6:7],
                                                        scalar1=1e-5, scalar2=None,
                                                        op0=ALU.max), after=(q1,))
            global_state["m1_last"] = OP("gpsimd", lambda e, r=r, g=g: e.dma_start(
                out=m1_scr[1 + PT1 * r + 128 * g:1 + PT1 * r + 128 * (g + 1), :],
                in_=mcol[:, 6:7]), after=(q2,), dmainc=True)
            q3 = OP("vector", lambda e: e.reciprocal(out=mcol[:, 7:8], in_=mcol[:, 6:7]),
                    after=(q2,))
            q4 = OP("vector", lambda e: e.tensor_scalar(out=gen[:, 0:1024],
                                                        in0=gen[:, 3072:4096],
                                                        scalar1=mcol[:, 7:8], scalar2=127.0,
                                                        op0=ALU.mult, op1=ALU.mult),
                    after=(q3,))
            q5a = OP("gpsimd", lambda e: e.tensor_copy(out=it32[:, :], in_=gen[:, 0:512]),
                     after=(q4,))
            q5b = OP("gpsimd", lambda e: e.tensor_copy(out=gen[:, 1024:1536], in_=it32[:, :]),
                     after=(q5a,))
            q5c = OP("gpsimd", lambda e: e.tensor_copy(out=it32[:, :], in_=gen[:, 512:1024]),
                     after=(q5b,))
            q6 = OP("gpsimd", lambda e: e.tensor_copy(out=gen[:, 1536:2048], in_=it32[:, :]),
                    after=(q5c,))
            q7 = OP("vector", lambda e: e.tensor_tensor(out=gen[:, 2048:3072],
                                                        in0=gen[:, 0:1024],
                                                        in1=gen[:, 1024:2048],
                                                        op=ALU.subtract),
                    after=(q6, global_state["y1_last"]))
            q8 = OP("vector", lambda e: e.scalar_tensor_tensor(out=gen[:, 3072:4096],
                                                               in0=gen[:, 2048:3072],
                                                               scalar=0.5,
                                                               in1=gen[:, 1024:2048],
                                                               op0=ALU.is_gt, op1=ALU.add),
                    after=(q7,))
            q9 = OP("vector", lambda e: e.tensor_scalar(out=gen[:, 0:1024],
                                                        in0=gen[:, 2048:3072],
                                                        scalar1=-0.5, scalar2=None,
                                                        op0=ALU.is_lt), after=(q7,))
            qa_ = OP("vector", lambda e: e.scalar_tensor_tensor(out=gen[:, 1024:2048],
                                                                in0=gen[:, 0:1024],
                                                                scalar=-1.0,
                                                                in1=gen[:, 3072:4096],
                                                                op0=ALU.mult, op1=ALU.add),
                     after=(q8, q9))
            qb_ = OP("vector", lambda e: e.tensor_copy(out=bfa[:, 0:1024],
                                                       in_=gen[:, 1024:2048]), after=(qa_,))
            prev_q = qb_
            prev_xq = global_state.get("xq_last")
            trs = None
            for dc in range(8):
                trs = OP("scalar", lambda e, dc=dc: e.dma_start(
                    out=tstage[:, 128 * dc:128 * (dc + 1)],
                    in_=bfa[:, 128 * dc:128 * (dc + 1)], transpose=True),
                    after=(qb_,) + ((prev_xq,) if prev_xq else ()), dmainc=True)
            global_state["xq_last"] = OP(
                "gpsimd", lambda e, r=r, g=g: e.dma_start(
                    out=xq_scr.rearrange("p (c t) -> p c t", c=8)[
                        :, :, 1 + PT1 * r + 128 * g:1 + PT1 * r + 128 * (g + 1)],
                    in_=tstage.rearrange("p (c t) -> p c t", c=8)),
                after=(trs,), dmainc=True)
        flush()

    for rnd in range(DH // 512):
        lds = []
        for k in range(4):
            i = 4 * rnd + k
            lds.append(OP("sync", lambda e, k=k, i=i: e.dma_start(
                out=gen[:, 1024 * k:1024 * (k + 1)],
                in_=w1_d[128 * i:128 * (i + 1), :]), dmainc=True))
        g1a = OP("vector", lambda e: e.scalar_tensor_tensor(
            out=gen[:, 4096:8192], in0=gen[:, 0:4096], scalar=-1.0,
            in1=gen[:, 0:4096], op0=ALU.mult, op1=ALU.max), after=tuple(lds))
        g1 = OP("vector", lambda e: e.tensor_scalar(out=gen[:, 4096:8192],
                                                    in0=gen[:, 4096:8192],
                                                    scalar1=thr1[:, 0:1], scalar2=None,
                                                    op0=ALU.is_ge), after=(g1a,))
        g2 = OP("scalar", lambda e: e.activation(out=arf[:, 0:4096], in_=gen[:, 0:4096],
                                                 func=AF.Sign), after=tuple(lds))
        g3 = OP("vector", lambda e: e.tensor_tensor(out=arf[:, 4096:8192],
                                                    in0=gen[:, 4096:8192],
                                                    in1=arf[:, 0:4096], op=ALU.mult),
                after=(g1, g2))
        g4 = OP("vector", lambda e: e.tensor_copy(out=bfa[:, 0:4096], in_=arf[:, 4096:8192]),
                after=(g3,))
        for k in range(4):
            ht = 4 * rnd + k
            for dc in range(8):
                OP("scalar", lambda e, k=k, ht=ht, dc=dc: e.dma_start(
                    out=wbig[:, 4096 * dc + 128 * ht:4096 * dc + 128 * (ht + 1)],
                    in_=bfa[:, 1024 * k + 128 * dc:1024 * k + 128 * (dc + 1)],
                    transpose=True), after=(g4,), dmainc=True)
        flush()

    for r in range(NT4):
        t0 = PT4 * r
        bdep = (global_state["xq_last"], global_state["m1_last"]) if r == 0 else ()
        xl = OP("sync", lambda e, t0=t0: e.dma_start(
            out=bfb[:, 0:8 * (PT4 + 2)].rearrange("p (c t) -> p c t", c=8),
            in_=xq_scr.rearrange("p (c t) -> p c t", c=8)[:, :, t0:t0 + PT4 + 2]),
            after=bdep, dmainc=True)
        ml = OP("sync", lambda e, t0=t0: e.dma_start(
            out=mrow[0:1, :], in_=m1_scr[t0:t0 + PT4 + 2, :].rearrange("t a -> a t"), allow_slow_non_contiguous=True),
            after=bdep, dmainc=True)
        bq = OP("tensor", lambda e: e.matmul(p_c[:, 0:PT4 + 2], c1_row[0:1, :], mrow[0:1, :],
                                             start=True, stop=True), after=(ml,))
        bqe = OP("vector", lambda e: e.tensor_copy(out=dq1b[:, :], in_=p_c[:, 0:PT4 + 2]),
                 after=(bq,))
        mxprev = OP("vector", lambda e: e.memset(mxt[:, :], 0.0))
        prev_ev = {0: None, 1: None}
        sil_ev = []
        for hc in range(32):
            ps = p_a if hc % 2 == 0 else p_b
            mmt = None
            for dc in range(8):
                fdeps = ()
                if dc == 0:
                    fdeps = (xl,)
                    if prev_ev[hc % 2] is not None:
                        fdeps = fdeps + (prev_ev[hc % 2],)
                mmt = OP("tensor", lambda e, hc=hc, dc=dc, ps=ps: e.matmul(
                    ps[:, 0:PT4 + 2],
                    wbig[:, 4096 * dc + 128 * hc:4096 * dc + 128 * (hc + 1)],
                    bfb[:, (PT4 + 2) * dc:(PT4 + 2) * (dc + 1)],
                    start=(dc == 0), stop=(dc == 7)), after=fdeps)
            dv = OP("vector", lambda e, ps=ps: e.tensor_tensor(out=gen[:, 0:PT4 + 2],
                                                               in0=ps[:, 0:PT4 + 2],
                                                               in1=dq1b[:, :], op=ALU.mult),
                    after=(mmt, bqe))
            prev_ev[hc % 2] = dv
            c1v = OP("vector", lambda e, hc=hc: e.tensor_scalar(
                out=gen[:, 1024:1024 + PT4], in0=gen[:, 1:1 + PT4],
                scalar1=cw_fm[:, 3 * hc + 1:3 * hc + 2], scalar2=cb_fm[:, hc:hc + 1],
                op0=ALU.mult, op1=ALU.add), after=(dv,))
            c2v = OP("vector", lambda e, hc=hc: e.scalar_tensor_tensor(
                out=gen[:, 2048:2048 + PT4], in0=gen[:, 0:PT4],
                scalar=cw_fm[:, 3 * hc:3 * hc + 1], in1=gen[:, 1024:1024 + PT4],
                op0=ALU.mult, op1=ALU.add), after=(c1v,))
            c3v = OP("vector", lambda e, hc=hc: e.scalar_tensor_tensor(
                out=gen[:, 1024:1024 + PT4], in0=gen[:, 2:2 + PT4],
                scalar=cw_fm[:, 3 * hc + 2:3 * hc + 3], in1=gen[:, 2048:2048 + PT4],
                op0=ALU.mult, op1=ALU.add), after=(c2v,))
            sl = OP("scalar", lambda e, hc=hc: e.activation(
                out=arf[:, PT4 * hc:PT4 * (hc + 1)], in_=gen[:, 1024:1024 + PT4],
                func=AF.Silu), after=(c3v,))
            mxa = OP("vector", lambda e, hc=hc: e.scalar_tensor_tensor(
                out=gen[:, 3072:3072 + PT4], in0=arf[:, PT4 * hc:PT4 * (hc + 1)],
                scalar=-1.0, in1=arf[:, PT4 * hc:PT4 * (hc + 1)],
                op0=ALU.mult, op1=ALU.max), after=(sl, mxprev))
            mxprev = OP("vector", lambda e: e.tensor_tensor(
                out=mxt[:, :], in0=mxt[:, :], in1=gen[:, 3072:3072 + PT4],
                op=ALU.max), after=(mxa,))
            sil_ev.append(sl)
        rds = []
        trprev = mxprev
        for g in range(NG4):
            tr = OP("tensor", lambda e, g=g: e.transpose(
                p_c[0:128, 128:256], mxt[:, 128 * g:128 * (g + 1)], idn[:, :]),
                after=(trprev, bqe))
            rd = OP("vector", lambda e, g=g: e.reduce_max(
                out=mx_tm[:, g:g + 1], in_=p_c[0:128, 128:256], axis=AX.X), after=(tr,))
            trprev = rd
            rds.append(rd)
        cl = OP("vector", lambda e: e.tensor_scalar(out=mx_tm[:, :], in0=mx_tm[:, :],
                                                    scalar1=1e-5, scalar2=None, op0=ALU.max),
                after=tuple(rds))
        dq2v = OP("vector", lambda e: e.tensor_scalar(out=dq2_tm[:, :], in0=mx_tm[:, :],
                                                      scalar1=c2_col[:, 0:1], scalar2=None,
                                                      op0=ALU.mult), after=(cl,))
        global_state["dq2_last"] = OP("gpsimd", lambda e, t0=t0: e.dma_start(
            out=dq2_scr[t0:t0 + PT4, :].rearrange("(g p) a -> p (g a)", p=128),
            in_=dq2_tm[:, :], allow_slow_non_contiguous=True), after=(dq2v,), dmainc=True)
        rc = OP("vector", lambda e: e.reciprocal(out=mx_tm[:, :], in_=mx_tm[:, :]),
                after=(cl, dq2v))
        s2toks = []
        prevrow = rc
        for g in range(NG4):
            trr = OP("tensor", lambda e, g=g: e.matmul(
                p_c[0:1, 256 + 128 * g:256 + 128 * (g + 1)], mx_tm[:, g:g + 1], idn[:, :],
                start=(g == 0), stop=(g == NG4 - 1)), after=(prevrow,))
            prevrow = trr
            s2toks.append(trr)
        s2r_ = OP("vector", lambda e: e.tensor_scalar(
            out=gen[0:1, 0:PT4], in0=p_c[0:1, 256:256 + PT4], scalar1=127.0,
            scalar2=None, op0=ALU.mult), after=tuple(s2toks))
        bq2 = OP("tensor", lambda e: e.matmul(p_c[:, 256:256 + PT4], ones_rf[0:1, :],
                                              gen[0:1, 0:PT4], start=True, stop=True),
                 after=(s2r_,))
        s2be = OP("vector", lambda e: e.tensor_copy(out=s2b[:, :], in_=p_c[:, 256:256 + PT4]),
                  after=(bq2,))
        prev_st = None
        st_hist = []
        for hc in range(32):
            u = OP("vector", lambda e, hc=hc: e.tensor_tensor(
                out=gen[:, 0:PT4], in0=arf[:, PT4 * hc:PT4 * (hc + 1)], in1=s2b[:, :],
                op=ALU.mult), after=(s2be, sil_ev[hc]) + ((prev_st,) if prev_st else ()))
            i1 = OP("gpsimd", lambda e: e.tensor_copy(out=it32[:, 0:PT4], in_=gen[:, 0:PT4]),
                    after=(u,))
            i2 = OP("gpsimd", lambda e: e.tensor_copy(out=gen[:, 1024:1024 + PT4],
                                                      in_=it32[:, 0:PT4]), after=(i1,))
            d1 = OP("vector", lambda e: e.tensor_tensor(out=gen[:, 2048:2048 + PT4],
                                                        in0=gen[:, 0:PT4],
                                                        in1=gen[:, 1024:1024 + PT4],
                                                        op=ALU.subtract), after=(i2,))
            e1 = OP("vector", lambda e: e.scalar_tensor_tensor(out=gen[:, 3072:3072 + PT4],
                                                               in0=gen[:, 2048:2048 + PT4],
                                                               scalar=0.5,
                                                               in1=gen[:, 1024:1024 + PT4],
                                                               op0=ALU.is_gt, op1=ALU.add),
                    after=(d1,))
            e2 = OP("vector", lambda e: e.tensor_scalar(out=gen[:, 0:PT4],
                                                        in0=gen[:, 2048:2048 + PT4],
                                                        scalar1=-0.5, scalar2=None,
                                                        op0=ALU.is_lt), after=(d1,))
            e3 = OP("vector", lambda e: e.scalar_tensor_tensor(out=gen[:, 1024:1024 + PT4],
                                                               in0=gen[:, 0:PT4],
                                                               scalar=-1.0,
                                                               in1=gen[:, 3072:3072 + PT4],
                                                               op0=ALU.mult, op1=ALU.add),
                    after=(e1, e2))
            e4deps = (e3,)
            if len(st_hist) >= 8:
                e4deps = e4deps + (st_hist[-8],)
            e4 = OP("vector", lambda e, hc=hc: e.tensor_copy(
                out=bfa[:, PT4 * (hc % 8):PT4 * (hc % 8 + 1)], in_=gen[:, 1024:1024 + PT4]),
                after=e4deps)
            prev_st = OP("gpsimd", lambda e, hc=hc, t0=t0: e.dma_start(
                out=xq2_scr[:, S * hc + t0:S * hc + t0 + PT4],
                in_=bfa[:, PT4 * (hc % 8):PT4 * (hc % 8 + 1)]), after=(e4,), dmainc=True)
            st_hist.append(prev_st)
            global_state["xq2_last"] = prev_st
        flush()

    for i in range(8):
        ld = OP("sync", lambda e, i=i: e.dma_start(
            out=gen[:, 0:4096], in_=w2_d[128 * i:128 * (i + 1), :]), dmainc=True)
        g1a = OP("vector", lambda e: e.scalar_tensor_tensor(
            out=gen[:, 4096:8192], in0=gen[:, 0:4096], scalar=-1.0,
            in1=gen[:, 0:4096], op0=ALU.mult, op1=ALU.max), after=(ld,))
        g1 = OP("vector", lambda e: e.tensor_scalar(out=gen[:, 4096:8192],
                                                    in0=gen[:, 4096:8192],
                                                    scalar1=thr2[:, 0:1], scalar2=None,
                                                    op0=ALU.is_ge), after=(g1a,))
        g2 = OP("scalar", lambda e: e.activation(out=arf[:, 0:4096], in_=gen[:, 0:4096],
                                                 func=AF.Sign), after=(ld,))
        g3 = OP("vector", lambda e: e.tensor_tensor(out=arf[:, 4096:8192],
                                                    in0=gen[:, 4096:8192],
                                                    in1=arf[:, 0:4096], op=ALU.mult),
                after=(g1, g2))
        g4 = OP("vector", lambda e: e.tensor_copy(out=bfa[:, 0:4096], in_=arf[:, 4096:8192]),
                after=(g3,))
        for hc in range(32):
            OP("scalar", lambda e, hc=hc, i=i: e.dma_start(
                out=wbig[:, 1024 * hc + 128 * i:1024 * hc + 128 * (i + 1)],
                in_=bfa[:, 128 * hc:128 * (hc + 1)], transpose=True),
               after=(g4,), dmainc=True)
        flush()

    ystores = []
    PT5 = min(512, S)
    NT5 = S // PT5
    NG5 = PT5 // 128
    dq5_tm = gen[:, 6656:6656 + NG5]
    for r in range(NT5):
        t0 = PT5 * r
        bdep5 = ((global_state["xq2_last"], global_state["dq2_last"],
                  global_state["y1_last"]) if r == 0 else ())
        x2l = OP("sync", lambda e, t0=t0: e.dma_start(
            out=whT[:, 0:32 * PT5].rearrange("p (c t) -> p c t", c=32),
            in_=xq2_scr.rearrange("p (c t) -> p c t", c=32)[:, :, t0:t0 + PT5]),
            after=bdep5, dmainc=True)
        y1l = []
        for g in range(NG5):
            y1l.append(OP("sync", lambda e, t0=t0, g=g: e.dma_start(
                out=arf[:, 1024 * g:1024 * (g + 1)],
                in_=y1_scr[t0 + 128 * g:t0 + 128 * (g + 1), :]), after=bdep5, dmainc=True))
        dql = OP("sync", lambda e, t0=t0: e.dma_start(
            out=dq5_tm[:, :],
            in_=dq2_scr[t0:t0 + PT5, :].rearrange("(g p) a -> p (g a)", p=128),
            allow_slow_non_contiguous=True), after=bdep5, dmainc=True)
        prev_f = {0: None, 1: None}
        for g in range(NG5):
            mm_done = []
            for eh, ps in ((0, p_a), (1, p_b)):
                mmt = None
                for hc in range(32):
                    fdeps = ()
                    if hc == 0:
                        fdeps = (x2l,)
                        if prev_f[eh] is not None:
                            fdeps = fdeps + (prev_f[eh],)
                    mmt = OP("tensor", lambda e, hc=hc, g=g, eh=eh, ps=ps: e.matmul(
                        ps[:, 0:512],
                        whT[:, PT5 * hc + 128 * g:PT5 * hc + 128 * (g + 1)],
                        wbig[:, 1024 * hc + 512 * eh:1024 * hc + 512 * (eh + 1)],
                        start=(hc == 0), stop=(hc == 31)), after=fdeps)
                mm_done.append(mmt)
            f1 = OP("vector", lambda e, g=g: e.tensor_scalar(
                out=gen[:, 0:512], in0=p_a[:, 0:512], scalar1=dq5_tm[:, g:g + 1],
                scalar2=None, op0=ALU.mult), after=(mm_done[0], dql))
            f2 = OP("vector", lambda e, g=g: e.tensor_scalar(
                out=gen[:, 512:1024], in0=p_b[:, 0:512], scalar1=dq5_tm[:, g:g + 1],
                scalar2=None, op0=ALU.mult), after=(mm_done[1], dql))
            prev_f[0], prev_f[1] = f1, f2
            ln4 = ln_plain(gen[:, 0:1024], gen[:, 1024:2048], gen[:, 2048:3072],
                           gen[:, 3072:4096], 0,
                           deps=(f1, f2) + ((ystores[-1],) if ystores else ()))
            y2v = OP("vector", lambda e, g=g: e.tensor_tensor(
                out=gen[:, 2048:3072], in0=arf[:, 1024 * g:1024 * (g + 1)],
                in1=gen[:, 1024:2048], op=ALU.add), after=(ln4, y1l[g]))
            ystores.append(OP("gpsimd", lambda e, t0=t0, g=g: e.dma_start(
                out=y_out[t0 + 128 * g:t0 + 128 * (g + 1), :], in_=gen[:, 2048:3072]),
                after=(y2v,), dmainc=True))
        flush()
    OP("vector", lambda e: e.memset(scs[:, 0:1], 0.0), after=tuple(ystores))
    sch.emit()


def kernel(L0_attn_x, L0_attn_W_in, L0_attn_b_in, L0_attn_W_h, L0_attn_b_h,
           L0_attn_W_out, L0_attn_b_out, L0_attn_ln_w, L0_attn_ln_b,
           L0_ffn_w1, L0_ffn_conv_w, L0_ffn_conv_b, L0_ffn_w2):
    L0_attn_x = np.asarray(L0_attn_x)
    B, S, D_ = L0_attn_x.shape
    nc = build_program(S=S)
    base = {
        "w_in_d": np.ascontiguousarray(L0_attn_W_in, dtype=np.float32),
        "b_in_d": np.ascontiguousarray(L0_attn_b_in, dtype=np.float32),
        "w_h_d": np.ascontiguousarray(L0_attn_W_h, dtype=np.float32),
        "b_h_d": np.ascontiguousarray(L0_attn_b_h, dtype=np.float32),
        "w_out_d": np.ascontiguousarray(L0_attn_W_out, dtype=np.float32),
        "b_out_d": np.ascontiguousarray(L0_attn_b_out, dtype=np.float32),
        "ln_w_d": np.ascontiguousarray(L0_attn_ln_w, dtype=np.float32),
        "ln_b_d": np.ascontiguousarray(L0_attn_ln_b, dtype=np.float32),
        "w1_d": np.ascontiguousarray(L0_ffn_w1, dtype=np.float32),
        "cw_d": np.ascontiguousarray(np.asarray(L0_ffn_conv_w).reshape(DH, 3),
                                     dtype=np.float32),
        "cb_d": np.ascontiguousarray(L0_ffn_conv_b, dtype=np.float32),
        "w2_d": np.ascontiguousarray(L0_ffn_w2, dtype=np.float32),
    }
    in_maps = []
    for b in range(B):
        m = dict(base)
        m["x_in"] = np.ascontiguousarray(L0_attn_x[b], dtype=np.float32)
        in_maps.append(m)
    res = run_bass_kernel_spmd(nc, in_maps, core_ids=list(range(B)))
    return np.stack([res.results[b]["y_out"] for b in range(B)],
                    axis=0).astype(np.float32)
